# version 20
# baseline (speedup 1.0000x reference)
import numpy as np

import concourse.bass as bass
import concourse.mybir as mybir
from concourse.bass_utils import run_bass_kernel_spmd

F32 = mybir.dt.float32
BF16 = mybir.dt.bfloat16
AF = mybir.ActivationFunctionType
OP = mybir.AluOpType

D = 1024
T = 2048
DC = 8
TT = 16
TG = 4
HC = 22
HLAST = 42
FC = 32
TM = 2

ENGS = ("tensor", "vector", "scalar", "gpsimd", "sync")


class Sched:

    def __init__(self):
        self.q = {e: [] for e in ENGS}
        self.c = {}

    def w(self, eng, sem, val):
        cur = self.c.get(sem, 0)
        assert 0 <= val <= cur, ("wait on future inc", eng, sem, val, cur)
        if val > 0:
            self.q[eng].append(("w", sem, val))

    def op(self, eng, fn, sem=None, n=1):
        v = None
        if sem is not None:
            self.c[sem] = self.c.get(sem, 0) + n
            v = self.c[sem]
        self.q[eng].append(("i", fn, sem, n))
        return v

    def wt(self, eng, tok):
        if tok is not None:
            self.w(eng, tok[0], tok[1])


_RACE_CHECK = False


def _build():
    nc = bass.Bass("TRN2", target_bir_lowering=False,
                   detect_race_conditions=_RACE_CHECK)

    xT = nc.dram_tensor("xT", [D, T], F32, kind="ExternalInput")
    outT = nc.dram_tensor("outT", [D, T], F32, kind="ExternalOutput")
    wq = nc.dram_tensor("wq", [D, 1024], F32, kind="ExternalInput")
    wkv = nc.dram_tensor("wkv", [D, 2048], F32, kind="ExternalInput")
    wg_t = nc.dram_tensor("wg_t", [HC * 128, 1024], F32, kind="ExternalInput")
    wu_t = nc.dram_tensor("wu_t", [HC * 128, 1024], F32, kind="ExternalInput")
    wd_t = nc.dram_tensor("wd_t", [DC * 128, 2816], F32, kind="ExternalInput")
    wfc_t = nc.dram_tensor("wfc_t", [FC * 128, 1024], F32, kind="ExternalInput")
    wproj_t = nc.dram_tensor("wproj_t", [DC * 128, 4096], F32, kind="ExternalInput")
    cols = nc.dram_tensor("cols", [128, 40], F32, kind="ExternalInput")
    colsgu = nc.dram_tensor("colsgu", [128, 44], F32, kind="ExternalInput")
    bkvrep = nc.dram_tensor("bkvrep", [128, 2048], F32, kind="ExternalInput")
    ssum_hbm = nc.dram_tensor("ssum_hbm", [D, T], BF16, kind="Internal")

    XN = nc.alloc_sbuf_tensor("XN", [128, DC * T], BF16)
    QPF = nc.alloc_sbuf_tensor("QPF", [128, DC * T], BF16)
    WKV = nc.alloc_sbuf_tensor("WKV", [128, DC * 2048], BF16)
    KVB = nc.alloc_sbuf_tensor("KVB", [128, DC * 1024], BF16)
    KSUM = nc.alloc_sbuf_tensor("KSUM", [128, 8], BF16)
    COLS = nc.alloc_sbuf_tensor("COLS", [128, 40], F32)
    COLSGU = nc.alloc_sbuf_tensor("COLSGU", [128, 44], F32)
    BKVREP = nc.alloc_sbuf_tensor("BKVREP", [128, 2048], F32)
    ONESC = nc.alloc_sbuf_tensor("ONESC", [128, 2], BF16)
    ONESR = nc.alloc_sbuf_tensor("ONESR", [1, 128], BF16)
    CONSTF = nc.alloc_sbuf_tensor("CONSTF", [128, 4], F32)
    R_RSTD = nc.alloc_sbuf_tensor("R_RSTD", [1, T], BF16)
    R_MURS = nc.alloc_sbuf_tensor("R_MURS", [1, T], BF16)

    ARENA_BYTES = 80896
    ARENA = nc.alloc_sbuf_tensor("ARENA", [128, ARENA_BYTES], mybir.dt.uint8)
    abase = nc.lookup_mloc(ARENA).addr

    def at(name, off, shape, dtype, base=None):
        b = abase if base is None else base
        return nc.alloc_sbuf_tensor_at(name, shape, dtype, offset=b + off)

    X0 = at("X0", 0, [128, DC * T], F32)
    HF = at("HF", 0, [128, FC * 1024], BF16)
    KP = at("KP", 0, [128, TT * 1024], BF16)
    VTOK = at("VTOK", 32768, [128, TT * 1024], BF16)
    Z = at("Z", 0, [128, DC * 512], F32)
    X1 = at("X1", 16384, [128, DC * 512], F32)
    X1B = at("X1B", 16384, [128, DC * 1024], BF16)
    X1F = at("X1F", 32768, [128, DC * 512], BF16)
    FUSED = at("FUSED", 40960, [128, HC * 512], BF16)
    CSCR = at("CSCR", 63488, [128, 512], F32)
    SLOT_A = at("SLOT_A", 65536, [128, 1024], BF16)
    GBUF2 = at("GBUF2", 65536, [128, 512], F32)
    SLOT_B = at("SLOT_B", 67584, [128, 512], F32)
    SLOT_C = at("SLOT_C", 69632, [128, 512], F32)
    SLOT_D = at("SLOT_D", 71680, [128, 512], F32)
    SLOT_E = at("SLOT_E", 73728, [128, 512], F32)
    SLOT_F = at("SLOT_F", 75776, [128, 512], F32)
    SLOT_G = at("SLOT_G", 77824, [128, 512], F32)
    SLOTA2 = at("SLOTA2", 75776, [128, 512], BF16)
    SLOTA3 = at("SLOTA3", 77824, [128, 512], BF16)
    WQB = [at("WQB0", 71680, [128, 1024], BF16),
           at("WQB1", 73728, [128, 1024], BF16)]
    SSB = at("SSB", 69632, [128, 1024], BF16)

    kvb_base = nc.lookup_mloc(KVB).addr
    WFCB = at("WFCB", 0, [128, 4 * 1024], BF16, base=kvb_base)
    MBUF = at("MBUF", 8192, [128, 512], F32, base=kvb_base)
    WP = [at("WP0", 10240, [128, 512], F32, base=kvb_base),
          at("WP1", 12288, [128, 512], F32, base=kvb_base)]
    SNB = at("SNB", 14336, [128, 512], F32, base=kvb_base)

    wkv_base = nc.lookup_mloc(WKV).addr
    WQF = at("WQF", 0, [128, DC * 1024], BF16, base=wkv_base)
    WDB = at("WDB", 0, [128, 2 * 2816], BF16, base=wkv_base)
    WGUB = at("WGUB", 11264, [128, 4 * 1024], BF16, base=wkv_base)
    RSTD1 = at("RSTD1", 19456, [1, 512], BF16, base=wkv_base)
    MURS1 = at("MURS1", 20480, [1, 512], BF16, base=wkv_base)
    RSTD2 = at("RSTD2", 21504, [1, 512], BF16, base=wkv_base)
    MURS2 = at("MURS2", 22528, [1, 512], BF16, base=wkv_base)
    SSG2 = at("SSG2", 23552, [128, 2 * 512], BF16, base=wkv_base)
    RECB4 = at("RECB4", 25600, [1, T], BF16, base=wkv_base)
    ROWW = at("ROWW", 29696, [1, 512], F32, base=wkv_base)

    qpf_base = nc.lookup_mloc(QPF).addr
    WPROJB = at("WPROJB", 0, [128, 4 * 4096], BF16, base=qpf_base)
    rrow_base = nc.lookup_mloc(R_RSTD).addr
    WFCB2 = at("WFCB2", 0, [128, 4 * 1024], BF16, base=rrow_base)


    PB = [nc.alloc_psum_tensor("pb%d" % i, [128, 512], F32) for i in range(8)]

    S = Sched()
    bank_rel = [("t", 0)] * 8

    def bwait(i):
        sem, val = bank_rel[i]
        S.w("tensor", sem, val)

    dcount = {}

    def dma(eng, stream, rot, out_ap, in_ap):
        idx = dcount.get(stream, 0)
        dcount[stream] = idx + 1
        sem = "%s%d" % (stream, idx % rot)
        cur = S.c.get(sem, 0)
        if cur > 0:
            S.q[eng].append(("w", sem, cur))
        v = S.op(eng, lambda e, o=out_ap, i=in_ap: e.dma_start(out=o, in_=i), sem, 16)
        return (sem, v)

    def sem_total(stream, rot):
        return [(("%s%d" % (stream, p)), S.c.get("%s%d" % (stream, p), 0))
                for p in range(rot) if S.c.get("%s%d" % (stream, p), 0) > 0]

    def mm(out_ap, lhsT, rhs, start, stop, sem=None):
        return S.op(
            "tensor",
            lambda e, o=out_ap, l=lhsT, r=rhs, st=start, sp=stop: e.matmul(
                o, l, r, start=st, stop=sp),
            sem)

    def vop(fn, sem=None):
        return S.op("vector", fn, sem)

    def sop(fn, sem=None):
        return S.op("scalar", fn, sem)

    def gop(fn, sem=None):
        return S.op("gpsimd", fn, sem)

    eps5 = CONSTF[0:1, 0:1]
    m02 = CONSTF[:, 1:2]
    eps6 = CONSTF[0:1, 2:3]
    onescol = ONESC[:, 1:2]
    ones1 = ONESC[:, 0:1]

    def xnc(d, a, b):
        return XN[:, d * T + a: d * T + b]

    def qpc(d, a, b):
        return QPF[:, d * T + a: d * T + b]

    def slotA(par):
        if par < 2:
            return SLOT_A[:, par * 512:(par + 1) * 512]
        return SLOTA2[:, :] if par == 2 else SLOTA3[:, :]

    vop(lambda e: e.memset(ONESC[:, 0:1], 1.0))
    vop(lambda e: e.memset(ONESC[:, 1:2], 1.0 / 1024.0))
    vop(lambda e: e.memset(ONESR[0:1, :], 1.0))
    vop(lambda e: e.memset(CONSTF[:, 0:1], 1.0e-5))
    vop(lambda e: e.memset(CONSTF[:, 1:2], -0.2))
    vop(lambda e: e.memset(CONSTF[:, 2:3], 1.0e-6), "v")

    dma("gpsimd", "cc", 1, COLS[:, :], cols[:, :])
    dma("gpsimd", "cc", 1, COLSGU[:, :], colsgu[:, :])
    g_consts = dma("gpsimd", "cc", 1, BKVREP[:, :], bkvrep[:, :])
    g_wkv = [dma("gpsimd", "wk", 2, WKV[:, d * 2048:(d + 1) * 2048],
                 wkv[d * 128:(d + 1) * 128, :]) for d in range(DC)]

    y_x = [dma("sync", "yx", 4, X0[:, d * T:(d + 1) * T], xT[d * 128:(d + 1) * 128, :])
           for d in range(DC)]

    v_xb = []
    for d in range(DC):
        S.wt("vector", y_x[d])
        v_xb.append(vop(lambda e, d=d: e.tensor_copy(
            xnc(d, 0, T), X0[:, d * T:(d + 1) * T]), "v"))

    slotA_rel = [("v", 0), ("v", 0), ("v", 0), ("v", 0)]
    v_rows = []
    for tg in range(TG):
        ia, ib = (tg % 2) * 2, (tg % 2) * 2 + 1
        pm, ps = PB[ia], PB[ib]
        bwait(ia)
        bwait(ib)
        t_last = None
        for d in range(DC):
            par = (tg * DC + d) % 4
            sem, val = slotA_rel[par]
            S.w("scalar", sem, val)
            S.wt("scalar", y_x[d])
            s_sq = sop(lambda e, d=d, tg=tg, par=par: e.activation(
                slotA(par), X0[:, d * T + tg * 512: d * T + tg * 512 + 512],
                AF.Square), "s")
            S.w("tensor", "v", v_xb[d])
            mm(pm[0:1, :], onescol, xnc(d, tg * 512, tg * 512 + 512),
               start=(d == 0), stop=(d == DC - 1))
            S.w("tensor", "s", s_sq)
            t_last = mm(ps[0:1, :], onescol, slotA(par),
                        start=(d == 0), stop=(d == DC - 1), sem="t")
            slotA_rel[par] = ("t", t_last)
        S.w("vector", "t", t_last)
        vop(lambda e, pm=pm: e.tensor_copy(SLOT_D[0:1, :], pm[0:1, :]))
        vop(lambda e: e.tensor_mul(SLOT_B[0:1, :], SLOT_D[0:1, :], SLOT_D[0:1, :]))
        v_var = vop(lambda e, ps=ps: e.tensor_sub(
            SLOT_C[0:1, :], ps[0:1, :], SLOT_B[0:1, :]), "v")
        S.w("scalar", "v", v_var)
        s_sqrt = sop(lambda e: e.activation(
            SLOT_B[0:1, :], SLOT_C[0:1, :], AF.Sqrt, bias=eps5), "s")
        S.w("vector", "s", s_sqrt)
        vop(lambda e: e.reciprocal(SLOT_C[0:1, :], SLOT_B[0:1, :]))
        vop(lambda e, tg=tg: e.tensor_copy(
            R_RSTD[0:1, tg * 512:(tg + 1) * 512], SLOT_C[0:1, :]))
        vr = vop(lambda e, tg=tg: e.tensor_mul(
            R_MURS[0:1, tg * 512:(tg + 1) * 512], SLOT_D[0:1, :],
            SLOT_C[0:1, :]), "v")
        v_rows.append(vr)
        bank_rel[ia] = ("v", vr)
        bank_rel[ib] = ("v", vr)

    v_xn = []
    for tg in range(TG):
        ia, ib = 4 + (tg % 2) * 2, 5 + (tg % 2) * 2
        pa, pbb = PB[ia], PB[ib]
        bwait(ia)
        bwait(ib)
        S.w("tensor", "v", v_rows[tg])
        mm(pa[:, :], ONESR[0:1, :], R_RSTD[0:1, tg * 512:(tg + 1) * 512], True, True)
        t_bc = mm(pbb[:, :], ONESR[0:1, :], R_MURS[0:1, tg * 512:(tg + 1) * 512],
                  True, True, "t")
        S.w("vector", "t", t_bc)
        vr = None
        for d in range(DC):
            vop(lambda e, d=d, tg=tg, pa=pa: e.tensor_mul(
                SLOT_B[:, :], X0[:, d * T + tg * 512: d * T + tg * 512 + 512], pa[:, :]))
            vr = vop(lambda e, d=d, tg=tg, pbb=pbb: e.tensor_sub(
                xnc(d, tg * 512, tg * 512 + 512), SLOT_B[:, :], pbb[:, :]),
                "v" if d == DC - 1 else None)
        v_xn.append(vr)
        bank_rel[ia] = ("v", vr)
        bank_rel[ib] = ("v", vr)

    wfcb_rel = [0, 0, 0, 0, 0, 0, 0, 0]
    tP0 = S.c["t"]
    for p8 in range(4, 8):
        wfcb_rel[p8] = tP0

    def wfcb_chunk(p8, c0, c1):
        if p8 < 4:
            return WFCB[:, p8 * 1024 + c0: p8 * 1024 + c1]
        return WFCB2[:, (p8 - 4) * 1024 + c0: (p8 - 4) * 1024 + c1]

    g_wfc_pend = {}

    def emit_wfc(fi):
        p8 = fi % 8
        S.w("gpsimd", "t", wfcb_rel[p8])
        g_wfc_pend[fi] = dma("gpsimd", "wf", 8, wfcb_chunk(p8, 0, 1024),
                             wfc_t[(fi % FC) * 128:(fi % FC + 1) * 128, :])

    for fi0 in range(8):
        emit_wfc(fi0)
    wprojb_rel = [0, 0, 0, 0]
    ysb_rel = {0: ("v", 0), 1: ("v", 0)}
    YSB = [SLOT_B, SLOT_D]
    wp_read_p = [0, 0]
    snb_v_gate = 0
    ssb_y = [None, None]
    p_cnt = 0
    v_spk_last = 0

    for tm in range(TM):
        g_wp = {}

        def emit_wp(dd):
            par4 = dd % 4
            S.w("gpsimd", "t", wprojb_rel[par4])
            g_wp[dd] = dma("gpsimd", "wp", 4, WPROJB[:, par4 * 4096:(par4 + 1) * 4096],
                           wproj_t[dd * 128:(dd + 1) * 128, :])

        v_hf = None
        for f in range(FC):
            fi = tm * FC + f
            p8 = fi % 8
            S.wt("tensor", g_wfc_pend[fi])
            S.w("tensor", "v", v_xn[2 * tm + 1])
            i0, i1 = 2 * (f % 2), 2 * (f % 2) + 1
            hb0, hb1 = PB[i0], PB[i1]
            bwait(i0)
            bwait(i1)
            t_m7 = None
            for d in range(DC):
                wt = wfcb_chunk(p8, d * 128, (d + 1) * 128)
                mm(hb0[:, :], wt, xnc(d, tm * 1024, tm * 1024 + 512),
                   start=(d == 0), stop=(d == DC - 1))
                t_m7 = mm(hb1[:, :], wt, xnc(d, tm * 1024 + 512, tm * 1024 + 1024),
                          start=(d == 0), stop=(d == DC - 1),
                          sem="t" if d == DC - 1 else None)
            wfcb_rel[p8] = t_m7
            if fi + 8 < TM * FC:
                emit_wfc(fi + 8)
            if f in (24, 27, 30):
                emit_wp((f - 24) // 3)
            for th in range(2):
                par2 = (f * 2 + th) % 4
                hb = hb0 if th == 0 else hb1
                sem, val = slotA_rel[par2]
                S.w("scalar", sem, val)
                S.w("scalar", "t", t_m7)
                s_r = sop(lambda e, hb=hb, par2=par2: e.activation(
                    slotA(par2), hb[:, :], AF.Relu), "s")
                bank_rel[i0 if th == 0 else i1] = ("s", s_r)
                S.w("vector", "s", s_r)
                v_hf = vop(lambda e, f=f, th=th, par2=par2: e.tensor_mul(
                    HF[:, f * 1024 + th * 512: f * 1024 + (th + 1) * 512],
                    slotA(par2), slotA(par2)), "v")
                slotA_rel[par2] = ("v", v_hf)
        hf_s_last = S.c.get("s", 0)
        hf_v_last = S.c.get("v", 0)
        for dout in range(DC):
            par = dout % 4
            if dout + 3 < DC:
                emit_wp(dout + 3)
            S.wt("tensor", g_wp[dout])
            S.w("tensor", "s", hf_s_last)
            S.w("tensor", "v", hf_v_last)
            i0, i1 = 4 + 2 * (dout % 2), 5 + 2 * (dout % 2)
            yb0, yb1 = PB[i0], PB[i1]
            bwait(i0)
            bwait(i1)
            t_m8 = None
            for f in range(FC):
                wt = WPROJB[:, par * 4096 + f * 128: par * 4096 + (f + 1) * 128]
                mm(yb0[:, :], wt, HF[:, f * 1024: f * 1024 + 512],
                   start=(f == 0), stop=(f == FC - 1))
                t_m8 = mm(yb1[:, :], wt, HF[:, f * 1024 + 512: f * 1024 + 1024],
                          start=(f == 0), stop=(f == FC - 1),
                          sem="t" if f == FC - 1 else None)
            wprojb_rel[par] = t_m8
            for th in range(2):
                yb = yb0 if th == 0 else yb1
                ypar = th
                sem, val = ysb_rel[ypar]
                S.w("scalar", sem, val)
                S.w("scalar", "t", t_m8)
                s_y = sop(lambda e, yb=yb, ypar=ypar: e.activation(
                    YSB[ypar][:, :], yb[:, :], AF.Copy), "s")
                bank_rel[i0 if th == 0 else i1] = ("s", s_y)
                y = YSB[ypar]
                S.w("vector", "s", s_y)
                S.w("vector", "p", snb_v_gate)
                vop(lambda e, y=y: e.tensor_scalar(
                    SNB[:, :], y[:, :], 1.0, -1.0, OP.is_gt, OP.mult))
                vop(lambda e, y=y: e.tensor_scalar_mul(MBUF[:, :], y[:, :], 1.9))
                vop(lambda e: e.tensor_add(MBUF[:, :], MBUF[:, :], SNB[:, :]))
                S.w("vector", "p", wp_read_p[0])
                v4 = vop(lambda e: e.tensor_scalar(
                    WP[0][:, :], MBUF[:, :], 1.0, -1.0, OP.is_gt, OP.mult), "v")
                S.w("gpsimd", "v", v4)
                p_cnt = gop(lambda e: e.tensor_add(
                    SNB[:, :], SNB[:, :], WP[0][:, :]), "p")
                wp_read_p[0] = p_cnt
                vlast = None
                for step in range(3):
                    wp_cur = WP[step % 2]
                    nxt = (step + 1) % 2
                    vop(lambda e, y=y: e.scalar_tensor_tensor(
                        MBUF[:, :], MBUF[:, :], 0.9, y[:, :], OP.mult, OP.add))
                    vop(lambda e, wp=wp_cur: e.tensor_add(
                        MBUF[:, :], MBUF[:, :], wp[:, :]))
                    S.w("vector", "p", wp_read_p[nxt])
                    vlast = vop(lambda e, wp=WP[nxt]: e.tensor_scalar(
                        wp[:, :], MBUF[:, :], 1.0, -1.0, OP.is_gt, OP.mult), "v")
                    S.w("gpsimd", "v", vlast)
                    p_cnt = gop(lambda e, wp=WP[nxt]: e.tensor_add(
                        SNB[:, :], SNB[:, :], wp[:, :]), "p")
                    wp_read_p[nxt] = p_cnt
                ysb_rel[ypar] = ("v", vlast)
                snb_v_gate = p_cnt
                spar = (dout * 2 + th) % 2
                S.w("vector", "p", p_cnt)
                S.wt("vector", ssb_y[spar])
                v_sn = vop(lambda e, spar=spar: e.tensor_copy(
                    SSB[:, spar * 512:(spar + 1) * 512], SNB[:, :]), "v")
                v_spk_last = v_sn
                S.w("sync", "v", v_sn)
                yd = dma("sync", "ss", 2,
                         ssum_hbm[dout * 128:(dout + 1) * 128,
                                  tm * 1024 + th * 512: tm * 1024 + (th + 1) * 512],
                         SSB[:, spar * 512:(spar + 1) * 512])
                ssb_y[spar] = yd

    for tok in sem_total("ss", 2):
        S.w("scalar", tok[0], tok[1])
    kv_stage_rel = 0
    for tok in sem_total("wk", 2):
        S.w("tensor", tok[0], tok[1])
    S.wt("vector", g_consts)
    v_kvdone = None
    for tt in range(TT):
        ks = [0, 1, 2, 3] if tt % 2 == 0 else [4, 5, 6, 7]
        for i in ks:
            bwait(i)
        S.w("tensor", "v", v_xn[tt // 4])
        t_kv = None
        for d in range(DC):
            st = XN[:, d * T + tt * 128: d * T + (tt + 1) * 128]
            for j in range(4):
                t_kv = mm(PB[ks[j]][:, :], st,
                          WKV[:, d * 2048 + j * 512: d * 2048 + (j + 1) * 512],
                          start=(d == 0), stop=(d == DC - 1),
                          sem="t" if (d == DC - 1 and j == 3) else None)
        for eh in range(2):
            tb = SLOT_B if eh == 0 else SLOT_G
            S.w("vector", "t", t_kv)
            v_t1 = vop(lambda e, tb=tb, eh=eh, ks=ks: e.tensor_add(
                tb[:, :], PB[ks[eh]][:, :], BKVREP[:, eh * 512:(eh + 1) * 512]), "v")
            bank_rel[ks[eh]] = ("v", v_t1)
            S.w("scalar", "v", max(v_t1, kv_stage_rel))
            sop(lambda e, tb=tb: e.activation(SLOT_C[:, :], tb[:, :], AF.Exp))
            s_rl = sop(lambda e, tb=tb: e.activation(SLOT_F[:, :], tb[:, :], AF.Relu), "s")
            S.w("vector", "s", s_rl)
            v_kp = vop(lambda e, tt=tt, eh=eh: e.scalar_tensor_tensor(
                KP[:, tt * 1024 + eh * 512: tt * 1024 + (eh + 1) * 512],
                SLOT_C[:, :], 1.0, SLOT_F[:, :], OP.min, OP.add), "v")
            kv_stage_rel = v_kp
        for eh in range(2):
            S.w("vector", "t", t_kv)
            v_v = vop(lambda e, tt=tt, eh=eh, ks=ks: e.tensor_add(
                VTOK[:, tt * 1024 + eh * 512: tt * 1024 + (eh + 1) * 512],
                PB[ks[2 + eh]][:, :],
                BKVREP[:, 1024 + eh * 512: 1024 + (eh + 1) * 512]), "v")
            bank_rel[ks[2 + eh]] = ("v", v_v)
        v_kvdone = v_v

    S.w("gpsimd", "t", S.c["t"])
    g_wq = [dma("gpsimd", "wq", 2, WQF[:, d * 1024:(d + 1) * 1024],
                wq[d * 128:(d + 1) * 128, :]) for d in range(DC)]
    S.w("scalar", "v", v_kvdone)
    q_stage_rel = [0, 0]
    q_stage_rel2 = [0, 0]
    v_q = []
    t_q_last = None
    for tg in range(TG):
        for i in range(8):
            bwait(i)
        t_d = None
        for d in range(DC):
            S.wt("tensor", g_wq[d])
            for e_ in range(8):
                t_d = mm(PB[e_][:, :], WQF[:, d * 1024 + e_ * 128: d * 1024 + (e_ + 1) * 128],
                         xnc(d, tg * 512, tg * 512 + 512),
                         start=(d == 0), stop=(d == DC - 1),
                         sem="t" if e_ == 7 else None)
        t_q_last = t_d
        vq = None
        for e_ in range(8):
            ex = SLOT_C if e_ % 2 == 0 else SLOT_B
            rl = SLOT_F if e_ % 2 == 0 else SLOT_G
            bq = COLS[:, e_ * 5: e_ * 5 + 1]
            S.w("scalar", "t", t_d)
            S.wt("scalar", g_consts)
            S.w("scalar", "v", q_stage_rel[e_ % 2])
            sop(lambda e, e_=e_, ex=ex, bq=bq: e.activation(
                ex[:, :], PB[e_][:, :], AF.Exp, bias=bq))
            s_rl = sop(lambda e, e_=e_, rl=rl, bq=bq: e.activation(
                rl[:, :], PB[e_][:, :], AF.Relu, bias=bq), "s")
            bank_rel[e_] = ("s", s_rl)
            S.w("vector", "s", s_rl)
            vq = vop(lambda e, e_=e_, tg=tg, ex=ex, rl=rl: e.scalar_tensor_tensor(
                qpc(e_, tg * 512, tg * 512 + 512), ex[:, :], 1.0, rl[:, :],
                OP.min, OP.add), "v")
            q_stage_rel[e_ % 2] = vq
        v_q.append(vq)

    v_kv2 = []
    t_p2_last = None
    for d in range(DC):
        ks = [0, 1, 2] if d % 2 == 0 else [3, 4, 5]
        for i in ks:
            bwait(i)
        S.w("tensor", "v", v_kvdone)
        t_2 = None
        for si in range(TT):
            st = KP[:, si * 1024 + d * 128: si * 1024 + (d + 1) * 128]
            mm(PB[ks[0]][:, :], st, VTOK[:, si * 1024: si * 1024 + 512],
               start=(si == 0), stop=(si == TT - 1))
            mm(PB[ks[1]][:, :], st, VTOK[:, si * 1024 + 512: si * 1024 + 1024],
               start=(si == 0), stop=(si == TT - 1))
            t_2 = mm(PB[ks[2]][:, 0:1], st, ones1,
                     start=(si == 0), stop=(si == TT - 1),
                     sem="t" if si == TT - 1 else None)
        t_p2_last = t_2
        S.w("vector", "t", t_2)
        vop(lambda e, d=d, ks=ks: e.tensor_copy(
            KVB[:, d * 1024: d * 1024 + 512], PB[ks[0]][:, :]))
        vop(lambda e, d=d, ks=ks: e.tensor_copy(
            KVB[:, d * 1024 + 512: d * 1024 + 1024], PB[ks[1]][:, :]))
        v2 = vop(lambda e, d=d, ks=ks: e.tensor_copy(
            KSUM[:, d: d + 1], PB[ks[2]][:, 0:1]), "v")
        v_kv2.append(v2)
        for i in ks:
            bank_rel[i] = ("v", v2)

    v_den4 = []
    for g in range(TG):
        bwait(6)
        S.w("tensor", "v", v_kv2[-1])
        S.w("tensor", "v", v_q[g])
        t_den = None
        for d in range(DC):
            t_den = mm(PB[6][0:1, :], KSUM[:, d: d + 1],
                       qpc(d, g * 512, g * 512 + 512),
                       start=(d == 0), stop=(d == DC - 1),
                       sem="t" if d == DC - 1 else None)
        S.w("vector", "t", t_den)
        vop(lambda e: e.tensor_scalar_add(SLOT_B[0:1, :], PB[6][0:1, :], 1.0e-6))
        vop(lambda e: e.reciprocal(SLOT_B[0:1, :], SLOT_B[0:1, :]))
        vd0 = vop(lambda e, g=g: e.tensor_copy(
            RECB4[0:1, g * 512:(g + 1) * 512], SLOT_B[0:1, :]), "v")
        bank_rel[6] = ("v", vd0)
        bwait(7)
        S.w("tensor", "v", vd0)
        t_rb0 = mm(PB[7][:, :], ONESR[0:1, :],
                   RECB4[0:1, g * 512:(g + 1) * 512], True, True, "t")
        S.w("vector", "t", t_rb0)
        vd = None
        for d in range(DC):
            vd = vop(lambda e, d=d, g=g: e.tensor_mul(
                qpc(d, g * 512, g * 512 + 512), qpc(d, g * 512, g * 512 + 512),
                PB[7][:, :]), "v" if d == DC - 1 else None)
        v_den4.append(vd)
        bank_rel[7] = ("v", vd)

    wdb_rel = [0, 0]
    wgub_rel = [0, 0, 0, 0]
    ssg_rel = [0, 0]
    outt2 = [None, None]
    gb_rel = [("p", 0), ("p", 0)]
    xr2_rel = [("p", 0), ("p", 0)]
    cscr_rel = ("p", 0)
    g5_last = 0
    v3_last = 0
    comb_vf = None
    comb_tbc3 = None
    ssg_started = [False]
    g_ssg = {}
    g_wgu = {}

    NUMBANKS = [0, 1, 2, 3, 6, 7, 0, 1]

    def emit_num(gx, es):
        ts_ = []
        for e_ in es:
            bk = NUMBANKS[e_]
            bwait(bk)
            if e_ == 0:
                S.w("tensor", "v", v_den4[gx])
            t_num = None
            for d in range(DC):
                t_num = mm(PB[bk][:, :],
                           KVB[:, d * 1024 + e_ * 128: d * 1024 + (e_ + 1) * 128],
                           qpc(d, gx * 512, gx * 512 + 512),
                           start=(d == 0), stop=(d == DC - 1),
                           sem="t" if d == DC - 1 else None)
            ts_.append(t_num)
        return ts_

    def emit_wgu(h):
        gpar, upar = h % 2, 2 + h % 2
        S.w("gpsimd", "t", wgub_rel[gpar])
        g_wg = dma("gpsimd", "wg", 2, WGUB[:, gpar * 1024:(gpar + 1) * 1024],
                   wg_t[h * 128:(h + 1) * 128, :])
        S.w("gpsimd", "t", wgub_rel[upar])
        g_wu = dma("gpsimd", "wu", 2, WGUB[:, upar * 1024:(upar + 1) * 1024],
                   wu_t[h * 128:(h + 1) * 128, :])
        return (g_wg, g_wu)

    def emit_ssg(cg, dd):
        par = dd % 2
        S.w("gpsimd", "s", ssg_rel[par])
        g_ssg[dd] = dma("gpsimd", "sg", 2, SSG2[:, par * 512:(par + 1) * 512],
                        ssum_hbm[dd * 128:(dd + 1) * 128,
                                 cg * 512:(cg + 1) * 512])

    def combine_start(cg):
        if not ssg_started[0]:
            for tok in sem_total("ss", 2):
                S.w("gpsimd", tok[0], tok[1])
            ssg_started[0] = True
        emit_ssg(cg, 0)
        emit_ssg(cg, 1)

    def combine_dout(cg, dout, tail=False):
        nonlocal cscr_rel, g5_last, v3_last
        par = dout % 2
        on_v = tail and par == 1
        if dout < 2:
            S.w("sync", "v", comb_vf)
        S.wt("sync", xr2_rel[par])
        XR2 = SLOT_D if par == 0 else SLOT_E
        y_x2 = dma("sync", "x2", 2, XR2[:, :],
                   xT[dout * 128:(dout + 1) * 128, cg * 512:(cg + 1) * 512])
        GB = SLOT_G if par == 0 else GBUF2
        OB = SLOT_F if par == 0 else SLOT_C
        S.wt("scalar", g_ssg[dout])
        if dout == 0:
            S.w("scalar", "v", comb_vf)
        S.wt("scalar", gb_rel[par])
        s_g = sop(lambda e, par=par, GB=GB: e.activation(
            GB[:, :], SSG2[:, par * 512:(par + 1) * 512],
            AF.Silu, scale=-0.2), "s")
        ssg_rel[par] = s_g
        if dout == 0:
            S.w("vector", "t", comb_tbc3)
        S.wt("vector", cscr_rel)
        vop(lambda e, dout=dout: e.tensor_mul(
            CSCR[:, :], Z[:, dout * 512:(dout + 1) * 512], PB[4][:, :]))
        vop(lambda e: e.tensor_sub(CSCR[:, :], CSCR[:, :], PB[5][:, :]))
        v3 = vop(lambda e, dout=dout: e.tensor_scalar(
            CSCR[:, :], CSCR[:, :],
            COLS[:, dout * 5 + 3: dout * 5 + 4], COLS[:, dout * 5 + 4: dout * 5 + 5],
            OP.mult, OP.add), "v")
        v3_last = v3
        if on_v:
            S.w("vector", "s", s_g)
            S.wt("vector", outt2[par])
            g4v = vop(lambda e, GB=GB, OB=OB: e.tensor_mul(
                OB[:, :], CSCR[:, :], GB[:, :]), "v")
            cscr_rel = ("v", g4v)
            gb_rel[par] = ("v", g4v)
            S.wt("vector", y_x2)
            g5 = vop(lambda e, XR2=XR2, OB=OB: e.tensor_add(
                OB[:, :], OB[:, :], XR2[:, :]), "v")
            xr2_rel[par] = ("v", g5)
            S.w("sync", "v", g5)
        else:
            S.w("gpsimd", "v", v3)
            S.w("gpsimd", "s", s_g)
            S.wt("gpsimd", outt2[par])
            g4 = gop(lambda e, GB=GB, OB=OB: e.tensor_mul(
                OB[:, :], CSCR[:, :], GB[:, :]), "p")
            cscr_rel = ("p", g4)
            gb_rel[par] = ("p", g4)
            S.wt("gpsimd", y_x2)
            g5 = gop(lambda e, XR2=XR2, OB=OB: e.tensor_add(
                OB[:, :], OB[:, :], XR2[:, :]), "p")
            xr2_rel[par] = ("p", g5)
            g5_last = g5
            S.w("sync", "p", g5)
        outt2[par] = dma("sync", "ou", 2,
                         outT[dout * 128:(dout + 1) * 128, cg * 512:(cg + 1) * 512],
                         OB[:, :])
        if dout + 2 < DC:
            emit_ssg(cg, dout + 2)

    pend_tnum = None
    for g in range(TG):
        if g == 0:
            S.w("gpsimd", "t", t_q_last)
            S.w("vector", "t", t_p2_last)
            S.wt("vector", g_consts)
            S.wt("scalar", g_consts)
            vop(lambda e: e.memset(FUSED[32:64, 21 * 512: 22 * 512], 1.0), "v")
        for hh in range(2):
            g_wgu[hh] = emit_wgu(hh)
        if g > 0:
            combine_start(g - 1)

        v_z = []
        s_z2 = []
        if pend_tnum is None:
            pend_tnum = emit_num(g, range(6))
        tnums, pend_tnum = pend_tnum, None
        for e_ in range(DC):
            if g > 0:
                combine_dout(g - 1, e_)
            bk = NUMBANKS[e_]
            pe = PB[bk]
            if e_ >= 6:
                tnums.append(emit_num(g, [e_])[0])
            S.w("vector", "t", tnums[e_])
            vz = vop(lambda e, e_=e_, g=g, pe=pe: e.tensor_add(
                Z[:, e_ * 512:(e_ + 1) * 512], pe[:, :],
                xnc(e_, g * 512, g * 512 + 512)), "v")
            v_z.append(vz)
            bank_rel[bk] = ("v", vz)
            S.w("scalar", "v", vz)
            sop(lambda e, e_=e_: e.activation(
                X1F[:, e_ * 512:(e_ + 1) * 512], Z[:, e_ * 512:(e_ + 1) * 512], AF.Copy))
            sz = sop(lambda e, e_=e_: e.activation(
                X1B[:, e_ * 512:(e_ + 1) * 512], Z[:, e_ * 512:(e_ + 1) * 512],
                AF.Square), "s")
            s_z2.append(sz)
        if g > 0:
            bank_rel[4] = ("v", v3_last)
            bank_rel[5] = ("v", v3_last)

        bwait(2)
        bwait(3)
        t_st1 = None
        for e_ in range(DC):
            S.w("tensor", "s", s_z2[e_])
            mm(PB[2][0:1, :], onescol, X1F[:, e_ * 512:(e_ + 1) * 512],
               start=(e_ == 0), stop=(e_ == DC - 1))
            t_st1 = mm(PB[3][0:1, :], onescol, X1B[:, e_ * 512:(e_ + 1) * 512],
                       start=(e_ == 0), stop=(e_ == DC - 1), sem="t")
        S.w("vector", "t", t_st1)
        vop(lambda e: e.tensor_copy(ROWW[0:1, :], PB[2][0:1, :]))
        vop(lambda e: e.tensor_mul(SLOT_B[0:1, :], ROWW[0:1, :], ROWW[0:1, :]))
        v_var = vop(lambda e: e.tensor_sub(SLOT_B[0:1, :], PB[3][0:1, :], SLOT_B[0:1, :]), "v")
        S.w("scalar", "v", v_var)
        s_sq1 = sop(lambda e: e.activation(
            SLOT_B[0:1, :], SLOT_B[0:1, :], AF.Sqrt, bias=eps5), "s")
        S.w("vector", "s", s_sq1)
        vop(lambda e: e.reciprocal(SLOT_B[0:1, :], SLOT_B[0:1, :]))
        vop(lambda e: e.tensor_copy(RSTD1[0:1, :], SLOT_B[0:1, :]))
        v_r1 = vop(lambda e: e.tensor_mul(
            MURS1[0:1, :], ROWW[0:1, :], SLOT_B[0:1, :]), "v")
        bank_rel[2] = ("v", v_r1)
        bank_rel[3] = ("v", v_r1)

        bwait(6)
        bwait(7)
        S.w("tensor", "v", v_r1)
        mm(PB[6][:, :], ONESR[0:1, :], RSTD1[0:1, :], True, True)
        t_bc2 = mm(PB[7][:, :], ONESR[0:1, :], MURS1[0:1, :], True, True, "t")
        v_x1 = []
        s_x1 = []
        for e_ in range(DC):
            S.w("vector", "t", t_bc2)
            vop(lambda e, e_=e_: e.tensor_mul(
                SLOT_B[:, :], Z[:, e_ * 512:(e_ + 1) * 512], PB[6][:, :]))
            vop(lambda e: e.tensor_sub(SLOT_B[:, :], SLOT_B[:, :], PB[7][:, :]))
            vx = vop(lambda e, e_=e_: e.tensor_scalar(
                X1[:, e_ * 512:(e_ + 1) * 512], SLOT_B[:, :],
                COLS[:, e_ * 5 + 1: e_ * 5 + 2], COLS[:, e_ * 5 + 2: e_ * 5 + 3],
                OP.mult, OP.add), "v")
            v_x1.append(vx)
            S.w("scalar", "t", t_st1)
            S.w("scalar", "v", vx)
            s_x1.append(sop(lambda e, e_=e_: e.activation(
                X1F[:, e_ * 512:(e_ + 1) * 512], X1[:, e_ * 512:(e_ + 1) * 512],
                AF.Copy), "s"))
        bank_rel[6] = ("v", v_x1[-1])
        bank_rel[7] = ("v", v_x1[-1])

        v_f = None
        t_5 = None
        for h in range(HC):
            gpar, upar = h % 2, 2 + h % 2
            g_wg, g_wu = g_wgu[h]
            ig, iu = (0, 1) if h % 2 == 0 else (2, 3)
            bwait(ig)
            bwait(iu)
            S.wt("tensor", g_wg)
            S.wt("tensor", g_wu)
            for d in range(DC):
                if h == 0:
                    S.w("tensor", "s", s_x1[d])
                mm(PB[ig][:, :],
                   WGUB[:, gpar * 1024 + d * 128: gpar * 1024 + (d + 1) * 128],
                   X1F[:, d * 512:(d + 1) * 512], start=(d == 0), stop=(d == DC - 1))
                t_5 = mm(PB[iu][:, :],
                         WGUB[:, upar * 1024 + d * 128: upar * 1024 + (d + 1) * 128],
                         X1F[:, d * 512:(d + 1) * 512], start=(d == 0), stop=(d == DC - 1),
                         sem="t" if d == DC - 1 else None)
            wgub_rel[gpar] = t_5
            wgub_rel[upar] = t_5
            if h + 2 < HC:
                g_wgu[h + 2] = emit_wgu(h + 2)
            rows = 128 if h < HC - 1 else HLAST
            SIL = SLOT_D if h % 2 == 0 else SLOT_F
            UPT = SLOT_E if h % 2 == 0 else SLOT_G
            S.w("scalar", "t", t_5)
            if h == 0:
                S.wt("scalar", outt2[0])
                S.wt("scalar", outt2[1])
                if g5_last:
                    S.w("scalar", "p", g5_last)
            sop(lambda e, h=h, ig=ig, rows=rows, SIL=SIL: e.activation(
                SIL[0:rows, :], PB[ig][0:rows, :], AF.Silu,
                bias=COLSGU[0:rows, 2 * h: 2 * h + 1]))
            s_5 = sop(lambda e, h=h, iu=iu, rows=rows, UPT=UPT: e.activation(
                UPT[0:rows, :], PB[iu][0:rows, :], AF.Identity,
                bias=COLSGU[0:rows, 2 * h + 1: 2 * h + 2]), "s")
            bank_rel[ig] = ("s", s_5)
            bank_rel[iu] = ("s", s_5)
            S.w("vector", "s", s_5)
            v_f = vop(lambda e, h=h, rows=rows, SIL=SIL, UPT=UPT: e.tensor_mul(
                FUSED[0:rows, h * 512:(h + 1) * 512], SIL[0:rows, :], UPT[0:rows, :]), "v")

        v_u = []
        s_u2 = []
        t_6 = None
        for dout in range(DC):
            par = dout % 2
            S.w("gpsimd", "t", wdb_rel[par])
            g_wd = dma("gpsimd", "wd", 2, WDB[:, par * 2816:(par + 1) * 2816],
                       wd_t[dout * 128:(dout + 1) * 128, :])
            ib_ = dout % 2
            bwait(ib_)
            S.wt("tensor", g_wd)
            S.w("tensor", "v", v_f)
            for h in range(HC):
                K = 128 if h < HC - 1 else HLAST + 1
                t_6 = mm(PB[ib_][:, :],
                         WDB[0:K, par * 2816 + h * 128: par * 2816 + (h + 1) * 128],
                         FUSED[0:K, h * 512:(h + 1) * 512],
                         start=(h == 0), stop=(h == HC - 1),
                         sem="t" if h == HC - 1 else None)
            wdb_rel[par] = t_6
            S.w("vector", "t", t_6)
            if dout == 0:
                S.w("vector", "t", t_5)
            vu = vop(lambda e, dout=dout, ib_=ib_: e.tensor_add(
                Z[:, dout * 512:(dout + 1) * 512], PB[ib_][:, :],
                X1[:, dout * 512:(dout + 1) * 512]), "v")
            v_u.append(vu)
            bank_rel[ib_] = ("v", vu)
            S.w("scalar", "v", vu)
            if dout == 0:
                S.w("scalar", "t", t_5)
                S.w("scalar", "t", t_st1)
            sop(lambda e, dout=dout: e.activation(
                X1F[:, dout * 512:(dout + 1) * 512],
                Z[:, dout * 512:(dout + 1) * 512], AF.Copy))
            su = sop(lambda e, dout=dout: e.activation(
                X1B[:, dout * 512:(dout + 1) * 512],
                Z[:, dout * 512:(dout + 1) * 512], AF.Square), "s")
            s_u2.append(su)

        bwait(2)
        bwait(3)
        t_st2 = None
        for dout in range(DC):
            S.w("tensor", "s", s_u2[dout])
            mm(PB[2][0:1, :], onescol, X1F[:, dout * 512:(dout + 1) * 512],
               start=(dout == 0), stop=(dout == DC - 1))
            t_st2 = mm(PB[3][0:1, :], onescol, X1B[:, dout * 512:(dout + 1) * 512],
                       start=(dout == 0), stop=(dout == DC - 1), sem="t")
        S.w("vector", "t", t_st2)
        vop(lambda e: e.tensor_copy(ROWW[0:1, :], PB[2][0:1, :]))
        vop(lambda e: e.tensor_mul(SLOT_B[0:1, :], ROWW[0:1, :], ROWW[0:1, :]))
        v_var2 = vop(lambda e: e.tensor_sub(SLOT_B[0:1, :], PB[3][0:1, :], SLOT_B[0:1, :]), "v")
        S.w("scalar", "v", v_var2)
        s_sq2 = sop(lambda e: e.activation(
            SLOT_B[0:1, :], SLOT_B[0:1, :], AF.Sqrt, bias=eps5), "s")
        S.w("vector", "s", s_sq2)
        vop(lambda e: e.reciprocal(SLOT_B[0:1, :], SLOT_B[0:1, :]))
        vop(lambda e: e.tensor_copy(RSTD2[0:1, :], SLOT_B[0:1, :]))
        v_r2 = vop(lambda e: e.tensor_mul(
            MURS2[0:1, :], ROWW[0:1, :], SLOT_B[0:1, :]), "v")
        bank_rel[2] = ("v", v_r2)
        bank_rel[3] = ("v", v_r2)

        bwait(4)
        bwait(5)
        S.w("tensor", "v", v_r2)
        mm(PB[4][:, :], ONESR[0:1, :], RSTD2[0:1, :], True, True)
        t_bc3 = mm(PB[5][:, :], ONESR[0:1, :], MURS2[0:1, :], True, True, "t")
        if g + 1 < TG:
            pend_tnum = emit_num(g + 1, range(6))
        comb_vf = v_f
        comb_tbc3 = t_bc3

    combine_start(TG - 1)
    for dout in range(DC):
        combine_dout(TG - 1, dout, tail=True)

    for tok in sem_total("ou", 2):
        S.w("sync", tok[0], tok[1])

    sems = {n: nc.alloc_semaphore(n) for n in sorted(S.c.keys())}
    with nc.Block() as block:
        def make(engname):
            def body(eng):
                for item in S.q[engname]:
                    if item[0] == "w":
                        eng.wait_ge(sems[item[1]], item[2])
                    else:
                        ins = item[1](eng)
                        if item[2] is not None:
                            ins.then_inc(sems[item[2]], item[3])
            return body

        block.tensor(make("tensor"))
        block.vector(make("vector"))
        block.scalar(make("scalar"))
        block.gpsimd(make("gpsimd"))
        block.sync(make("sync"))

    return nc


def _pack_weights(i):
    f32 = np.float32
    Wqkv = np.ascontiguousarray(np.asarray(i["L0_attn_Wqkv"], dtype=f32))
    bqkv = np.asarray(i["L0_attn_bqkv"], dtype=f32)
    Wg = np.asarray(i["L0_attn_Wg"], dtype=f32)
    bg = np.asarray(i["L0_attn_bg"], dtype=f32)
    Wu = np.asarray(i["L0_attn_Wu"], dtype=f32)
    bu = np.asarray(i["L0_attn_bu"], dtype=f32)
    Wd = np.asarray(i["L0_attn_Wd"], dtype=f32)
    bd = np.asarray(i["L0_attn_bd"], dtype=f32)
    g1 = np.asarray(i["L0_attn_g1"], dtype=f32)
    b1 = np.asarray(i["L0_attn_b1"], dtype=f32)
    g2 = np.asarray(i["L0_attn_g2"], dtype=f32)
    b2 = np.asarray(i["L0_attn_b2"], dtype=f32)
    W_fc = np.asarray(i["L0_ffn_W_fc"], dtype=f32)
    W_proj = np.asarray(i["L0_ffn_W_proj"], dtype=f32)

    d = {}
    d["wq"] = np.ascontiguousarray(Wqkv[:, 0:1024])
    d["wkv"] = np.ascontiguousarray(Wqkv[:, 1024:3072])

    wgp = np.zeros((D, HC * 128), dtype=f32)
    wgp[:, :2730] = Wg
    d["wg_t"] = np.ascontiguousarray(
        wgp.reshape(DC, 128, HC, 128).transpose(2, 1, 0, 3).reshape(HC * 128, 1024))
    wup = np.zeros((D, HC * 128), dtype=f32)
    wup[:, :2730] = Wu
    d["wu_t"] = np.ascontiguousarray(
        wup.reshape(DC, 128, HC, 128).transpose(2, 1, 0, 3).reshape(HC * 128, 1024))

    wda = np.zeros((HC * 128, D), dtype=f32)
    wda[:2730] = Wd
    wda[2730] = bd
    d["wd_t"] = np.ascontiguousarray(
        wda.reshape(HC, 128, DC, 128).transpose(2, 1, 0, 3).reshape(DC * 128, HC * 128))

    d["wfc_t"] = np.ascontiguousarray(
        W_fc.reshape(FC, 128, DC, 128).transpose(0, 3, 2, 1).reshape(FC * 128, 1024))
    d["wproj_t"] = np.ascontiguousarray(
        W_proj.reshape(DC, 128, FC, 128).transpose(0, 3, 2, 1).reshape(DC * 128, FC * 128))

    colsmat = np.stack([bqkv[0:1024], g1, b1, g2, b2], axis=1)
    d["cols"] = np.ascontiguousarray(
        colsmat.reshape(DC, 128, 5).transpose(1, 0, 2).reshape(128, 40))
    gum = np.zeros((HC * 128, 2), dtype=f32)
    gum[:2730, 0] = bg
    gum[:2730, 1] = bu
    d["colsgu"] = np.ascontiguousarray(
        gum.reshape(HC, 128, 2).transpose(1, 0, 2).reshape(128, 44))
    d["bkvrep"] = np.ascontiguousarray(
        np.broadcast_to(bqkv[1024:3072], (128, 2048)))
    return d


def kernel(**inputs):
    x = np.asarray(inputs["L0_attn_x"], dtype=np.float32)
    wd = _pack_weights(inputs)

    nc = _build()

    in_maps = []
    for c in range(8):
        m = dict(wd)
        m["xT"] = np.ascontiguousarray(x[c].T)
        in_maps.append(m)

    res = run_bass_kernel_spmd(nc, in_maps, core_ids=list(range(8)))
    out = np.stack([np.asarray(r["outT"]).T for r in res.results], axis=0)
    return np.ascontiguousarray(out.astype(np.float32))


# version 21
# speedup vs baseline: 1.0970x; 1.0970x over previous
import numpy as np

import concourse.bass as bass
import concourse.mybir as mybir
from concourse.bass_utils import run_bass_kernel_spmd

F32 = mybir.dt.float32
BF16 = mybir.dt.bfloat16
AF = mybir.ActivationFunctionType
OP = mybir.AluOpType

D = 1024
T = 2048
DC = 8
TT = 16
TG = 4
HC = 22
HLAST = 42
FC = 32
TM = 2

ENGS = ("tensor", "vector", "scalar", "gpsimd", "sync")


class Sched:

    def __init__(self):
        self.q = {e: [] for e in ENGS}
        self.c = {}

    def w(self, eng, sem, val):
        cur = self.c.get(sem, 0)
        assert 0 <= val <= cur, ("wait on future inc", eng, sem, val, cur)
        if val > 0:
            self.q[eng].append(("w", sem, val))

    def op(self, eng, fn, sem=None, n=1):
        v = None
        if sem is not None:
            self.c[sem] = self.c.get(sem, 0) + n
            v = self.c[sem]
        self.q[eng].append(("i", fn, sem, n))
        return v

    def wt(self, eng, tok):
        if tok is not None:
            self.w(eng, tok[0], tok[1])


_RACE_CHECK = False


def _build():
    nc = bass.Bass("TRN2", target_bir_lowering=False,
                   detect_race_conditions=_RACE_CHECK)

    xT = nc.dram_tensor("xT", [D, T], F32, kind="ExternalInput")
    outT = nc.dram_tensor("outT", [D, T], F32, kind="ExternalOutput")
    wq = nc.dram_tensor("wq", [D, 1024], BF16, kind="ExternalInput")
    wkv = nc.dram_tensor("wkv", [D, 2048], BF16, kind="ExternalInput")
    wg_t = nc.dram_tensor("wg_t", [HC * 128, 1024], BF16, kind="ExternalInput")
    wu_t = nc.dram_tensor("wu_t", [HC * 128, 1024], BF16, kind="ExternalInput")
    wd_t = nc.dram_tensor("wd_t", [DC * 128, 2816], BF16, kind="ExternalInput")
    wfc_t = nc.dram_tensor("wfc_t", [FC * 128, 1024], BF16, kind="ExternalInput")
    wproj_t = nc.dram_tensor("wproj_t", [DC * 128, 4096], BF16, kind="ExternalInput")
    cols = nc.dram_tensor("cols", [128, 40], F32, kind="ExternalInput")
    colsgu = nc.dram_tensor("colsgu", [128, 44], F32, kind="ExternalInput")
    bkvrep = nc.dram_tensor("bkvrep", [128, 2048], F32, kind="ExternalInput")
    ssum_hbm = nc.dram_tensor("ssum_hbm", [D, T], BF16, kind="Internal")

    XN = nc.alloc_sbuf_tensor("XN", [128, DC * T], BF16)
    QPF = nc.alloc_sbuf_tensor("QPF", [128, DC * T], BF16)
    WKV = nc.alloc_sbuf_tensor("WKV", [128, DC * 2048], BF16)
    KVB = nc.alloc_sbuf_tensor("KVB", [128, DC * 1024], BF16)
    KSUM = nc.alloc_sbuf_tensor("KSUM", [128, 8], BF16)
    COLS = nc.alloc_sbuf_tensor("COLS", [128, 40], F32)
    COLSGU = nc.alloc_sbuf_tensor("COLSGU", [128, 44], F32)
    BKVREP = nc.alloc_sbuf_tensor("BKVREP", [128, 2048], F32)
    ONESC = nc.alloc_sbuf_tensor("ONESC", [128, 2], BF16)
    ONESR = nc.alloc_sbuf_tensor("ONESR", [1, 128], BF16)
    CONSTF = nc.alloc_sbuf_tensor("CONSTF", [128, 4], F32)
    R_RSTD = nc.alloc_sbuf_tensor("R_RSTD", [1, T], BF16)
    R_MURS = nc.alloc_sbuf_tensor("R_MURS", [1, T], BF16)

    ARENA_BYTES = 80896
    ARENA = nc.alloc_sbuf_tensor("ARENA", [128, ARENA_BYTES], mybir.dt.uint8)
    abase = nc.lookup_mloc(ARENA).addr

    def at(name, off, shape, dtype, base=None):
        b = abase if base is None else base
        return nc.alloc_sbuf_tensor_at(name, shape, dtype, offset=b + off)

    X0 = at("X0", 0, [128, DC * T], F32)
    HF = at("HF", 0, [128, FC * 1024], BF16)
    KP = at("KP", 0, [128, TT * 1024], BF16)
    VTOK = at("VTOK", 32768, [128, TT * 1024], BF16)
    Z = at("Z", 0, [128, DC * 512], F32)
    X1 = at("X1", 16384, [128, DC * 512], F32)
    X1B = at("X1B", 16384, [128, DC * 1024], BF16)
    X1F = at("X1F", 32768, [128, DC * 512], BF16)
    FUSED = at("FUSED", 40960, [128, HC * 512], BF16)
    CSCR = at("CSCR", 63488, [128, 512], F32)
    SLOT_A = at("SLOT_A", 65536, [128, 1024], BF16)
    GBUF2 = at("GBUF2", 65536, [128, 512], F32)
    SLOT_B = at("SLOT_B", 67584, [128, 512], F32)
    SLOT_C = at("SLOT_C", 69632, [128, 512], F32)
    SLOT_D = at("SLOT_D", 71680, [128, 512], F32)
    SLOT_E = at("SLOT_E", 73728, [128, 512], F32)
    SLOT_F = at("SLOT_F", 75776, [128, 512], F32)
    SLOT_G = at("SLOT_G", 77824, [128, 512], F32)
    SLOTA2 = at("SLOTA2", 75776, [128, 512], BF16)
    SLOTA3 = at("SLOTA3", 77824, [128, 512], BF16)
    WQB = [at("WQB0", 71680, [128, 1024], BF16),
           at("WQB1", 73728, [128, 1024], BF16)]
    SSB = at("SSB", 69632, [128, 1024], BF16)

    kvb_base = nc.lookup_mloc(KVB).addr
    WFCB = at("WFCB", 0, [128, 4 * 1024], BF16, base=kvb_base)
    MBUF = at("MBUF", 8192, [128, 512], F32, base=kvb_base)
    WP = [at("WP0", 10240, [128, 512], F32, base=kvb_base),
          at("WP1", 12288, [128, 512], F32, base=kvb_base)]
    SNB = at("SNB", 14336, [128, 512], F32, base=kvb_base)

    wkv_base = nc.lookup_mloc(WKV).addr
    WQF = at("WQF", 0, [128, DC * 1024], BF16, base=wkv_base)
    WDB = at("WDB", 0, [128, 2 * 2816], BF16, base=wkv_base)
    WGUB = at("WGUB", 11264, [128, 4 * 1024], BF16, base=wkv_base)
    RSTD1 = at("RSTD1", 19456, [1, 512], BF16, base=wkv_base)
    MURS1 = at("MURS1", 20480, [1, 512], BF16, base=wkv_base)
    RSTD2 = at("RSTD2", 21504, [1, 512], BF16, base=wkv_base)
    MURS2 = at("MURS2", 22528, [1, 512], BF16, base=wkv_base)
    SSG2 = at("SSG2", 23552, [128, 2 * 512], BF16, base=wkv_base)
    RECB4 = at("RECB4", 25600, [1, T], BF16, base=wkv_base)
    ROWW = at("ROWW", 29696, [1, 512], F32, base=wkv_base)

    qpf_base = nc.lookup_mloc(QPF).addr
    WPROJB = at("WPROJB", 0, [128, 4 * 4096], BF16, base=qpf_base)
    rrow_base = nc.lookup_mloc(R_RSTD).addr
    WFCB2 = at("WFCB2", 0, [128, 4 * 1024], BF16, base=rrow_base)


    PB = [nc.alloc_psum_tensor("pb%d" % i, [128, 512], F32) for i in range(8)]

    S = Sched()
    bank_rel = [("t", 0)] * 8

    def bwait(i):
        sem, val = bank_rel[i]
        S.w("tensor", sem, val)

    dcount = {}

    def dma(eng, stream, rot, out_ap, in_ap):
        idx = dcount.get(stream, 0)
        dcount[stream] = idx + 1
        sem = "%s%d" % (stream, idx % rot)
        cur = S.c.get(sem, 0)
        if cur > 0:
            S.q[eng].append(("w", sem, cur))
        v = S.op(eng, lambda e, o=out_ap, i=in_ap: e.dma_start(out=o, in_=i), sem, 16)
        return (sem, v)

    def sem_total(stream, rot):
        return [(("%s%d" % (stream, p)), S.c.get("%s%d" % (stream, p), 0))
                for p in range(rot) if S.c.get("%s%d" % (stream, p), 0) > 0]

    def mm(out_ap, lhsT, rhs, start, stop, sem=None):
        return S.op(
            "tensor",
            lambda e, o=out_ap, l=lhsT, r=rhs, st=start, sp=stop: e.matmul(
                o, l, r, start=st, stop=sp),
            sem)

    def vop(fn, sem=None):
        return S.op("vector", fn, sem)

    def sop(fn, sem=None):
        return S.op("scalar", fn, sem)

    def gop(fn, sem=None):
        return S.op("gpsimd", fn, sem)

    eps5 = CONSTF[0:1, 0:1]
    m02 = CONSTF[:, 1:2]
    eps6 = CONSTF[0:1, 2:3]
    onescol = ONESC[:, 1:2]
    ones1 = ONESC[:, 0:1]

    def xnc(d, a, b):
        return XN[:, d * T + a: d * T + b]

    def qpc(d, a, b):
        return QPF[:, d * T + a: d * T + b]

    def slotA(par):
        if par < 2:
            return SLOT_A[:, par * 512:(par + 1) * 512]
        return SLOTA2[:, :] if par == 2 else SLOTA3[:, :]

    vop(lambda e: e.memset(ONESC[:, 0:1], 1.0))
    vop(lambda e: e.memset(ONESC[:, 1:2], 1.0 / 1024.0))
    vop(lambda e: e.memset(ONESR[0:1, :], 1.0))
    vop(lambda e: e.memset(CONSTF[:, 0:1], 1.0e-5))
    vop(lambda e: e.memset(CONSTF[:, 1:2], -0.2))
    vop(lambda e: e.memset(CONSTF[:, 2:3], 1.0e-6), "v")

    dma("gpsimd", "cc", 1, COLS[:, :], cols[:, :])
    dma("gpsimd", "cc", 1, COLSGU[:, :], colsgu[:, :])
    g_consts = dma("gpsimd", "cc", 1, BKVREP[:, :], bkvrep[:, :])
    g_wkv = [dma("gpsimd", "wk", 2, WKV[:, d * 2048:(d + 1) * 2048],
                 wkv[d * 128:(d + 1) * 128, :]) for d in range(DC)]

    y_x = [dma("sync", "yx", 4, X0[:, d * T:(d + 1) * T], xT[d * 128:(d + 1) * 128, :])
           for d in range(DC)]

    v_xb = []
    for d in range(DC):
        S.wt("vector", y_x[d])
        v_xb.append(vop(lambda e, d=d: e.tensor_copy(
            xnc(d, 0, T), X0[:, d * T:(d + 1) * T]), "v"))

    slotA_rel = [("v", 0), ("v", 0), ("v", 0), ("v", 0)]
    v_rows = []
    for tg in range(TG):
        ia, ib = (tg % 2) * 2, (tg % 2) * 2 + 1
        pm, ps = PB[ia], PB[ib]
        bwait(ia)
        bwait(ib)
        t_last = None
        for d in range(DC):
            par = (tg * DC + d) % 4
            sem, val = slotA_rel[par]
            S.w("scalar", sem, val)
            S.wt("scalar", y_x[d])
            s_sq = sop(lambda e, d=d, tg=tg, par=par: e.activation(
                slotA(par), X0[:, d * T + tg * 512: d * T + tg * 512 + 512],
                AF.Square), "s")
            S.w("tensor", "v", v_xb[d])
            mm(pm[0:1, :], onescol, xnc(d, tg * 512, tg * 512 + 512),
               start=(d == 0), stop=(d == DC - 1))
            S.w("tensor", "s", s_sq)
            t_last = mm(ps[0:1, :], onescol, slotA(par),
                        start=(d == 0), stop=(d == DC - 1), sem="t")
            slotA_rel[par] = ("t", t_last)
        S.w("vector", "t", t_last)
        vop(lambda e, pm=pm: e.tensor_copy(SLOT_D[0:1, :], pm[0:1, :]))
        vop(lambda e: e.tensor_mul(SLOT_B[0:1, :], SLOT_D[0:1, :], SLOT_D[0:1, :]))
        v_var = vop(lambda e, ps=ps: e.tensor_sub(
            SLOT_C[0:1, :], ps[0:1, :], SLOT_B[0:1, :]), "v")
        S.w("scalar", "v", v_var)
        s_sqrt = sop(lambda e: e.activation(
            SLOT_B[0:1, :], SLOT_C[0:1, :], AF.Sqrt, bias=eps5), "s")
        S.w("vector", "s", s_sqrt)
        vop(lambda e: e.reciprocal(SLOT_C[0:1, :], SLOT_B[0:1, :]))
        vop(lambda e, tg=tg: e.tensor_copy(
            R_RSTD[0:1, tg * 512:(tg + 1) * 512], SLOT_C[0:1, :]))
        vr = vop(lambda e, tg=tg: e.tensor_mul(
            R_MURS[0:1, tg * 512:(tg + 1) * 512], SLOT_D[0:1, :],
            SLOT_C[0:1, :]), "v")
        v_rows.append(vr)
        bank_rel[ia] = ("v", vr)
        bank_rel[ib] = ("v", vr)

    v_xn = []
    for tg in range(TG):
        ia, ib = 4 + (tg % 2) * 2, 5 + (tg % 2) * 2
        pa, pbb = PB[ia], PB[ib]
        bwait(ia)
        bwait(ib)
        S.w("tensor", "v", v_rows[tg])
        mm(pa[:, :], ONESR[0:1, :], R_RSTD[0:1, tg * 512:(tg + 1) * 512], True, True)
        t_bc = mm(pbb[:, :], ONESR[0:1, :], R_MURS[0:1, tg * 512:(tg + 1) * 512],
                  True, True, "t")
        S.w("vector", "t", t_bc)
        vr = None
        for d in range(DC):
            vop(lambda e, d=d, tg=tg, pa=pa: e.tensor_mul(
                SLOT_B[:, :], X0[:, d * T + tg * 512: d * T + tg * 512 + 512], pa[:, :]))
            vr = vop(lambda e, d=d, tg=tg, pbb=pbb: e.tensor_sub(
                xnc(d, tg * 512, tg * 512 + 512), SLOT_B[:, :], pbb[:, :]),
                "v" if d == DC - 1 else None)
        v_xn.append(vr)
        bank_rel[ia] = ("v", vr)
        bank_rel[ib] = ("v", vr)

    wfcb_rel = [0, 0, 0, 0, 0, 0, 0, 0]
    tP0 = S.c["t"]
    for p8 in range(4, 8):
        wfcb_rel[p8] = tP0

    def wfcb_chunk(p8, c0, c1):
        if p8 < 4:
            return WFCB[:, p8 * 1024 + c0: p8 * 1024 + c1]
        return WFCB2[:, (p8 - 4) * 1024 + c0: (p8 - 4) * 1024 + c1]

    g_wfc_pend = {}

    def emit_wfc(fi):
        p8 = fi % 8
        S.w("gpsimd", "t", wfcb_rel[p8])
        g_wfc_pend[fi] = dma("gpsimd", "wf", 8, wfcb_chunk(p8, 0, 1024),
                             wfc_t[(fi % FC) * 128:(fi % FC + 1) * 128, :])

    for fi0 in range(8):
        emit_wfc(fi0)
    wprojb_rel = [0, 0, 0, 0]
    ysb_rel = {0: ("v", 0), 1: ("v", 0)}
    YSB = [SLOT_B, SLOT_D]
    wp_read_p = [0, 0]
    snb_v_gate = 0
    ssb_y = [None, None]
    p_cnt = 0
    v_spk_last = 0

    for tm in range(TM):
        g_wp = {}

        def emit_wp(dd):
            par4 = dd % 4
            S.w("gpsimd", "t", wprojb_rel[par4])
            g_wp[dd] = dma("gpsimd", "wp", 4, WPROJB[:, par4 * 4096:(par4 + 1) * 4096],
                           wproj_t[dd * 128:(dd + 1) * 128, :])

        v_hf = None
        for f in range(FC):
            fi = tm * FC + f
            p8 = fi % 8
            S.wt("tensor", g_wfc_pend[fi])
            S.w("tensor", "v", v_xn[2 * tm + 1])
            i0, i1 = 2 * (f % 2), 2 * (f % 2) + 1
            hb0, hb1 = PB[i0], PB[i1]
            bwait(i0)
            bwait(i1)
            t_m7 = None
            for d in range(DC):
                wt = wfcb_chunk(p8, d * 128, (d + 1) * 128)
                mm(hb0[:, :], wt, xnc(d, tm * 1024, tm * 1024 + 512),
                   start=(d == 0), stop=(d == DC - 1))
                t_m7 = mm(hb1[:, :], wt, xnc(d, tm * 1024 + 512, tm * 1024 + 1024),
                          start=(d == 0), stop=(d == DC - 1),
                          sem="t" if d == DC - 1 else None)
            wfcb_rel[p8] = t_m7
            if fi + 8 < TM * FC:
                emit_wfc(fi + 8)
            if f in (24, 27, 30):
                emit_wp((f - 24) // 3)
            for th in range(2):
                par2 = (f * 2 + th) % 4
                hb = hb0 if th == 0 else hb1
                sem, val = slotA_rel[par2]
                S.w("scalar", sem, val)
                S.w("scalar", "t", t_m7)
                s_r = sop(lambda e, hb=hb, par2=par2: e.activation(
                    slotA(par2), hb[:, :], AF.Relu), "s")
                bank_rel[i0 if th == 0 else i1] = ("s", s_r)
                s_hf = sop(lambda e, f=f, th=th, par2=par2: e.activation(
                    HF[:, f * 1024 + th * 512: f * 1024 + (th + 1) * 512],
                    slotA(par2), AF.Square), "s")
                slotA_rel[par2] = ("s", s_hf)
        hf_s_last = S.c.get("s", 0)
        hf_v_last = S.c.get("v", 0)
        for dout in range(DC):
            par = dout % 4
            if dout + 3 < DC:
                emit_wp(dout + 3)
            S.wt("tensor", g_wp[dout])
            S.w("tensor", "s", hf_s_last)
            S.w("tensor", "v", hf_v_last)
            i0, i1 = 4 + 2 * (dout % 2), 5 + 2 * (dout % 2)
            yb0, yb1 = PB[i0], PB[i1]
            bwait(i0)
            bwait(i1)
            t_m8 = None
            for f in range(FC):
                wt = WPROJB[:, par * 4096 + f * 128: par * 4096 + (f + 1) * 128]
                mm(yb0[:, :], wt, HF[:, f * 1024: f * 1024 + 512],
                   start=(f == 0), stop=(f == FC - 1))
                t_m8 = mm(yb1[:, :], wt, HF[:, f * 1024 + 512: f * 1024 + 1024],
                          start=(f == 0), stop=(f == FC - 1),
                          sem="t" if f == FC - 1 else None)
            wprojb_rel[par] = t_m8
            for th in range(2):
                yb = yb0 if th == 0 else yb1
                ypar = th
                sem, val = ysb_rel[ypar]
                S.w("scalar", sem, val)
                S.w("scalar", "t", t_m8)
                s_y = sop(lambda e, yb=yb, ypar=ypar: e.activation(
                    YSB[ypar][:, :], yb[:, :], AF.Copy), "s")
                bank_rel[i0 if th == 0 else i1] = ("s", s_y)
                y = YSB[ypar]
                S.w("vector", "s", s_y)
                S.w("vector", "p", snb_v_gate)
                vop(lambda e, y=y: e.tensor_scalar(
                    SNB[:, :], y[:, :], 1.0, -1.0, OP.is_gt, OP.mult))
                vop(lambda e, y=y: e.tensor_scalar_mul(MBUF[:, :], y[:, :], 1.9))
                vop(lambda e: e.tensor_add(MBUF[:, :], MBUF[:, :], SNB[:, :]))
                S.w("vector", "p", wp_read_p[0])
                v4 = vop(lambda e: e.tensor_scalar(
                    WP[0][:, :], MBUF[:, :], 1.0, -1.0, OP.is_gt, OP.mult), "v")
                S.w("gpsimd", "v", v4)
                p_cnt = gop(lambda e: e.tensor_add(
                    SNB[:, :], SNB[:, :], WP[0][:, :]), "p")
                wp_read_p[0] = p_cnt
                vlast = None
                for step in range(3):
                    wp_cur = WP[step % 2]
                    nxt = (step + 1) % 2
                    vop(lambda e, y=y: e.scalar_tensor_tensor(
                        MBUF[:, :], MBUF[:, :], 0.9, y[:, :], OP.mult, OP.add))
                    vop(lambda e, wp=wp_cur: e.tensor_add(
                        MBUF[:, :], MBUF[:, :], wp[:, :]))
                    S.w("vector", "p", wp_read_p[nxt])
                    vlast = vop(lambda e, wp=WP[nxt]: e.tensor_scalar(
                        wp[:, :], MBUF[:, :], 1.0, -1.0, OP.is_gt, OP.mult), "v")
                    S.w("gpsimd", "v", vlast)
                    p_cnt = gop(lambda e, wp=WP[nxt]: e.tensor_add(
                        SNB[:, :], SNB[:, :], wp[:, :]), "p")
                    wp_read_p[nxt] = p_cnt
                ysb_rel[ypar] = ("v", vlast)
                snb_v_gate = p_cnt
                spar = (dout * 2 + th) % 2
                S.w("vector", "p", p_cnt)
                S.wt("vector", ssb_y[spar])
                v_sn = vop(lambda e, spar=spar: e.tensor_copy(
                    SSB[:, spar * 512:(spar + 1) * 512], SNB[:, :]), "v")
                v_spk_last = v_sn
                S.w("sync", "v", v_sn)
                yd = dma("sync", "ss", 2,
                         ssum_hbm[dout * 128:(dout + 1) * 128,
                                  tm * 1024 + th * 512: tm * 1024 + (th + 1) * 512],
                         SSB[:, spar * 512:(spar + 1) * 512])
                ssb_y[spar] = yd

    for tok in sem_total("ss", 2):
        S.w("scalar", tok[0], tok[1])
    kv_stage_rel = 0
    for tok in sem_total("wk", 2):
        S.w("tensor", tok[0], tok[1])
    S.wt("vector", g_consts)
    v_kvdone = None
    for tt in range(TT):
        ks = [0, 1, 2, 3] if tt % 2 == 0 else [4, 5, 6, 7]
        for i in ks:
            bwait(i)
        S.w("tensor", "v", v_xn[tt // 4])
        t_kv = None
        for d in range(DC):
            st = XN[:, d * T + tt * 128: d * T + (tt + 1) * 128]
            for j in range(4):
                t_kv = mm(PB[ks[j]][:, :], st,
                          WKV[:, d * 2048 + j * 512: d * 2048 + (j + 1) * 512],
                          start=(d == 0), stop=(d == DC - 1),
                          sem="t" if (d == DC - 1 and j == 3) else None)
        for eh in range(2):
            tb = SLOT_B if eh == 0 else SLOT_G
            S.w("vector", "t", t_kv)
            v_t1 = vop(lambda e, tb=tb, eh=eh, ks=ks: e.tensor_add(
                tb[:, :], PB[ks[eh]][:, :], BKVREP[:, eh * 512:(eh + 1) * 512]), "v")
            bank_rel[ks[eh]] = ("v", v_t1)
            S.w("scalar", "v", max(v_t1, kv_stage_rel))
            sop(lambda e, tb=tb: e.activation(SLOT_C[:, :], tb[:, :], AF.Exp))
            s_rl = sop(lambda e, tb=tb: e.activation(SLOT_F[:, :], tb[:, :], AF.Relu), "s")
            S.w("vector", "s", s_rl)
            v_kp = vop(lambda e, tt=tt, eh=eh: e.scalar_tensor_tensor(
                KP[:, tt * 1024 + eh * 512: tt * 1024 + (eh + 1) * 512],
                SLOT_C[:, :], 1.0, SLOT_F[:, :], OP.min, OP.add), "v")
            kv_stage_rel = v_kp
        for eh in range(2):
            S.w("vector", "t", t_kv)
            v_v = vop(lambda e, tt=tt, eh=eh, ks=ks: e.tensor_add(
                VTOK[:, tt * 1024 + eh * 512: tt * 1024 + (eh + 1) * 512],
                PB[ks[2 + eh]][:, :],
                BKVREP[:, 1024 + eh * 512: 1024 + (eh + 1) * 512]), "v")
            bank_rel[ks[2 + eh]] = ("v", v_v)
        v_kvdone = v_v

    S.w("gpsimd", "t", S.c["t"])
    g_wq = [dma("gpsimd", "wq", 2, WQF[:, d * 1024:(d + 1) * 1024],
                wq[d * 128:(d + 1) * 128, :]) for d in range(DC)]
    S.w("scalar", "v", v_kvdone)
    q_stage_rel = [0, 0]
    q_stage_rel2 = [0, 0]
    v_q = []
    t_q_last = None
    for tg in range(TG):
        for i in range(8):
            bwait(i)
        t_d = None
        for d in range(DC):
            S.wt("tensor", g_wq[d])
            for e_ in range(8):
                t_d = mm(PB[e_][:, :], WQF[:, d * 1024 + e_ * 128: d * 1024 + (e_ + 1) * 128],
                         xnc(d, tg * 512, tg * 512 + 512),
                         start=(d == 0), stop=(d == DC - 1),
                         sem="t" if e_ == 7 else None)
        t_q_last = t_d
        vq = None
        for e_ in range(8):
            ex = SLOT_C if e_ % 2 == 0 else SLOT_B
            rl = SLOT_F if e_ % 2 == 0 else SLOT_G
            bq = COLS[:, e_ * 5: e_ * 5 + 1]
            S.w("scalar", "t", t_d)
            S.wt("scalar", g_consts)
            S.w("scalar", "v", q_stage_rel[e_ % 2])
            sop(lambda e, e_=e_, ex=ex, bq=bq: e.activation(
                ex[:, :], PB[e_][:, :], AF.Exp, bias=bq))
            s_rl = sop(lambda e, e_=e_, rl=rl, bq=bq: e.activation(
                rl[:, :], PB[e_][:, :], AF.Relu, bias=bq), "s")
            bank_rel[e_] = ("s", s_rl)
            S.w("vector", "s", s_rl)
            vq = vop(lambda e, e_=e_, tg=tg, ex=ex, rl=rl: e.scalar_tensor_tensor(
                qpc(e_, tg * 512, tg * 512 + 512), ex[:, :], 1.0, rl[:, :],
                OP.min, OP.add), "v")
            q_stage_rel[e_ % 2] = vq
        v_q.append(vq)

    v_kv2 = []
    t_p2_last = None
    for d in range(DC):
        ks = [0, 1, 2] if d % 2 == 0 else [3, 4, 5]
        for i in ks:
            bwait(i)
        S.w("tensor", "v", v_kvdone)
        t_2 = None
        for si in range(TT):
            st = KP[:, si * 1024 + d * 128: si * 1024 + (d + 1) * 128]
            mm(PB[ks[0]][:, :], st, VTOK[:, si * 1024: si * 1024 + 512],
               start=(si == 0), stop=(si == TT - 1))
            mm(PB[ks[1]][:, :], st, VTOK[:, si * 1024 + 512: si * 1024 + 1024],
               start=(si == 0), stop=(si == TT - 1))
            t_2 = mm(PB[ks[2]][:, 0:1], st, ones1,
                     start=(si == 0), stop=(si == TT - 1),
                     sem="t" if si == TT - 1 else None)
        t_p2_last = t_2
        S.w("vector", "t", t_2)
        vop(lambda e, d=d, ks=ks: e.tensor_copy(
            KVB[:, d * 1024: d * 1024 + 512], PB[ks[0]][:, :]))
        vop(lambda e, d=d, ks=ks: e.tensor_copy(
            KVB[:, d * 1024 + 512: d * 1024 + 1024], PB[ks[1]][:, :]))
        v2 = vop(lambda e, d=d, ks=ks: e.tensor_copy(
            KSUM[:, d: d + 1], PB[ks[2]][:, 0:1]), "v")
        v_kv2.append(v2)
        for i in ks:
            bank_rel[i] = ("v", v2)

    v_den4 = []
    for g in range(TG):
        bwait(6)
        S.w("tensor", "v", v_kv2[-1])
        S.w("tensor", "v", v_q[g])
        t_den = None
        for d in range(DC):
            t_den = mm(PB[6][0:1, :], KSUM[:, d: d + 1],
                       qpc(d, g * 512, g * 512 + 512),
                       start=(d == 0), stop=(d == DC - 1),
                       sem="t" if d == DC - 1 else None)
        S.w("vector", "t", t_den)
        vop(lambda e: e.tensor_scalar_add(SLOT_B[0:1, :], PB[6][0:1, :], 1.0e-6))
        vop(lambda e: e.reciprocal(SLOT_B[0:1, :], SLOT_B[0:1, :]))
        vd0 = vop(lambda e, g=g: e.tensor_copy(
            RECB4[0:1, g * 512:(g + 1) * 512], SLOT_B[0:1, :]), "v")
        bank_rel[6] = ("v", vd0)
        bwait(7)
        S.w("tensor", "v", vd0)
        t_rb0 = mm(PB[7][:, :], ONESR[0:1, :],
                   RECB4[0:1, g * 512:(g + 1) * 512], True, True, "t")
        S.w("vector", "t", t_rb0)
        vd = None
        for d in range(DC):
            vd = vop(lambda e, d=d, g=g: e.tensor_mul(
                qpc(d, g * 512, g * 512 + 512), qpc(d, g * 512, g * 512 + 512),
                PB[7][:, :]), "v" if d == DC - 1 else None)
        v_den4.append(vd)
        bank_rel[7] = ("v", vd)

    wdb_rel = [0, 0]
    wgub_rel = [0, 0, 0, 0]
    ssg_rel = [0, 0]
    outt2 = [None, None]
    gb_rel = [("p", 0), ("p", 0)]
    xr2_rel = [("p", 0), ("p", 0)]
    cscr_rel = ("p", 0)
    g5_last = 0
    v3_last = 0
    comb_vf = None
    comb_tbc3 = None
    ssg_started = [False]
    g_ssg = {}
    g_wgu = {}

    NUMBANKS = [0, 1, 2, 3, 6, 7, 0, 1]

    def emit_num(gx, es):
        ts_ = []
        for e_ in es:
            bk = NUMBANKS[e_]
            bwait(bk)
            if e_ == 0:
                S.w("tensor", "v", v_den4[gx])
            t_num = None
            for d in range(DC):
                t_num = mm(PB[bk][:, :],
                           KVB[:, d * 1024 + e_ * 128: d * 1024 + (e_ + 1) * 128],
                           qpc(d, gx * 512, gx * 512 + 512),
                           start=(d == 0), stop=(d == DC - 1),
                           sem="t" if d == DC - 1 else None)
            ts_.append(t_num)
        return ts_

    def emit_wgu(h):
        gpar, upar = h % 2, 2 + h % 2
        S.w("gpsimd", "t", wgub_rel[gpar])
        g_wg = dma("gpsimd", "wg", 2, WGUB[:, gpar * 1024:(gpar + 1) * 1024],
                   wg_t[h * 128:(h + 1) * 128, :])
        S.w("gpsimd", "t", wgub_rel[upar])
        g_wu = dma("gpsimd", "wu", 2, WGUB[:, upar * 1024:(upar + 1) * 1024],
                   wu_t[h * 128:(h + 1) * 128, :])
        return (g_wg, g_wu)

    def emit_ssg(cg, dd):
        par = dd % 2
        S.w("gpsimd", "s", ssg_rel[par])
        g_ssg[dd] = dma("gpsimd", "sg", 2, SSG2[:, par * 512:(par + 1) * 512],
                        ssum_hbm[dd * 128:(dd + 1) * 128,
                                 cg * 512:(cg + 1) * 512])

    def combine_start(cg):
        if not ssg_started[0]:
            for tok in sem_total("ss", 2):
                S.w("gpsimd", tok[0], tok[1])
            ssg_started[0] = True
        emit_ssg(cg, 0)
        emit_ssg(cg, 1)

    def combine_dout(cg, dout, tail=False):
        nonlocal cscr_rel, g5_last, v3_last
        par = dout % 2
        on_v = tail and par == 1
        if dout < 2:
            S.w("sync", "v", comb_vf)
        S.wt("sync", xr2_rel[par])
        XR2 = SLOT_D if par == 0 else SLOT_E
        y_x2 = dma("sync", "x2", 2, XR2[:, :],
                   xT[dout * 128:(dout + 1) * 128, cg * 512:(cg + 1) * 512])
        GB = SLOT_G if par == 0 else GBUF2
        OB = SLOT_F if par == 0 else SLOT_C
        S.wt("scalar", g_ssg[dout])
        if dout == 0:
            S.w("scalar", "v", comb_vf)
        S.wt("scalar", gb_rel[par])
        s_g = sop(lambda e, par=par, GB=GB: e.activation(
            GB[:, :], SSG2[:, par * 512:(par + 1) * 512],
            AF.Silu, scale=-0.2), "s")
        ssg_rel[par] = s_g
        if dout == 0:
            S.w("vector", "t", comb_tbc3)
        S.wt("vector", cscr_rel)
        vop(lambda e, dout=dout: e.tensor_mul(
            CSCR[:, :], Z[:, dout * 512:(dout + 1) * 512], PB[4][:, :]))
        vop(lambda e: e.tensor_sub(CSCR[:, :], CSCR[:, :], PB[5][:, :]))
        v3 = vop(lambda e, dout=dout: e.tensor_scalar(
            CSCR[:, :], CSCR[:, :],
            COLS[:, dout * 5 + 3: dout * 5 + 4], COLS[:, dout * 5 + 4: dout * 5 + 5],
            OP.mult, OP.add), "v")
        v3_last = v3
        if on_v:
            S.w("vector", "s", s_g)
            S.wt("vector", outt2[par])
            g4v = vop(lambda e, GB=GB, OB=OB: e.tensor_mul(
                OB[:, :], CSCR[:, :], GB[:, :]), "v")
            cscr_rel = ("v", g4v)
            gb_rel[par] = ("v", g4v)
            S.wt("vector", y_x2)
            g5 = vop(lambda e, XR2=XR2, OB=OB: e.tensor_add(
                OB[:, :], OB[:, :], XR2[:, :]), "v")
            xr2_rel[par] = ("v", g5)
            S.w("sync", "v", g5)
        else:
            S.w("gpsimd", "v", v3)
            S.w("gpsimd", "s", s_g)
            S.wt("gpsimd", outt2[par])
            g4 = gop(lambda e, GB=GB, OB=OB: e.tensor_mul(
                OB[:, :], CSCR[:, :], GB[:, :]), "p")
            cscr_rel = ("p", g4)
            gb_rel[par] = ("p", g4)
            S.wt("gpsimd", y_x2)
            g5 = gop(lambda e, XR2=XR2, OB=OB: e.tensor_add(
                OB[:, :], OB[:, :], XR2[:, :]), "p")
            xr2_rel[par] = ("p", g5)
            g5_last = g5
            S.w("sync", "p", g5)
        outt2[par] = dma("sync", "ou", 2,
                         outT[dout * 128:(dout + 1) * 128, cg * 512:(cg + 1) * 512],
                         OB[:, :])
        if dout + 2 < DC:
            emit_ssg(cg, dout + 2)

    pend_tnum = None
    for g in range(TG):
        if g == 0:
            S.w("gpsimd", "t", t_q_last)
            S.w("vector", "t", t_p2_last)
            S.wt("vector", g_consts)
            S.wt("scalar", g_consts)
            vop(lambda e: e.memset(FUSED[32:64, 21 * 512: 22 * 512], 1.0), "v")
        for hh in range(2):
            g_wgu[hh] = emit_wgu(hh)
        if g > 0:
            combine_start(g - 1)

        v_z = []
        s_z2 = []
        if pend_tnum is None:
            pend_tnum = emit_num(g, range(6))
        tnums, pend_tnum = pend_tnum, None
        for e_ in range(DC):
            if g > 0:
                combine_dout(g - 1, e_)
            bk = NUMBANKS[e_]
            pe = PB[bk]
            if e_ >= 6:
                tnums.append(emit_num(g, [e_])[0])
            S.w("vector", "t", tnums[e_])
            vz = vop(lambda e, e_=e_, g=g, pe=pe: e.tensor_add(
                Z[:, e_ * 512:(e_ + 1) * 512], pe[:, :],
                xnc(e_, g * 512, g * 512 + 512)), "v")
            v_z.append(vz)
            bank_rel[bk] = ("v", vz)
            S.w("scalar", "v", vz)
            sop(lambda e, e_=e_: e.activation(
                X1F[:, e_ * 512:(e_ + 1) * 512], Z[:, e_ * 512:(e_ + 1) * 512], AF.Copy))
            sz = sop(lambda e, e_=e_: e.activation(
                X1B[:, e_ * 512:(e_ + 1) * 512], Z[:, e_ * 512:(e_ + 1) * 512],
                AF.Square), "s")
            s_z2.append(sz)
        if g > 0:
            bank_rel[4] = ("v", v3_last)
            bank_rel[5] = ("v", v3_last)

        bwait(2)
        bwait(3)
        t_st1 = None
        for e_ in range(DC):
            S.w("tensor", "s", s_z2[e_])
            mm(PB[2][0:1, :], onescol, X1F[:, e_ * 512:(e_ + 1) * 512],
               start=(e_ == 0), stop=(e_ == DC - 1))
            t_st1 = mm(PB[3][0:1, :], onescol, X1B[:, e_ * 512:(e_ + 1) * 512],
                       start=(e_ == 0), stop=(e_ == DC - 1), sem="t")
        S.w("vector", "t", t_st1)
        vop(lambda e: e.tensor_copy(ROWW[0:1, :], PB[2][0:1, :]))
        vop(lambda e: e.tensor_mul(SLOT_B[0:1, :], ROWW[0:1, :], ROWW[0:1, :]))
        v_var = vop(lambda e: e.tensor_sub(SLOT_B[0:1, :], PB[3][0:1, :], SLOT_B[0:1, :]), "v")
        S.w("scalar", "v", v_var)
        s_sq1 = sop(lambda e: e.activation(
            SLOT_B[0:1, :], SLOT_B[0:1, :], AF.Sqrt, bias=eps5), "s")
        S.w("vector", "s", s_sq1)
        vop(lambda e: e.reciprocal(SLOT_B[0:1, :], SLOT_B[0:1, :]))
        vop(lambda e: e.tensor_copy(RSTD1[0:1, :], SLOT_B[0:1, :]))
        v_r1 = vop(lambda e: e.tensor_mul(
            MURS1[0:1, :], ROWW[0:1, :], SLOT_B[0:1, :]), "v")
        bank_rel[2] = ("v", v_r1)
        bank_rel[3] = ("v", v_r1)

        bwait(6)
        bwait(7)
        S.w("tensor", "v", v_r1)
        mm(PB[6][:, :], ONESR[0:1, :], RSTD1[0:1, :], True, True)
        t_bc2 = mm(PB[7][:, :], ONESR[0:1, :], MURS1[0:1, :], True, True, "t")
        v_x1 = []
        s_x1 = []
        for e_ in range(DC):
            S.w("vector", "t", t_bc2)
            vop(lambda e, e_=e_: e.tensor_mul(
                SLOT_B[:, :], Z[:, e_ * 512:(e_ + 1) * 512], PB[6][:, :]))
            vop(lambda e: e.tensor_sub(SLOT_B[:, :], SLOT_B[:, :], PB[7][:, :]))
            vx = vop(lambda e, e_=e_: e.tensor_scalar(
                X1[:, e_ * 512:(e_ + 1) * 512], SLOT_B[:, :],
                COLS[:, e_ * 5 + 1: e_ * 5 + 2], COLS[:, e_ * 5 + 2: e_ * 5 + 3],
                OP.mult, OP.add), "v")
            v_x1.append(vx)
            S.w("scalar", "t", t_st1)
            S.w("scalar", "v", vx)
            s_x1.append(sop(lambda e, e_=e_: e.activation(
                X1F[:, e_ * 512:(e_ + 1) * 512], X1[:, e_ * 512:(e_ + 1) * 512],
                AF.Copy), "s"))
        bank_rel[6] = ("v", v_x1[-1])
        bank_rel[7] = ("v", v_x1[-1])

        v_f = None
        t_5 = None
        for h in range(HC):
            gpar, upar = h % 2, 2 + h % 2
            g_wg, g_wu = g_wgu[h]
            ig, iu = [(0, 1), (2, 3), (6, 7)][h % 3]
            bwait(ig)
            bwait(iu)
            S.wt("tensor", g_wg)
            S.wt("tensor", g_wu)
            for d in range(DC):
                if h == 0:
                    S.w("tensor", "s", s_x1[d])
                mm(PB[ig][:, :],
                   WGUB[:, gpar * 1024 + d * 128: gpar * 1024 + (d + 1) * 128],
                   X1F[:, d * 512:(d + 1) * 512], start=(d == 0), stop=(d == DC - 1))
                t_5 = mm(PB[iu][:, :],
                         WGUB[:, upar * 1024 + d * 128: upar * 1024 + (d + 1) * 128],
                         X1F[:, d * 512:(d + 1) * 512], start=(d == 0), stop=(d == DC - 1),
                         sem="t" if d == DC - 1 else None)
            wgub_rel[gpar] = t_5
            wgub_rel[upar] = t_5
            if h + 2 < HC:
                g_wgu[h + 2] = emit_wgu(h + 2)
            rows = 128 if h < HC - 1 else HLAST
            SIL = SLOT_D if h % 2 == 0 else SLOT_F
            UPT = SLOT_E if h % 2 == 0 else SLOT_G
            S.w("scalar", "t", t_5)
            if h == 0:
                S.wt("scalar", outt2[0])
                S.wt("scalar", outt2[1])
                if g5_last:
                    S.w("scalar", "p", g5_last)
            sop(lambda e, h=h, ig=ig, rows=rows, SIL=SIL: e.activation(
                SIL[0:rows, :], PB[ig][0:rows, :], AF.Silu,
                bias=COLSGU[0:rows, 2 * h: 2 * h + 1]))
            s_5 = sop(lambda e, h=h, iu=iu, rows=rows, UPT=UPT: e.activation(
                UPT[0:rows, :], PB[iu][0:rows, :], AF.Identity,
                bias=COLSGU[0:rows, 2 * h + 1: 2 * h + 2]), "s")
            bank_rel[ig] = ("s", s_5)
            bank_rel[iu] = ("s", s_5)
            S.w("vector", "s", s_5)
            v_f = vop(lambda e, h=h, rows=rows, SIL=SIL, UPT=UPT: e.tensor_mul(
                FUSED[0:rows, h * 512:(h + 1) * 512], SIL[0:rows, :], UPT[0:rows, :]), "v")

        v_u = []
        s_u2 = []
        t_6 = None
        for dout in range(DC):
            par = dout % 2
            S.w("gpsimd", "t", wdb_rel[par])
            g_wd = dma("gpsimd", "wd", 2, WDB[:, par * 2816:(par + 1) * 2816],
                       wd_t[dout * 128:(dout + 1) * 128, :])
            ib_ = dout % 4
            bwait(ib_)
            S.wt("tensor", g_wd)
            S.w("tensor", "v", v_f)
            for h in range(HC):
                K = 128 if h < HC - 1 else HLAST + 1
                t_6 = mm(PB[ib_][:, :],
                         WDB[0:K, par * 2816 + h * 128: par * 2816 + (h + 1) * 128],
                         FUSED[0:K, h * 512:(h + 1) * 512],
                         start=(h == 0), stop=(h == HC - 1),
                         sem="t" if h == HC - 1 else None)
            wdb_rel[par] = t_6
            S.w("vector", "t", t_6)
            if dout == 0:
                S.w("vector", "t", t_5)
            vu = vop(lambda e, dout=dout, ib_=ib_: e.tensor_add(
                Z[:, dout * 512:(dout + 1) * 512], PB[ib_][:, :],
                X1[:, dout * 512:(dout + 1) * 512]), "v")
            v_u.append(vu)
            bank_rel[ib_] = ("v", vu)
            S.w("scalar", "v", vu)
            if dout == 0:
                S.w("scalar", "t", t_5)
                S.w("scalar", "t", t_st1)
            sop(lambda e, dout=dout: e.activation(
                X1F[:, dout * 512:(dout + 1) * 512],
                Z[:, dout * 512:(dout + 1) * 512], AF.Copy))
            su = sop(lambda e, dout=dout: e.activation(
                X1B[:, dout * 512:(dout + 1) * 512],
                Z[:, dout * 512:(dout + 1) * 512], AF.Square), "s")
            s_u2.append(su)

        bwait(2)
        bwait(3)
        t_st2 = None
        for dout in range(DC):
            S.w("tensor", "s", s_u2[dout])
            mm(PB[2][0:1, :], onescol, X1F[:, dout * 512:(dout + 1) * 512],
               start=(dout == 0), stop=(dout == DC - 1))
            t_st2 = mm(PB[3][0:1, :], onescol, X1B[:, dout * 512:(dout + 1) * 512],
                       start=(dout == 0), stop=(dout == DC - 1), sem="t")
        S.w("vector", "t", t_st2)
        vop(lambda e: e.tensor_copy(ROWW[0:1, :], PB[2][0:1, :]))
        vop(lambda e: e.tensor_mul(SLOT_B[0:1, :], ROWW[0:1, :], ROWW[0:1, :]))
        v_var2 = vop(lambda e: e.tensor_sub(SLOT_B[0:1, :], PB[3][0:1, :], SLOT_B[0:1, :]), "v")
        S.w("scalar", "v", v_var2)
        s_sq2 = sop(lambda e: e.activation(
            SLOT_B[0:1, :], SLOT_B[0:1, :], AF.Sqrt, bias=eps5), "s")
        S.w("vector", "s", s_sq2)
        vop(lambda e: e.reciprocal(SLOT_B[0:1, :], SLOT_B[0:1, :]))
        vop(lambda e: e.tensor_copy(RSTD2[0:1, :], SLOT_B[0:1, :]))
        v_r2 = vop(lambda e: e.tensor_mul(
            MURS2[0:1, :], ROWW[0:1, :], SLOT_B[0:1, :]), "v")
        bank_rel[2] = ("v", v_r2)
        bank_rel[3] = ("v", v_r2)

        bwait(4)
        bwait(5)
        S.w("tensor", "v", v_r2)
        mm(PB[4][:, :], ONESR[0:1, :], RSTD2[0:1, :], True, True)
        t_bc3 = mm(PB[5][:, :], ONESR[0:1, :], MURS2[0:1, :], True, True, "t")
        if g + 1 < TG:
            pend_tnum = emit_num(g + 1, range(6))
        comb_vf = v_f
        comb_tbc3 = t_bc3

    combine_start(TG - 1)
    for dout in range(DC):
        combine_dout(TG - 1, dout, tail=True)

    for tok in sem_total("ou", 2):
        S.w("sync", tok[0], tok[1])

    sems = {n: nc.alloc_semaphore(n) for n in sorted(S.c.keys())}
    with nc.Block() as block:
        def make(engname):
            def body(eng):
                for item in S.q[engname]:
                    if item[0] == "w":
                        eng.wait_ge(sems[item[1]], item[2])
                    else:
                        ins = item[1](eng)
                        if item[2] is not None:
                            ins.then_inc(sems[item[2]], item[3])
            return body

        block.tensor(make("tensor"))
        block.vector(make("vector"))
        block.scalar(make("scalar"))
        block.gpsimd(make("gpsimd"))
        block.sync(make("sync"))

    return nc


def _pack_weights(i):
    f32 = np.float32
    Wqkv = np.ascontiguousarray(np.asarray(i["L0_attn_Wqkv"], dtype=f32))
    bqkv = np.asarray(i["L0_attn_bqkv"], dtype=f32)
    Wg = np.asarray(i["L0_attn_Wg"], dtype=f32)
    bg = np.asarray(i["L0_attn_bg"], dtype=f32)
    Wu = np.asarray(i["L0_attn_Wu"], dtype=f32)
    bu = np.asarray(i["L0_attn_bu"], dtype=f32)
    Wd = np.asarray(i["L0_attn_Wd"], dtype=f32)
    bd = np.asarray(i["L0_attn_bd"], dtype=f32)
    g1 = np.asarray(i["L0_attn_g1"], dtype=f32)
    b1 = np.asarray(i["L0_attn_b1"], dtype=f32)
    g2 = np.asarray(i["L0_attn_g2"], dtype=f32)
    b2 = np.asarray(i["L0_attn_b2"], dtype=f32)
    W_fc = np.asarray(i["L0_ffn_W_fc"], dtype=f32)
    W_proj = np.asarray(i["L0_ffn_W_proj"], dtype=f32)

    d = {}
    d["wq"] = np.ascontiguousarray(Wqkv[:, 0:1024])
    d["wkv"] = np.ascontiguousarray(Wqkv[:, 1024:3072])

    wgp = np.zeros((D, HC * 128), dtype=f32)
    wgp[:, :2730] = Wg
    d["wg_t"] = np.ascontiguousarray(
        wgp.reshape(DC, 128, HC, 128).transpose(2, 1, 0, 3).reshape(HC * 128, 1024))
    wup = np.zeros((D, HC * 128), dtype=f32)
    wup[:, :2730] = Wu
    d["wu_t"] = np.ascontiguousarray(
        wup.reshape(DC, 128, HC, 128).transpose(2, 1, 0, 3).reshape(HC * 128, 1024))

    wda = np.zeros((HC * 128, D), dtype=f32)
    wda[:2730] = Wd
    wda[2730] = bd
    d["wd_t"] = np.ascontiguousarray(
        wda.reshape(HC, 128, DC, 128).transpose(2, 1, 0, 3).reshape(DC * 128, HC * 128))

    d["wfc_t"] = np.ascontiguousarray(
        W_fc.reshape(FC, 128, DC, 128).transpose(0, 3, 2, 1).reshape(FC * 128, 1024))
    d["wproj_t"] = np.ascontiguousarray(
        W_proj.reshape(DC, 128, FC, 128).transpose(0, 3, 2, 1).reshape(DC * 128, FC * 128))

    import ml_dtypes
    for kk in ("wq", "wkv", "wg_t", "wu_t", "wd_t", "wfc_t", "wproj_t"):
        d[kk] = np.ascontiguousarray(d[kk].astype(ml_dtypes.bfloat16))

    colsmat = np.stack([bqkv[0:1024], g1, b1, g2, b2], axis=1)
    d["cols"] = np.ascontiguousarray(
        colsmat.reshape(DC, 128, 5).transpose(1, 0, 2).reshape(128, 40))
    gum = np.zeros((HC * 128, 2), dtype=f32)
    gum[:2730, 0] = bg
    gum[:2730, 1] = bu
    d["colsgu"] = np.ascontiguousarray(
        gum.reshape(HC, 128, 2).transpose(1, 0, 2).reshape(128, 44))
    d["bkvrep"] = np.ascontiguousarray(
        np.broadcast_to(bqkv[1024:3072], (128, 2048)))
    return d


def kernel(**inputs):
    x = np.asarray(inputs["L0_attn_x"], dtype=np.float32)
    wd = _pack_weights(inputs)

    nc = _build()

    in_maps = []
    for c in range(8):
        m = dict(wd)
        m["xT"] = np.ascontiguousarray(x[c].T)
        in_maps.append(m)

    res = run_bass_kernel_spmd(nc, in_maps, core_ids=list(range(8)))
    out = np.stack([np.asarray(r["outT"]).T for r in res.results], axis=0)
    return np.ascontiguousarray(out.astype(np.float32))


# version 22
# speedup vs baseline: 1.1074x; 1.0095x over previous
import numpy as np

import concourse.bass as bass
import concourse.mybir as mybir
from concourse.bass_utils import run_bass_kernel_spmd

F32 = mybir.dt.float32
BF16 = mybir.dt.bfloat16
AF = mybir.ActivationFunctionType
OP = mybir.AluOpType

D = 1024
T = 2048
DC = 8
TT = 16
TG = 4
HC = 22
HLAST = 42
FC = 32
TM = 2

ENGS = ("tensor", "vector", "scalar", "gpsimd", "sync")


class Sched:

    def __init__(self):
        self.q = {e: [] for e in ENGS}
        self.c = {}

    def w(self, eng, sem, val):
        cur = self.c.get(sem, 0)
        assert 0 <= val <= cur, ("wait on future inc", eng, sem, val, cur)
        if val > 0:
            self.q[eng].append(("w", sem, val))

    def op(self, eng, fn, sem=None, n=1):
        v = None
        if sem is not None:
            self.c[sem] = self.c.get(sem, 0) + n
            v = self.c[sem]
        self.q[eng].append(("i", fn, sem, n))
        return v

    def wt(self, eng, tok):
        if tok is not None:
            self.w(eng, tok[0], tok[1])


_RACE_CHECK = False


def _build():
    nc = bass.Bass("TRN2", target_bir_lowering=False,
                   detect_race_conditions=_RACE_CHECK)

    xT = nc.dram_tensor("xT", [D, T], F32, kind="ExternalInput")
    outT = nc.dram_tensor("outT", [D, T], F32, kind="ExternalOutput")
    wq = nc.dram_tensor("wq", [D, 1024], BF16, kind="ExternalInput")
    wkv = nc.dram_tensor("wkv", [D, 2048], BF16, kind="ExternalInput")
    wg_t = nc.dram_tensor("wg_t", [HC * 128, 1024], BF16, kind="ExternalInput")
    wu_t = nc.dram_tensor("wu_t", [HC * 128, 1024], BF16, kind="ExternalInput")
    wd_t = nc.dram_tensor("wd_t", [DC * 128, 2816], BF16, kind="ExternalInput")
    wfc_t = nc.dram_tensor("wfc_t", [FC * 128, 1024], BF16, kind="ExternalInput")
    wproj_t = nc.dram_tensor("wproj_t", [DC * 128, 4096], BF16, kind="ExternalInput")
    cols = nc.dram_tensor("cols", [128, 40], F32, kind="ExternalInput")
    colsgu = nc.dram_tensor("colsgu", [128, 44], F32, kind="ExternalInput")
    bkvrep = nc.dram_tensor("bkvrep", [128, 2048], F32, kind="ExternalInput")
    ssum_hbm = nc.dram_tensor("ssum_hbm", [D, T], BF16, kind="Internal")

    XN = nc.alloc_sbuf_tensor("XN", [128, DC * T], BF16)
    QPF = nc.alloc_sbuf_tensor("QPF", [128, DC * T], BF16)
    WKV = nc.alloc_sbuf_tensor("WKV", [128, DC * 2048], BF16)
    KVB = nc.alloc_sbuf_tensor("KVB", [128, DC * 1024], BF16)
    KSUM = nc.alloc_sbuf_tensor("KSUM", [128, 8], BF16)
    COLS = nc.alloc_sbuf_tensor("COLS", [128, 40], F32)
    COLSGU = nc.alloc_sbuf_tensor("COLSGU", [128, 44], F32)
    BKVREP = nc.alloc_sbuf_tensor("BKVREP", [128, 2048], F32)
    ONESC = nc.alloc_sbuf_tensor("ONESC", [128, 2], BF16)
    ONESR = nc.alloc_sbuf_tensor("ONESR", [1, 128], BF16)
    CONSTF = nc.alloc_sbuf_tensor("CONSTF", [128, 4], F32)
    R_RSTD = nc.alloc_sbuf_tensor("R_RSTD", [1, T], BF16)
    R_MURS = nc.alloc_sbuf_tensor("R_MURS", [1, T], BF16)

    ARENA_BYTES = 80896
    ARENA = nc.alloc_sbuf_tensor("ARENA", [128, ARENA_BYTES], mybir.dt.uint8)
    abase = nc.lookup_mloc(ARENA).addr

    def at(name, off, shape, dtype, base=None):
        b = abase if base is None else base
        return nc.alloc_sbuf_tensor_at(name, shape, dtype, offset=b + off)

    X0 = at("X0", 0, [128, DC * T], F32)
    HF = at("HF", 0, [128, FC * 1024], BF16)
    KP = at("KP", 0, [128, TT * 1024], BF16)
    VTOK = at("VTOK", 32768, [128, TT * 1024], BF16)
    Z = at("Z", 0, [128, DC * 512], F32)
    X1 = at("X1", 16384, [128, DC * 512], F32)
    X1B = at("X1B", 16384, [128, DC * 1024], BF16)
    X1F = at("X1F", 32768, [128, DC * 512], BF16)
    FUSED = at("FUSED", 40960, [128, HC * 512], BF16)
    CSCR = at("CSCR", 63488, [128, 512], F32)
    SLOT_A = at("SLOT_A", 65536, [128, 1024], BF16)
    GBUF2 = at("GBUF2", 65536, [128, 512], F32)
    SLOT_B = at("SLOT_B", 67584, [128, 512], F32)
    SLOT_C = at("SLOT_C", 69632, [128, 512], F32)
    SLOT_D = at("SLOT_D", 71680, [128, 512], F32)
    SLOT_E = at("SLOT_E", 73728, [128, 512], F32)
    SLOT_F = at("SLOT_F", 75776, [128, 512], F32)
    SLOT_G = at("SLOT_G", 77824, [128, 512], F32)
    SLOTA2 = at("SLOTA2", 75776, [128, 512], BF16)
    SLOTA3 = at("SLOTA3", 77824, [128, 512], BF16)
    WQB = [at("WQB0", 71680, [128, 1024], BF16),
           at("WQB1", 73728, [128, 1024], BF16)]
    SSB = at("SSB", 69632, [128, 1024], BF16)

    kvb_base = nc.lookup_mloc(KVB).addr
    WFCB = at("WFCB", 0, [128, 4 * 1024], BF16, base=kvb_base)
    MBUF = at("MBUF", 8192, [128, 512], F32, base=kvb_base)
    WP = [at("WP0", 10240, [128, 512], F32, base=kvb_base),
          at("WP1", 12288, [128, 512], F32, base=kvb_base)]
    SNB = at("SNB", 14336, [128, 512], F32, base=kvb_base)

    wkv_base = nc.lookup_mloc(WKV).addr
    WQF = at("WQF", 0, [128, DC * 1024], BF16, base=wkv_base)
    WDB = at("WDB", 0, [128, 2 * 2816], BF16, base=wkv_base)
    WGUB = at("WGUB", 11264, [128, 4 * 1024], BF16, base=wkv_base)
    RSTD1 = at("RSTD1", 19456, [1, 512], BF16, base=wkv_base)
    MURS1 = at("MURS1", 20480, [1, 512], BF16, base=wkv_base)
    RSTD2 = at("RSTD2", 21504, [1, 512], BF16, base=wkv_base)
    MURS2 = at("MURS2", 22528, [1, 512], BF16, base=wkv_base)
    SSG2 = at("SSG2", 23552, [128, 2 * 512], BF16, base=wkv_base)
    RECB4 = at("RECB4", 25600, [1, T], BF16, base=wkv_base)
    WGUB2 = at("WGUB2", 25600, [128, 2 * 1024], BF16, base=wkv_base)
    ROWW = at("ROWW", 29696, [1, 512], F32, base=wkv_base)

    qpf_base = nc.lookup_mloc(QPF).addr
    WPROJB = at("WPROJB", 0, [128, 4 * 4096], BF16, base=qpf_base)
    rrow_base = nc.lookup_mloc(R_RSTD).addr
    WFCB2 = at("WFCB2", 0, [128, 4 * 1024], BF16, base=rrow_base)


    PB = [nc.alloc_psum_tensor("pb%d" % i, [128, 512], F32) for i in range(8)]

    S = Sched()
    bank_rel = [("t", 0)] * 8

    def bwait(i):
        sem, val = bank_rel[i]
        S.w("tensor", sem, val)

    dcount = {}

    def dma(eng, stream, rot, out_ap, in_ap):
        idx = dcount.get(stream, 0)
        dcount[stream] = idx + 1
        sem = "%s%d" % (stream, idx % rot)
        cur = S.c.get(sem, 0)
        if cur > 0:
            S.q[eng].append(("w", sem, cur))
        v = S.op(eng, lambda e, o=out_ap, i=in_ap: e.dma_start(out=o, in_=i), sem, 16)
        return (sem, v)

    def sem_total(stream, rot):
        return [(("%s%d" % (stream, p)), S.c.get("%s%d" % (stream, p), 0))
                for p in range(rot) if S.c.get("%s%d" % (stream, p), 0) > 0]

    def mm(out_ap, lhsT, rhs, start, stop, sem=None):
        return S.op(
            "tensor",
            lambda e, o=out_ap, l=lhsT, r=rhs, st=start, sp=stop: e.matmul(
                o, l, r, start=st, stop=sp),
            sem)

    def vop(fn, sem=None):
        return S.op("vector", fn, sem)

    def sop(fn, sem=None):
        return S.op("scalar", fn, sem)

    def gop(fn, sem=None):
        return S.op("gpsimd", fn, sem)

    eps5 = CONSTF[0:1, 0:1]
    m02 = CONSTF[:, 1:2]
    eps6 = CONSTF[0:1, 2:3]
    onescol = ONESC[:, 1:2]
    ones1 = ONESC[:, 0:1]

    def xnc(d, a, b):
        return XN[:, d * T + a: d * T + b]

    def qpc(d, a, b):
        return QPF[:, d * T + a: d * T + b]

    def slotA(par):
        if par < 2:
            return SLOT_A[:, par * 512:(par + 1) * 512]
        return SLOTA2[:, :] if par == 2 else SLOTA3[:, :]

    vop(lambda e: e.memset(ONESC[:, 0:1], 1.0))
    vop(lambda e: e.memset(ONESC[:, 1:2], 1.0 / 1024.0))
    vop(lambda e: e.memset(ONESR[0:1, :], 1.0))
    vop(lambda e: e.memset(CONSTF[:, 0:1], 1.0e-5))
    vop(lambda e: e.memset(CONSTF[:, 1:2], -0.2))
    vop(lambda e: e.memset(CONSTF[:, 2:3], 1.0e-6), "v")

    dma("gpsimd", "cc", 1, COLS[:, :], cols[:, :])
    dma("gpsimd", "cc", 1, COLSGU[:, :], colsgu[:, :])
    g_consts = dma("gpsimd", "cc", 1, BKVREP[:, :], bkvrep[:, :])
    g_wkv = [dma("gpsimd", "wk", 2, WKV[:, d * 2048:(d + 1) * 2048],
                 wkv[d * 128:(d + 1) * 128, :]) for d in range(DC)]

    y_x = [dma("sync", "yx", 4, X0[:, d * T:(d + 1) * T], xT[d * 128:(d + 1) * 128, :])
           for d in range(DC)]

    v_xb = []
    for d in range(DC):
        S.wt("vector", y_x[d])
        v_xb.append(vop(lambda e, d=d: e.tensor_copy(
            xnc(d, 0, T), X0[:, d * T:(d + 1) * T]), "v"))

    slotA_rel = [("v", 0), ("v", 0), ("v", 0), ("v", 0)]
    v_rows = []
    for tg in range(TG):
        ia, ib = (tg % 2) * 2, (tg % 2) * 2 + 1
        pm, ps = PB[ia], PB[ib]
        bwait(ia)
        bwait(ib)
        t_last = None
        for d in range(DC):
            par = (tg * DC + d) % 4
            sem, val = slotA_rel[par]
            S.w("scalar", sem, val)
            S.wt("scalar", y_x[d])
            s_sq = sop(lambda e, d=d, tg=tg, par=par: e.activation(
                slotA(par), X0[:, d * T + tg * 512: d * T + tg * 512 + 512],
                AF.Square), "s")
            S.w("tensor", "v", v_xb[d])
            mm(pm[0:1, :], onescol, xnc(d, tg * 512, tg * 512 + 512),
               start=(d == 0), stop=(d == DC - 1))
            S.w("tensor", "s", s_sq)
            t_last = mm(ps[0:1, :], onescol, slotA(par),
                        start=(d == 0), stop=(d == DC - 1), sem="t")
            slotA_rel[par] = ("t", t_last)
        S.w("vector", "t", t_last)
        vop(lambda e, pm=pm: e.tensor_copy(SLOT_D[0:1, :], pm[0:1, :]))
        vop(lambda e: e.tensor_mul(SLOT_B[0:1, :], SLOT_D[0:1, :], SLOT_D[0:1, :]))
        v_var = vop(lambda e, ps=ps: e.tensor_sub(
            SLOT_C[0:1, :], ps[0:1, :], SLOT_B[0:1, :]), "v")
        S.w("scalar", "v", v_var)
        s_sqrt = sop(lambda e: e.activation(
            SLOT_B[0:1, :], SLOT_C[0:1, :], AF.Sqrt, bias=eps5), "s")
        S.w("vector", "s", s_sqrt)
        vop(lambda e: e.reciprocal(SLOT_C[0:1, :], SLOT_B[0:1, :]))
        vop(lambda e, tg=tg: e.tensor_copy(
            R_RSTD[0:1, tg * 512:(tg + 1) * 512], SLOT_C[0:1, :]))
        vr = vop(lambda e, tg=tg: e.tensor_mul(
            R_MURS[0:1, tg * 512:(tg + 1) * 512], SLOT_D[0:1, :],
            SLOT_C[0:1, :]), "v")
        v_rows.append(vr)
        bank_rel[ia] = ("v", vr)
        bank_rel[ib] = ("v", vr)

    v_xn = []
    for tg in range(TG):
        ia, ib = 4 + (tg % 2) * 2, 5 + (tg % 2) * 2
        pa, pbb = PB[ia], PB[ib]
        bwait(ia)
        bwait(ib)
        S.w("tensor", "v", v_rows[tg])
        mm(pa[:, :], ONESR[0:1, :], R_RSTD[0:1, tg * 512:(tg + 1) * 512], True, True)
        t_bc = mm(pbb[:, :], ONESR[0:1, :], R_MURS[0:1, tg * 512:(tg + 1) * 512],
                  True, True, "t")
        S.w("vector", "t", t_bc)
        vr = None
        for d in range(DC):
            vop(lambda e, d=d, tg=tg, pa=pa: e.tensor_mul(
                SLOT_B[:, :], X0[:, d * T + tg * 512: d * T + tg * 512 + 512], pa[:, :]))
            vr = vop(lambda e, d=d, tg=tg, pbb=pbb: e.tensor_sub(
                xnc(d, tg * 512, tg * 512 + 512), SLOT_B[:, :], pbb[:, :]),
                "v" if d == DC - 1 else None)
        v_xn.append(vr)
        bank_rel[ia] = ("v", vr)
        bank_rel[ib] = ("v", vr)

    wfcb_rel = [0, 0, 0, 0, 0, 0, 0, 0]
    tP0 = S.c["t"]
    for p8 in range(4, 8):
        wfcb_rel[p8] = tP0

    def wfcb_chunk(p8, c0, c1):
        if p8 < 4:
            return WFCB[:, p8 * 1024 + c0: p8 * 1024 + c1]
        return WFCB2[:, (p8 - 4) * 1024 + c0: (p8 - 4) * 1024 + c1]

    g_wfc_pend = {}

    def emit_wfc(fi):
        p8 = fi % 8
        S.w("gpsimd", "t", wfcb_rel[p8])
        g_wfc_pend[fi] = dma("gpsimd", "wf", 8, wfcb_chunk(p8, 0, 1024),
                             wfc_t[(fi % FC) * 128:(fi % FC + 1) * 128, :])

    for fi0 in range(8):
        emit_wfc(fi0)
    wprojb_rel = [0, 0, 0, 0]
    ysb_rel = {0: ("v", 0), 1: ("v", 0)}
    mbuf_gate = 0
    snb_s_gate = 0
    YSB = [SLOT_B, SLOT_D]
    wp_read_p = [0, 0]
    ssb_y = [None, None]
    p_cnt = 0
    v_spk_last = 0

    for tm in range(TM):
        g_wp = {}

        def emit_wp(dd):
            par4 = dd % 4
            S.w("gpsimd", "t", wprojb_rel[par4])
            g_wp[dd] = dma("gpsimd", "wp", 4, WPROJB[:, par4 * 4096:(par4 + 1) * 4096],
                           wproj_t[dd * 128:(dd + 1) * 128, :])

        v_hf = None
        for f in range(FC):
            fi = tm * FC + f
            p8 = fi % 8
            S.wt("tensor", g_wfc_pend[fi])
            S.w("tensor", "v", v_xn[2 * tm + 1])
            i0, i1 = 2 * (f % 2), 2 * (f % 2) + 1
            hb0, hb1 = PB[i0], PB[i1]
            bwait(i0)
            bwait(i1)
            t_m7 = None
            for d in range(DC):
                wt = wfcb_chunk(p8, d * 128, (d + 1) * 128)
                mm(hb0[:, :], wt, xnc(d, tm * 1024, tm * 1024 + 512),
                   start=(d == 0), stop=(d == DC - 1))
                t_m7 = mm(hb1[:, :], wt, xnc(d, tm * 1024 + 512, tm * 1024 + 1024),
                          start=(d == 0), stop=(d == DC - 1),
                          sem="t" if d == DC - 1 else None)
            wfcb_rel[p8] = t_m7
            if fi + 8 < TM * FC:
                emit_wfc(fi + 8)
            if f in (24, 27, 30):
                emit_wp((f - 24) // 3)
            for th in range(2):
                par2 = (f * 2 + th) % 4
                hb = hb0 if th == 0 else hb1
                sem, val = slotA_rel[par2]
                S.w("scalar", sem, val)
                S.w("scalar", "t", t_m7)
                s_r = sop(lambda e, hb=hb, par2=par2: e.activation(
                    slotA(par2), hb[:, :], AF.Relu), "s")
                bank_rel[i0 if th == 0 else i1] = ("s", s_r)
                s_hf = sop(lambda e, f=f, th=th, par2=par2: e.activation(
                    HF[:, f * 1024 + th * 512: f * 1024 + (th + 1) * 512],
                    slotA(par2), AF.Square), "s")
                slotA_rel[par2] = ("s", s_hf)
        hf_s_last = S.c.get("s", 0)
        hf_v_last = S.c.get("v", 0)
        for dout in range(DC):
            par = dout % 4
            if dout + 3 < DC:
                emit_wp(dout + 3)
            S.wt("tensor", g_wp[dout])
            S.w("tensor", "s", hf_s_last)
            S.w("tensor", "v", hf_v_last)
            i0, i1 = 4 + 2 * (dout % 2), 5 + 2 * (dout % 2)
            yb0, yb1 = PB[i0], PB[i1]
            bwait(i0)
            bwait(i1)
            t_m8 = None
            for f in range(FC):
                wt = WPROJB[:, par * 4096 + f * 128: par * 4096 + (f + 1) * 128]
                mm(yb0[:, :], wt, HF[:, f * 1024: f * 1024 + 512],
                   start=(f == 0), stop=(f == FC - 1))
                t_m8 = mm(yb1[:, :], wt, HF[:, f * 1024 + 512: f * 1024 + 1024],
                          start=(f == 0), stop=(f == FC - 1),
                          sem="t" if f == FC - 1 else None)
            wprojb_rel[par] = t_m8
            for th in range(2):
                yb = yb0 if th == 0 else yb1
                ypar = th
                sem, val = ysb_rel[ypar]
                S.w("scalar", sem, val)
                S.w("scalar", "t", t_m8)
                s_y = sop(lambda e, yb=yb, ypar=ypar: e.activation(
                    YSB[ypar][:, :], yb[:, :], AF.Copy), "s")
                bank_rel[i0 if th == 0 else i1] = ("s", s_y)
                y = YSB[ypar]
                S.w("scalar", "v", mbuf_gate)
                s_mb = sop(lambda e, y=y: e.activation(
                    MBUF[:, :], y[:, :], AF.Copy, scale=1.9), "s")
                S.w("vector", "s", s_y)
                S.w("vector", "s", snb_s_gate)
                vop(lambda e, y=y: e.tensor_scalar(
                    SNB[:, :], y[:, :], 1.0, -1.0, OP.is_gt, OP.mult))
                S.w("vector", "s", s_mb)
                vop(lambda e: e.tensor_add(MBUF[:, :], MBUF[:, :], SNB[:, :]))
                S.w("vector", "p", wp_read_p[0])
                v4 = vop(lambda e: e.tensor_scalar(
                    WP[0][:, :], MBUF[:, :], 1.0, -1.0, OP.is_gt, OP.mult), "v")
                S.w("gpsimd", "v", v4)
                p_cnt = gop(lambda e: e.tensor_add(
                    SNB[:, :], SNB[:, :], WP[0][:, :]), "p")
                wp_read_p[0] = p_cnt
                vlast = None
                for step in range(3):
                    wp_cur = WP[step % 2]
                    nxt = (step + 1) % 2
                    vop(lambda e, y=y: e.scalar_tensor_tensor(
                        MBUF[:, :], MBUF[:, :], 0.9, y[:, :], OP.mult, OP.add))
                    vop(lambda e, wp=wp_cur: e.tensor_add(
                        MBUF[:, :], MBUF[:, :], wp[:, :]))
                    S.w("vector", "p", wp_read_p[nxt])
                    vlast = vop(lambda e, wp=WP[nxt]: e.tensor_scalar(
                        wp[:, :], MBUF[:, :], 1.0, -1.0, OP.is_gt, OP.mult), "v")
                    S.w("gpsimd", "v", vlast)
                    p_cnt = gop(lambda e, wp=WP[nxt]: e.tensor_add(
                        SNB[:, :], SNB[:, :], wp[:, :]), "p")
                    wp_read_p[nxt] = p_cnt
                ysb_rel[ypar] = ("v", vlast)
                mbuf_gate = vlast
                spar = (dout * 2 + th) % 2
                S.w("scalar", "p", p_cnt)
                S.wt("scalar", ssb_y[spar])
                s_sn = sop(lambda e, spar=spar: e.activation(
                    SSB[:, spar * 512:(spar + 1) * 512], SNB[:, :], AF.Copy), "s")
                snb_s_gate = s_sn
                S.w("sync", "s", s_sn)
                yd = dma("sync", "ss", 2,
                         ssum_hbm[dout * 128:(dout + 1) * 128,
                                  tm * 1024 + th * 512: tm * 1024 + (th + 1) * 512],
                         SSB[:, spar * 512:(spar + 1) * 512])
                ssb_y[spar] = yd

    for tok in sem_total("ss", 2):
        S.w("scalar", tok[0], tok[1])
    kv_stage_rel = 0
    for tok in sem_total("wk", 2):
        S.w("tensor", tok[0], tok[1])
    S.wt("vector", g_consts)
    v_kvdone = None
    for tt in range(TT):
        ks = [0, 1, 2, 3] if tt % 2 == 0 else [4, 5, 6, 7]
        for i in ks:
            bwait(i)
        S.w("tensor", "v", v_xn[tt // 4])
        t_kv = None
        for d in range(DC):
            st = XN[:, d * T + tt * 128: d * T + (tt + 1) * 128]
            for j in range(4):
                t_kv = mm(PB[ks[j]][:, :], st,
                          WKV[:, d * 2048 + j * 512: d * 2048 + (j + 1) * 512],
                          start=(d == 0), stop=(d == DC - 1),
                          sem="t" if (d == DC - 1 and j == 3) else None)
        for eh in range(2):
            tb = SLOT_B if eh == 0 else SLOT_G
            S.w("vector", "t", t_kv)
            v_t1 = vop(lambda e, tb=tb, eh=eh, ks=ks: e.tensor_add(
                tb[:, :], PB[ks[eh]][:, :], BKVREP[:, eh * 512:(eh + 1) * 512]), "v")
            bank_rel[ks[eh]] = ("v", v_t1)
            S.w("scalar", "v", max(v_t1, kv_stage_rel))
            sop(lambda e, tb=tb: e.activation(SLOT_C[:, :], tb[:, :], AF.Exp))
            s_rl = sop(lambda e, tb=tb: e.activation(SLOT_F[:, :], tb[:, :], AF.Relu), "s")
            S.w("vector", "s", s_rl)
            v_kp = vop(lambda e, tt=tt, eh=eh: e.scalar_tensor_tensor(
                KP[:, tt * 1024 + eh * 512: tt * 1024 + (eh + 1) * 512],
                SLOT_C[:, :], 1.0, SLOT_F[:, :], OP.min, OP.add), "v")
            kv_stage_rel = v_kp
        for eh in range(2):
            S.w("vector", "t", t_kv)
            v_v = vop(lambda e, tt=tt, eh=eh, ks=ks: e.tensor_add(
                VTOK[:, tt * 1024 + eh * 512: tt * 1024 + (eh + 1) * 512],
                PB[ks[2 + eh]][:, :],
                BKVREP[:, 1024 + eh * 512: 1024 + (eh + 1) * 512]), "v")
            bank_rel[ks[2 + eh]] = ("v", v_v)
        v_kvdone = v_v

    S.w("gpsimd", "t", S.c["t"])
    g_wq = [dma("gpsimd", "wq", 2, WQF[:, d * 1024:(d + 1) * 1024],
                wq[d * 128:(d + 1) * 128, :]) for d in range(DC)]
    S.w("scalar", "v", v_kvdone)
    q_stage_rel = [0, 0]
    q_stage_rel2 = [0, 0]
    v_q = []
    t_q_last = None
    for tg in range(TG):
        for i in range(8):
            bwait(i)
        t_d = None
        for d in range(DC):
            S.wt("tensor", g_wq[d])
            for e_ in range(8):
                t_d = mm(PB[e_][:, :], WQF[:, d * 1024 + e_ * 128: d * 1024 + (e_ + 1) * 128],
                         xnc(d, tg * 512, tg * 512 + 512),
                         start=(d == 0), stop=(d == DC - 1),
                         sem="t" if e_ == 7 else None)
        t_q_last = t_d
        vq = None
        for e_ in range(8):
            ex = SLOT_C if e_ % 2 == 0 else SLOT_B
            rl = SLOT_F if e_ % 2 == 0 else SLOT_G
            bq = COLS[:, e_ * 5: e_ * 5 + 1]
            S.w("scalar", "t", t_d)
            S.wt("scalar", g_consts)
            S.w("scalar", "v", q_stage_rel[e_ % 2])
            sop(lambda e, e_=e_, ex=ex, bq=bq: e.activation(
                ex[:, :], PB[e_][:, :], AF.Exp, bias=bq))
            s_rl = sop(lambda e, e_=e_, rl=rl, bq=bq: e.activation(
                rl[:, :], PB[e_][:, :], AF.Relu, bias=bq), "s")
            bank_rel[e_] = ("s", s_rl)
            S.w("vector", "s", s_rl)
            vq = vop(lambda e, e_=e_, tg=tg, ex=ex, rl=rl: e.scalar_tensor_tensor(
                qpc(e_, tg * 512, tg * 512 + 512), ex[:, :], 1.0, rl[:, :],
                OP.min, OP.add), "v")
            q_stage_rel[e_ % 2] = vq
        v_q.append(vq)

    v_kv2 = []
    t_p2_last = None
    for d in range(DC):
        ks = [0, 1, 2] if d % 2 == 0 else [3, 4, 5]
        for i in ks:
            bwait(i)
        S.w("tensor", "v", v_kvdone)
        t_2 = None
        for si in range(TT):
            st = KP[:, si * 1024 + d * 128: si * 1024 + (d + 1) * 128]
            mm(PB[ks[0]][:, :], st, VTOK[:, si * 1024: si * 1024 + 512],
               start=(si == 0), stop=(si == TT - 1))
            mm(PB[ks[1]][:, :], st, VTOK[:, si * 1024 + 512: si * 1024 + 1024],
               start=(si == 0), stop=(si == TT - 1))
            t_2 = mm(PB[ks[2]][:, 0:1], st, ones1,
                     start=(si == 0), stop=(si == TT - 1),
                     sem="t" if si == TT - 1 else None)
        t_p2_last = t_2
        S.w("vector", "t", t_2)
        vop(lambda e, d=d, ks=ks: e.tensor_copy(
            KVB[:, d * 1024: d * 1024 + 512], PB[ks[0]][:, :]))
        vop(lambda e, d=d, ks=ks: e.tensor_copy(
            KVB[:, d * 1024 + 512: d * 1024 + 1024], PB[ks[1]][:, :]))
        v2 = vop(lambda e, d=d, ks=ks: e.tensor_copy(
            KSUM[:, d: d + 1], PB[ks[2]][:, 0:1]), "v")
        v_kv2.append(v2)
        for i in ks:
            bank_rel[i] = ("v", v2)

    v_den4 = []
    for g in range(TG):
        bwait(6)
        S.w("tensor", "v", v_kv2[-1])
        S.w("tensor", "v", v_q[g])
        t_den = None
        for d in range(DC):
            t_den = mm(PB[6][0:1, :], KSUM[:, d: d + 1],
                       qpc(d, g * 512, g * 512 + 512),
                       start=(d == 0), stop=(d == DC - 1),
                       sem="t" if d == DC - 1 else None)
        S.w("vector", "t", t_den)
        vop(lambda e: e.tensor_scalar_add(SLOT_B[0:1, :], PB[6][0:1, :], 1.0e-6))
        vop(lambda e: e.reciprocal(SLOT_B[0:1, :], SLOT_B[0:1, :]))
        vd0 = vop(lambda e, g=g: e.tensor_copy(
            RECB4[0:1, g * 512:(g + 1) * 512], SLOT_B[0:1, :]), "v")
        bank_rel[6] = ("v", vd0)
        bwait(7)
        S.w("tensor", "v", vd0)
        t_rb0 = mm(PB[7][:, :], ONESR[0:1, :],
                   RECB4[0:1, g * 512:(g + 1) * 512], True, True, "t")
        S.w("vector", "t", t_rb0)
        vd = None
        for d in range(DC):
            vd = vop(lambda e, d=d, g=g: e.tensor_mul(
                qpc(d, g * 512, g * 512 + 512), qpc(d, g * 512, g * 512 + 512),
                PB[7][:, :]), "v" if d == DC - 1 else None)
        v_den4.append(vd)
        bank_rel[7] = ("v", vd)
    t_den_done = S.c["t"]

    wdb_rel = [0, 0]
    wgu_rel3 = [0, 0, t_den_done]
    ssg_rel = [0, 0]
    outt2 = [None, None]
    gb_rel = [("p", 0), ("p", 0)]
    xr2_rel = [("p", 0), ("p", 0)]
    cscr_rel = ("p", 0)
    g5_last = 0
    v3_last = 0
    comb_vf = None
    comb_tbc3 = None
    ssg_started = [False]
    g_ssg = {}
    g_wgu = {}

    NUMBANKS = [0, 1, 2, 3, 6, 7, 0, 1]

    def emit_num(gx, es):
        ts_ = []
        for e_ in es:
            bk = NUMBANKS[e_]
            bwait(bk)
            if e_ == 0:
                S.w("tensor", "v", v_den4[gx])
            t_num = None
            for d in range(DC):
                t_num = mm(PB[bk][:, :],
                           KVB[:, d * 1024 + e_ * 128: d * 1024 + (e_ + 1) * 128],
                           qpc(d, gx * 512, gx * 512 + 512),
                           start=(d == 0), stop=(d == DC - 1),
                           sem="t" if d == DC - 1 else None)
            ts_.append(t_num)
        return ts_

    def wgu_aps(par3):
        if par3 < 2:
            return (WGUB, par3 * 1024, WGUB, (2 + par3) * 1024)
        return (WGUB2, 0, WGUB2, 1024)

    def emit_wgu(h):
        par3 = h % 3
        gt_, go_, ut_, uo_ = wgu_aps(par3)
        S.w("gpsimd", "t", wgu_rel3[par3])
        g_wg = dma("gpsimd", "wg", 2, gt_[:, go_:go_ + 1024],
                   wg_t[h * 128:(h + 1) * 128, :])
        g_wu = dma("gpsimd", "wu", 2, ut_[:, uo_:uo_ + 1024],
                   wu_t[h * 128:(h + 1) * 128, :])
        return (g_wg, g_wu)

    def emit_ssg(cg, dd):
        par = dd % 2
        S.w("gpsimd", "s", ssg_rel[par])
        g_ssg[dd] = dma("gpsimd", "sg", 2, SSG2[:, par * 512:(par + 1) * 512],
                        ssum_hbm[dd * 128:(dd + 1) * 128,
                                 cg * 512:(cg + 1) * 512])

    def combine_start(cg):
        if not ssg_started[0]:
            for tok in sem_total("ss", 2):
                S.w("gpsimd", tok[0], tok[1])
            ssg_started[0] = True
        emit_ssg(cg, 0)
        emit_ssg(cg, 1)

    def combine_dout(cg, dout, tail=False):
        nonlocal cscr_rel, g5_last, v3_last
        par = dout % 2
        on_v = tail and par == 1
        if dout < 2:
            S.w("sync", "v", comb_vf)
        S.wt("sync", xr2_rel[par])
        XR2 = SLOT_D if par == 0 else SLOT_E
        y_x2 = dma("sync", "x2", 2, XR2[:, :],
                   xT[dout * 128:(dout + 1) * 128, cg * 512:(cg + 1) * 512])
        GB = SLOT_G if par == 0 else GBUF2
        OB = SLOT_F if par == 0 else SLOT_C
        S.wt("scalar", g_ssg[dout])
        if dout == 0:
            S.w("scalar", "v", comb_vf)
        S.wt("scalar", gb_rel[par])
        s_g = sop(lambda e, par=par, GB=GB: e.activation(
            GB[:, :], SSG2[:, par * 512:(par + 1) * 512],
            AF.Silu, scale=-0.2), "s")
        ssg_rel[par] = s_g
        if dout == 0:
            S.w("vector", "t", comb_tbc3)
        S.wt("vector", cscr_rel)
        vop(lambda e, dout=dout: e.tensor_mul(
            CSCR[:, :], Z[:, dout * 512:(dout + 1) * 512], PB[4][:, :]))
        vop(lambda e: e.tensor_sub(CSCR[:, :], CSCR[:, :], PB[5][:, :]))
        v3 = vop(lambda e, dout=dout: e.tensor_scalar(
            CSCR[:, :], CSCR[:, :],
            COLS[:, dout * 5 + 3: dout * 5 + 4], COLS[:, dout * 5 + 4: dout * 5 + 5],
            OP.mult, OP.add), "v")
        v3_last = v3
        if on_v:
            S.w("vector", "s", s_g)
            S.wt("vector", outt2[par])
            g4v = vop(lambda e, GB=GB, OB=OB: e.tensor_mul(
                OB[:, :], CSCR[:, :], GB[:, :]), "v")
            cscr_rel = ("v", g4v)
            gb_rel[par] = ("v", g4v)
            S.wt("vector", y_x2)
            g5 = vop(lambda e, XR2=XR2, OB=OB: e.tensor_add(
                OB[:, :], OB[:, :], XR2[:, :]), "v")
            xr2_rel[par] = ("v", g5)
            S.w("sync", "v", g5)
        else:
            S.w("gpsimd", "v", v3)
            S.w("gpsimd", "s", s_g)
            S.wt("gpsimd", outt2[par])
            g4 = gop(lambda e, GB=GB, OB=OB: e.tensor_mul(
                OB[:, :], CSCR[:, :], GB[:, :]), "p")
            cscr_rel = ("p", g4)
            gb_rel[par] = ("p", g4)
            S.wt("gpsimd", y_x2)
            g5 = gop(lambda e, XR2=XR2, OB=OB: e.tensor_add(
                OB[:, :], OB[:, :], XR2[:, :]), "p")
            xr2_rel[par] = ("p", g5)
            g5_last = g5
            S.w("sync", "p", g5)
        outt2[par] = dma("sync", "ou", 2,
                         outT[dout * 128:(dout + 1) * 128, cg * 512:(cg + 1) * 512],
                         OB[:, :])
        if dout + 2 < DC:
            emit_ssg(cg, dout + 2)

    pend_tnum = None
    for g in range(TG):
        if g == 0:
            S.w("gpsimd", "t", t_q_last)
            S.w("vector", "t", t_p2_last)
            S.wt("vector", g_consts)
            S.wt("scalar", g_consts)
            vop(lambda e: e.memset(FUSED[32:64, 21 * 512: 22 * 512], 1.0), "v")
        for hh in range(3):
            g_wgu[hh] = emit_wgu(hh)
        if g > 0:
            combine_start(g - 1)

        v_z = []
        s_z2 = []
        if pend_tnum is None:
            pend_tnum = emit_num(g, range(6))
        tnums, pend_tnum = pend_tnum, None
        for e_ in range(DC):
            if g > 0:
                combine_dout(g - 1, e_)
            bk = NUMBANKS[e_]
            pe = PB[bk]
            if e_ >= 6:
                tnums.append(emit_num(g, [e_])[0])
            S.w("vector", "t", tnums[e_])
            vz = vop(lambda e, e_=e_, g=g, pe=pe: e.tensor_add(
                Z[:, e_ * 512:(e_ + 1) * 512], pe[:, :],
                xnc(e_, g * 512, g * 512 + 512)), "v")
            v_z.append(vz)
            bank_rel[bk] = ("v", vz)
            S.w("scalar", "v", vz)
            sop(lambda e, e_=e_: e.activation(
                X1F[:, e_ * 512:(e_ + 1) * 512], Z[:, e_ * 512:(e_ + 1) * 512], AF.Copy))
            sz = sop(lambda e, e_=e_: e.activation(
                X1B[:, e_ * 512:(e_ + 1) * 512], Z[:, e_ * 512:(e_ + 1) * 512],
                AF.Square), "s")
            s_z2.append(sz)
        if g > 0:
            bank_rel[4] = ("v", v3_last)
            bank_rel[5] = ("v", v3_last)

        bwait(2)
        bwait(3)
        t_st1 = None
        for e_ in range(DC):
            S.w("tensor", "s", s_z2[e_])
            mm(PB[2][0:1, :], onescol, X1F[:, e_ * 512:(e_ + 1) * 512],
               start=(e_ == 0), stop=(e_ == DC - 1))
            t_st1 = mm(PB[3][0:1, :], onescol, X1B[:, e_ * 512:(e_ + 1) * 512],
                       start=(e_ == 0), stop=(e_ == DC - 1), sem="t")
        S.w("vector", "t", t_st1)
        vop(lambda e: e.tensor_copy(ROWW[0:1, :], PB[2][0:1, :]))
        vop(lambda e: e.tensor_mul(SLOT_B[0:1, :], ROWW[0:1, :], ROWW[0:1, :]))
        v_var = vop(lambda e: e.tensor_sub(SLOT_B[0:1, :], PB[3][0:1, :], SLOT_B[0:1, :]), "v")
        S.w("scalar", "v", v_var)
        s_sq1 = sop(lambda e: e.activation(
            SLOT_B[0:1, :], SLOT_B[0:1, :], AF.Sqrt, bias=eps5), "s")
        S.w("vector", "s", s_sq1)
        vop(lambda e: e.reciprocal(SLOT_B[0:1, :], SLOT_B[0:1, :]))
        vop(lambda e: e.tensor_copy(RSTD1[0:1, :], SLOT_B[0:1, :]))
        v_r1 = vop(lambda e: e.tensor_mul(
            MURS1[0:1, :], ROWW[0:1, :], SLOT_B[0:1, :]), "v")
        bank_rel[2] = ("v", v_r1)
        bank_rel[3] = ("v", v_r1)

        bwait(6)
        bwait(7)
        S.w("tensor", "v", v_r1)
        mm(PB[6][:, :], ONESR[0:1, :], RSTD1[0:1, :], True, True)
        t_bc2 = mm(PB[7][:, :], ONESR[0:1, :], MURS1[0:1, :], True, True, "t")
        v_x1 = []
        s_x1 = []
        for e_ in range(DC):
            S.w("vector", "t", t_bc2)
            vop(lambda e, e_=e_: e.tensor_mul(
                SLOT_B[:, :], Z[:, e_ * 512:(e_ + 1) * 512], PB[6][:, :]))
            vop(lambda e: e.tensor_sub(SLOT_B[:, :], SLOT_B[:, :], PB[7][:, :]))
            vx = vop(lambda e, e_=e_: e.tensor_scalar(
                X1[:, e_ * 512:(e_ + 1) * 512], SLOT_B[:, :],
                COLS[:, e_ * 5 + 1: e_ * 5 + 2], COLS[:, e_ * 5 + 2: e_ * 5 + 3],
                OP.mult, OP.add), "v")
            v_x1.append(vx)
            S.w("scalar", "t", t_st1)
            S.w("scalar", "v", vx)
            s_x1.append(sop(lambda e, e_=e_: e.activation(
                X1F[:, e_ * 512:(e_ + 1) * 512], X1[:, e_ * 512:(e_ + 1) * 512],
                AF.Copy), "s"))
        bank_rel[6] = ("v", v_x1[-1])
        bank_rel[7] = ("v", v_x1[-1])

        v_f = None
        t_5 = None
        for h in range(HC):
            par3 = h % 3
            gt_, go_, ut_, uo_ = wgu_aps(par3)
            g_wg, g_wu = g_wgu[h]
            ig, iu = [(0, 1), (2, 3), (6, 7)][h % 3]
            bwait(ig)
            bwait(iu)
            S.wt("tensor", g_wg)
            S.wt("tensor", g_wu)
            for d in range(DC):
                if h == 0:
                    S.w("tensor", "s", s_x1[d])
                mm(PB[ig][:, :], gt_[:, go_ + d * 128: go_ + (d + 1) * 128],
                   X1F[:, d * 512:(d + 1) * 512], start=(d == 0), stop=(d == DC - 1))
                t_5 = mm(PB[iu][:, :], ut_[:, uo_ + d * 128: uo_ + (d + 1) * 128],
                         X1F[:, d * 512:(d + 1) * 512], start=(d == 0), stop=(d == DC - 1),
                         sem="t" if d == DC - 1 else None)
            wgu_rel3[par3] = t_5
            if h + 3 < HC:
                g_wgu[h + 3] = emit_wgu(h + 3)
            rows = 128 if h < HC - 1 else HLAST
            SIL = SLOT_D if h % 2 == 0 else SLOT_F
            UPT = SLOT_E if h % 2 == 0 else SLOT_G
            S.w("scalar", "t", t_5)
            if h == 0:
                S.wt("scalar", outt2[0])
                S.wt("scalar", outt2[1])
                if g5_last:
                    S.w("scalar", "p", g5_last)
            sop(lambda e, h=h, ig=ig, rows=rows, SIL=SIL: e.activation(
                SIL[0:rows, :], PB[ig][0:rows, :], AF.Silu,
                bias=COLSGU[0:rows, 2 * h: 2 * h + 1]))
            s_5 = sop(lambda e, h=h, iu=iu, rows=rows, UPT=UPT: e.activation(
                UPT[0:rows, :], PB[iu][0:rows, :], AF.Identity,
                bias=COLSGU[0:rows, 2 * h + 1: 2 * h + 2]), "s")
            bank_rel[ig] = ("s", s_5)
            bank_rel[iu] = ("s", s_5)
            S.w("vector", "s", s_5)
            v_f = vop(lambda e, h=h, rows=rows, SIL=SIL, UPT=UPT: e.tensor_mul(
                FUSED[0:rows, h * 512:(h + 1) * 512], SIL[0:rows, :], UPT[0:rows, :]), "v")

        v_u = []
        s_u2 = []
        t_6 = None
        for dout in range(DC):
            par = dout % 2
            S.w("gpsimd", "t", wdb_rel[par])
            g_wd = dma("gpsimd", "wd", 2, WDB[:, par * 2816:(par + 1) * 2816],
                       wd_t[dout * 128:(dout + 1) * 128, :])
            ib_ = dout % 4
            bwait(ib_)
            S.wt("tensor", g_wd)
            S.w("tensor", "v", v_f)
            for h in range(HC):
                K = 128 if h < HC - 1 else HLAST + 1
                t_6 = mm(PB[ib_][:, :],
                         WDB[0:K, par * 2816 + h * 128: par * 2816 + (h + 1) * 128],
                         FUSED[0:K, h * 512:(h + 1) * 512],
                         start=(h == 0), stop=(h == HC - 1),
                         sem="t" if h == HC - 1 else None)
            wdb_rel[par] = t_6
            S.w("vector", "t", t_6)
            if dout == 0:
                S.w("vector", "t", t_5)
            vu = vop(lambda e, dout=dout, ib_=ib_: e.tensor_add(
                Z[:, dout * 512:(dout + 1) * 512], PB[ib_][:, :],
                X1[:, dout * 512:(dout + 1) * 512]), "v")
            v_u.append(vu)
            bank_rel[ib_] = ("v", vu)
            S.w("scalar", "v", vu)
            if dout == 0:
                S.w("scalar", "t", t_5)
                S.w("scalar", "t", t_st1)
            sop(lambda e, dout=dout: e.activation(
                X1F[:, dout * 512:(dout + 1) * 512],
                Z[:, dout * 512:(dout + 1) * 512], AF.Copy))
            su = sop(lambda e, dout=dout: e.activation(
                X1B[:, dout * 512:(dout + 1) * 512],
                Z[:, dout * 512:(dout + 1) * 512], AF.Square), "s")
            s_u2.append(su)

        bwait(2)
        bwait(3)
        t_st2 = None
        for dout in range(DC):
            S.w("tensor", "s", s_u2[dout])
            mm(PB[2][0:1, :], onescol, X1F[:, dout * 512:(dout + 1) * 512],
               start=(dout == 0), stop=(dout == DC - 1))
            t_st2 = mm(PB[3][0:1, :], onescol, X1B[:, dout * 512:(dout + 1) * 512],
                       start=(dout == 0), stop=(dout == DC - 1), sem="t")
        S.w("vector", "t", t_st2)
        vop(lambda e: e.tensor_copy(ROWW[0:1, :], PB[2][0:1, :]))
        vop(lambda e: e.tensor_mul(SLOT_B[0:1, :], ROWW[0:1, :], ROWW[0:1, :]))
        v_var2 = vop(lambda e: e.tensor_sub(SLOT_B[0:1, :], PB[3][0:1, :], SLOT_B[0:1, :]), "v")
        S.w("scalar", "v", v_var2)
        s_sq2 = sop(lambda e: e.activation(
            SLOT_B[0:1, :], SLOT_B[0:1, :], AF.Sqrt, bias=eps5), "s")
        S.w("vector", "s", s_sq2)
        vop(lambda e: e.reciprocal(SLOT_B[0:1, :], SLOT_B[0:1, :]))
        vop(lambda e: e.tensor_copy(RSTD2[0:1, :], SLOT_B[0:1, :]))
        v_r2 = vop(lambda e: e.tensor_mul(
            MURS2[0:1, :], ROWW[0:1, :], SLOT_B[0:1, :]), "v")
        bank_rel[2] = ("v", v_r2)
        bank_rel[3] = ("v", v_r2)

        bwait(4)
        bwait(5)
        S.w("tensor", "v", v_r2)
        mm(PB[4][:, :], ONESR[0:1, :], RSTD2[0:1, :], True, True)
        t_bc3 = mm(PB[5][:, :], ONESR[0:1, :], MURS2[0:1, :], True, True, "t")
        if g + 1 < TG:
            pend_tnum = emit_num(g + 1, range(6))
        comb_vf = v_f
        comb_tbc3 = t_bc3

    combine_start(TG - 1)
    for dout in range(DC):
        combine_dout(TG - 1, dout, tail=True)

    for tok in sem_total("ou", 2):
        S.w("sync", tok[0], tok[1])

    sems = {n: nc.alloc_semaphore(n) for n in sorted(S.c.keys())}
    with nc.Block() as block:
        def make(engname):
            def body(eng):
                for item in S.q[engname]:
                    if item[0] == "w":
                        eng.wait_ge(sems[item[1]], item[2])
                    else:
                        ins = item[1](eng)
                        if item[2] is not None:
                            ins.then_inc(sems[item[2]], item[3])
            return body

        block.tensor(make("tensor"))
        block.vector(make("vector"))
        block.scalar(make("scalar"))
        block.gpsimd(make("gpsimd"))
        block.sync(make("sync"))

    return nc


def _pack_weights(i):
    f32 = np.float32
    Wqkv = np.ascontiguousarray(np.asarray(i["L0_attn_Wqkv"], dtype=f32))
    bqkv = np.asarray(i["L0_attn_bqkv"], dtype=f32)
    Wg = np.asarray(i["L0_attn_Wg"], dtype=f32)
    bg = np.asarray(i["L0_attn_bg"], dtype=f32)
    Wu = np.asarray(i["L0_attn_Wu"], dtype=f32)
    bu = np.asarray(i["L0_attn_bu"], dtype=f32)
    Wd = np.asarray(i["L0_attn_Wd"], dtype=f32)
    bd = np.asarray(i["L0_attn_bd"], dtype=f32)
    g1 = np.asarray(i["L0_attn_g1"], dtype=f32)
    b1 = np.asarray(i["L0_attn_b1"], dtype=f32)
    g2 = np.asarray(i["L0_attn_g2"], dtype=f32)
    b2 = np.asarray(i["L0_attn_b2"], dtype=f32)
    W_fc = np.asarray(i["L0_ffn_W_fc"], dtype=f32)
    W_proj = np.asarray(i["L0_ffn_W_proj"], dtype=f32)

    d = {}
    d["wq"] = np.ascontiguousarray(Wqkv[:, 0:1024])
    d["wkv"] = np.ascontiguousarray(Wqkv[:, 1024:3072])

    wgp = np.zeros((D, HC * 128), dtype=f32)
    wgp[:, :2730] = Wg
    d["wg_t"] = np.ascontiguousarray(
        wgp.reshape(DC, 128, HC, 128).transpose(2, 1, 0, 3).reshape(HC * 128, 1024))
    wup = np.zeros((D, HC * 128), dtype=f32)
    wup[:, :2730] = Wu
    d["wu_t"] = np.ascontiguousarray(
        wup.reshape(DC, 128, HC, 128).transpose(2, 1, 0, 3).reshape(HC * 128, 1024))

    wda = np.zeros((HC * 128, D), dtype=f32)
    wda[:2730] = Wd
    wda[2730] = bd
    d["wd_t"] = np.ascontiguousarray(
        wda.reshape(HC, 128, DC, 128).transpose(2, 1, 0, 3).reshape(DC * 128, HC * 128))

    d["wfc_t"] = np.ascontiguousarray(
        W_fc.reshape(FC, 128, DC, 128).transpose(0, 3, 2, 1).reshape(FC * 128, 1024))
    d["wproj_t"] = np.ascontiguousarray(
        W_proj.reshape(DC, 128, FC, 128).transpose(0, 3, 2, 1).reshape(DC * 128, FC * 128))

    import ml_dtypes
    for kk in ("wq", "wkv", "wg_t", "wu_t", "wd_t", "wfc_t", "wproj_t"):
        d[kk] = np.ascontiguousarray(d[kk].astype(ml_dtypes.bfloat16))

    colsmat = np.stack([bqkv[0:1024], g1, b1, g2, b2], axis=1)
    d["cols"] = np.ascontiguousarray(
        colsmat.reshape(DC, 128, 5).transpose(1, 0, 2).reshape(128, 40))
    gum = np.zeros((HC * 128, 2), dtype=f32)
    gum[:2730, 0] = bg
    gum[:2730, 1] = bu
    d["colsgu"] = np.ascontiguousarray(
        gum.reshape(HC, 128, 2).transpose(1, 0, 2).reshape(128, 44))
    d["bkvrep"] = np.ascontiguousarray(
        np.broadcast_to(bqkv[1024:3072], (128, 2048)))
    return d


def kernel(**inputs):
    x = np.asarray(inputs["L0_attn_x"], dtype=np.float32)
    wd = _pack_weights(inputs)

    nc = _build()

    in_maps = []
    for c in range(8):
        m = dict(wd)
        m["xT"] = np.ascontiguousarray(x[c].T)
        in_maps.append(m)

    res = run_bass_kernel_spmd(nc, in_maps, core_ids=list(range(8)))
    out = np.stack([np.asarray(r["outT"]).T for r in res.results], axis=0)
    return np.ascontiguousarray(out.astype(np.float32))


# version 23
# speedup vs baseline: 1.1199x; 1.0112x over previous
import numpy as np

import concourse.bass as bass
import concourse.mybir as mybir
from concourse.bass_utils import run_bass_kernel_spmd

F32 = mybir.dt.float32
BF16 = mybir.dt.bfloat16
AF = mybir.ActivationFunctionType
OP = mybir.AluOpType

D = 1024
T = 2048
DC = 8
TT = 16
TG = 4
HC = 22
HLAST = 42
FC = 32
TM = 2

ENGS = ("tensor", "vector", "scalar", "gpsimd", "sync")


class Sched:

    def __init__(self):
        self.q = {e: [] for e in ENGS}
        self.c = {}

    def w(self, eng, sem, val):
        cur = self.c.get(sem, 0)
        assert 0 <= val <= cur, ("wait on future inc", eng, sem, val, cur)
        if val > 0:
            self.q[eng].append(("w", sem, val))

    def op(self, eng, fn, sem=None, n=1):
        v = None
        if sem is not None:
            self.c[sem] = self.c.get(sem, 0) + n
            v = self.c[sem]
        self.q[eng].append(("i", fn, sem, n))
        return v

    def wt(self, eng, tok):
        if tok is not None:
            self.w(eng, tok[0], tok[1])


_RACE_CHECK = False


def _build():
    nc = bass.Bass("TRN2", target_bir_lowering=False,
                   detect_race_conditions=_RACE_CHECK)

    xT = nc.dram_tensor("xT", [D, T], F32, kind="ExternalInput")
    outT = nc.dram_tensor("outT", [D, T], F32, kind="ExternalOutput")
    wq = nc.dram_tensor("wq", [D, 1024], BF16, kind="ExternalInput")
    wkv = nc.dram_tensor("wkv", [D, 2048], BF16, kind="ExternalInput")
    wg_t = nc.dram_tensor("wg_t", [HC * 128, 1024], BF16, kind="ExternalInput")
    wu_t = nc.dram_tensor("wu_t", [HC * 128, 1024], BF16, kind="ExternalInput")
    wd_t = nc.dram_tensor("wd_t", [DC * 128, 2816], BF16, kind="ExternalInput")
    wfc_t = nc.dram_tensor("wfc_t", [FC * 128, 1024], BF16, kind="ExternalInput")
    wproj_t = nc.dram_tensor("wproj_t", [DC * 128, 4096], BF16, kind="ExternalInput")
    cols = nc.dram_tensor("cols", [128, 40], F32, kind="ExternalInput")
    colsgu = nc.dram_tensor("colsgu", [128, 44], F32, kind="ExternalInput")
    bkvrep = nc.dram_tensor("bkvrep", [128, 2048], F32, kind="ExternalInput")
    ssum_hbm = nc.dram_tensor("ssum_hbm", [D, T], BF16, kind="Internal")

    XN = nc.alloc_sbuf_tensor("XN", [128, DC * T], BF16)
    QPF = nc.alloc_sbuf_tensor("QPF", [128, DC * T], BF16)
    WKV = nc.alloc_sbuf_tensor("WKV", [128, DC * 2048], BF16)
    KVB = nc.alloc_sbuf_tensor("KVB", [128, DC * 1024], BF16)
    KSUM = nc.alloc_sbuf_tensor("KSUM", [128, 8], BF16)
    COLS = nc.alloc_sbuf_tensor("COLS", [128, 40], F32)
    COLSGU = nc.alloc_sbuf_tensor("COLSGU", [128, 44], F32)
    BKVREP = nc.alloc_sbuf_tensor("BKVREP", [128, 2048], F32)
    ONESC = nc.alloc_sbuf_tensor("ONESC", [128, 2], BF16)
    ONESR = nc.alloc_sbuf_tensor("ONESR", [1, 128], BF16)
    CONSTF = nc.alloc_sbuf_tensor("CONSTF", [128, 4], F32)
    R_RSTD = nc.alloc_sbuf_tensor("R_RSTD", [1, T], BF16)
    R_MURS = nc.alloc_sbuf_tensor("R_MURS", [1, T], BF16)

    ARENA_BYTES = 80896
    ARENA = nc.alloc_sbuf_tensor("ARENA", [128, ARENA_BYTES], mybir.dt.uint8)
    abase = nc.lookup_mloc(ARENA).addr

    def at(name, off, shape, dtype, base=None):
        b = abase if base is None else base
        return nc.alloc_sbuf_tensor_at(name, shape, dtype, offset=b + off)

    X0 = at("X0", 0, [128, DC * T], F32)
    HF = at("HF", 0, [128, FC * 1024], BF16)
    KP = at("KP", 0, [128, TT * 1024], BF16)
    VTOK = at("VTOK", 32768, [128, TT * 1024], BF16)
    Z = at("Z", 0, [128, DC * 512], F32)
    X1 = at("X1", 16384, [128, DC * 512], F32)
    X1B = at("X1B", 16384, [128, DC * 1024], BF16)
    X1F = at("X1F", 32768, [128, DC * 512], BF16)
    FUSED = at("FUSED", 40960, [128, HC * 512], BF16)
    CSCR = at("CSCR", 63488, [128, 512], F32)
    SLOT_A = at("SLOT_A", 65536, [128, 1024], BF16)
    GBUF2 = at("GBUF2", 65536, [128, 512], F32)
    SLOT_B = at("SLOT_B", 67584, [128, 512], F32)
    SLOT_C = at("SLOT_C", 69632, [128, 512], F32)
    SLOT_D = at("SLOT_D", 71680, [128, 512], F32)
    SLOT_E = at("SLOT_E", 73728, [128, 512], F32)
    SLOT_F = at("SLOT_F", 75776, [128, 512], F32)
    SLOT_G = at("SLOT_G", 77824, [128, 512], F32)
    SLOTA2 = at("SLOTA2", 75776, [128, 512], BF16)
    SLOTA3 = at("SLOTA3", 77824, [128, 512], BF16)
    WQB = [at("WQB0", 71680, [128, 1024], BF16),
           at("WQB1", 73728, [128, 1024], BF16)]
    SSB = at("SSB", 69632, [128, 1024], BF16)

    kvb_base = nc.lookup_mloc(KVB).addr
    WFCB = at("WFCB", 0, [128, 4 * 1024], BF16, base=kvb_base)
    MBUF = at("MBUF", 8192, [128, 512], F32, base=kvb_base)
    MBUF2 = at("MBUF2", 73728, [128, 512], F32)
    WP = [at("WP0", 10240, [128, 512], F32, base=kvb_base),
          at("WP1", 12288, [128, 512], F32, base=kvb_base)]
    SNB = at("SNB", 14336, [128, 512], F32, base=kvb_base)

    wkv_base = nc.lookup_mloc(WKV).addr
    WQF = at("WQF", 0, [128, DC * 1024], BF16, base=wkv_base)
    WDB = at("WDB", 0, [128, 2 * 2816], BF16, base=wkv_base)
    WGUB = at("WGUB", 11264, [128, 4 * 1024], BF16, base=wkv_base)
    RSTD1 = at("RSTD1", 19456, [1, 512], BF16, base=wkv_base)
    MURS1 = at("MURS1", 20480, [1, 512], BF16, base=wkv_base)
    RSTD2 = at("RSTD2", 21504, [1, 512], BF16, base=wkv_base)
    MURS2 = at("MURS2", 22528, [1, 512], BF16, base=wkv_base)
    SSG2 = at("SSG2", 23552, [128, 2 * 512], BF16, base=wkv_base)
    RECB4 = at("RECB4", 25600, [1, T], BF16, base=wkv_base)
    WGUB2 = at("WGUB2", 25600, [128, 2 * 1024], BF16, base=wkv_base)
    ROWW = at("ROWW", 29696, [1, 512], F32, base=wkv_base)

    qpf_base = nc.lookup_mloc(QPF).addr
    WPROJB = at("WPROJB", 0, [128, 4 * 4096], BF16, base=qpf_base)
    rrow_base = nc.lookup_mloc(R_RSTD).addr
    WFCB2 = at("WFCB2", 0, [128, 4 * 1024], BF16, base=rrow_base)


    PB = [nc.alloc_psum_tensor("pb%d" % i, [128, 512], F32) for i in range(8)]

    S = Sched()
    bank_rel = [("t", 0)] * 8

    def bwait(i):
        sem, val = bank_rel[i]
        S.w("tensor", sem, val)

    dcount = {}

    def dma(eng, stream, rot, out_ap, in_ap):
        idx = dcount.get(stream, 0)
        dcount[stream] = idx + 1
        sem = "%s%d" % (stream, idx % rot)
        cur = S.c.get(sem, 0)
        if cur > 0:
            S.q[eng].append(("w", sem, cur))
        v = S.op(eng, lambda e, o=out_ap, i=in_ap: e.dma_start(out=o, in_=i), sem, 16)
        return (sem, v)

    def sem_total(stream, rot):
        return [(("%s%d" % (stream, p)), S.c.get("%s%d" % (stream, p), 0))
                for p in range(rot) if S.c.get("%s%d" % (stream, p), 0) > 0]

    def mm(out_ap, lhsT, rhs, start, stop, sem=None):
        return S.op(
            "tensor",
            lambda e, o=out_ap, l=lhsT, r=rhs, st=start, sp=stop: e.matmul(
                o, l, r, start=st, stop=sp),
            sem)

    def vop(fn, sem=None):
        return S.op("vector", fn, sem)

    def sop(fn, sem=None):
        return S.op("scalar", fn, sem)

    def gop(fn, sem=None):
        return S.op("gpsimd", fn, sem)

    eps5 = CONSTF[0:1, 0:1]
    m02 = CONSTF[:, 1:2]
    eps6 = CONSTF[0:1, 2:3]
    onescol = ONESC[:, 1:2]
    ones1 = ONESC[:, 0:1]

    def xnc(d, a, b):
        return XN[:, d * T + a: d * T + b]

    def qpc(d, a, b):
        return QPF[:, d * T + a: d * T + b]

    def slotA(par):
        if par < 2:
            return SLOT_A[:, par * 512:(par + 1) * 512]
        return SLOTA2[:, :] if par == 2 else SLOTA3[:, :]

    vop(lambda e: e.memset(ONESC[:, 0:1], 1.0))
    vop(lambda e: e.memset(ONESC[:, 1:2], 1.0 / 1024.0))
    vop(lambda e: e.memset(ONESR[0:1, :], 1.0))
    vop(lambda e: e.memset(CONSTF[:, 0:1], 1.0e-5))
    vop(lambda e: e.memset(CONSTF[:, 1:2], -0.2))
    vop(lambda e: e.memset(CONSTF[:, 2:3], 1.0e-6), "v")

    dma("gpsimd", "cc", 1, COLS[:, :], cols[:, :])
    dma("gpsimd", "cc", 1, COLSGU[:, :], colsgu[:, :])
    g_consts = dma("gpsimd", "cc", 1, BKVREP[:, :], bkvrep[:, :])
    g_wkv = [dma("gpsimd", "wk", 2, WKV[:, d * 2048:(d + 1) * 2048],
                 wkv[d * 128:(d + 1) * 128, :]) for d in range(DC)]

    y_x = [dma("sync", "yx", 4, X0[:, d * T:(d + 1) * T], xT[d * 128:(d + 1) * 128, :])
           for d in range(DC)]

    v_xb = []
    for d in range(DC):
        S.wt("vector", y_x[d])
        v_xb.append(vop(lambda e, d=d: e.tensor_copy(
            xnc(d, 0, T), X0[:, d * T:(d + 1) * T]), "v"))

    slotA_rel = [("v", 0), ("v", 0), ("v", 0), ("v", 0)]
    v_rows = []
    for tg in range(TG):
        ia, ib = (tg % 2) * 2, (tg % 2) * 2 + 1
        pm, ps = PB[ia], PB[ib]
        bwait(ia)
        bwait(ib)
        t_last = None
        for d in range(DC):
            par = (tg * DC + d) % 4
            sem, val = slotA_rel[par]
            S.w("scalar", sem, val)
            S.wt("scalar", y_x[d])
            s_sq = sop(lambda e, d=d, tg=tg, par=par: e.activation(
                slotA(par), X0[:, d * T + tg * 512: d * T + tg * 512 + 512],
                AF.Square), "s")
            S.w("tensor", "v", v_xb[d])
            mm(pm[0:1, :], onescol, xnc(d, tg * 512, tg * 512 + 512),
               start=(d == 0), stop=(d == DC - 1))
            S.w("tensor", "s", s_sq)
            t_last = mm(ps[0:1, :], onescol, slotA(par),
                        start=(d == 0), stop=(d == DC - 1), sem="t")
            slotA_rel[par] = ("t", t_last)
        S.w("vector", "t", t_last)
        vop(lambda e, pm=pm: e.tensor_copy(SLOT_D[0:1, :], pm[0:1, :]))
        vop(lambda e: e.tensor_mul(SLOT_B[0:1, :], SLOT_D[0:1, :], SLOT_D[0:1, :]))
        v_var = vop(lambda e, ps=ps: e.tensor_sub(
            SLOT_C[0:1, :], ps[0:1, :], SLOT_B[0:1, :]), "v")
        S.w("scalar", "v", v_var)
        s_sqrt = sop(lambda e: e.activation(
            SLOT_B[0:1, :], SLOT_C[0:1, :], AF.Sqrt, bias=eps5), "s")
        S.w("vector", "s", s_sqrt)
        vop(lambda e: e.reciprocal(SLOT_C[0:1, :], SLOT_B[0:1, :]))
        vop(lambda e, tg=tg: e.tensor_copy(
            R_RSTD[0:1, tg * 512:(tg + 1) * 512], SLOT_C[0:1, :]))
        vr = vop(lambda e, tg=tg: e.tensor_mul(
            R_MURS[0:1, tg * 512:(tg + 1) * 512], SLOT_D[0:1, :],
            SLOT_C[0:1, :]), "v")
        v_rows.append(vr)
        bank_rel[ia] = ("v", vr)
        bank_rel[ib] = ("v", vr)

    v_xn = []
    for tg in range(TG):
        ia, ib = 4 + (tg % 2) * 2, 5 + (tg % 2) * 2
        pa, pbb = PB[ia], PB[ib]
        bwait(ia)
        bwait(ib)
        S.w("tensor", "v", v_rows[tg])
        mm(pa[:, :], ONESR[0:1, :], R_RSTD[0:1, tg * 512:(tg + 1) * 512], True, True)
        t_bc = mm(pbb[:, :], ONESR[0:1, :], R_MURS[0:1, tg * 512:(tg + 1) * 512],
                  True, True, "t")
        S.w("vector", "t", t_bc)
        vr = None
        for d in range(DC):
            vop(lambda e, d=d, tg=tg, pa=pa: e.tensor_mul(
                SLOT_B[:, :], X0[:, d * T + tg * 512: d * T + tg * 512 + 512], pa[:, :]))
            vr = vop(lambda e, d=d, tg=tg, pbb=pbb: e.tensor_sub(
                xnc(d, tg * 512, tg * 512 + 512), SLOT_B[:, :], pbb[:, :]),
                "v" if d == DC - 1 else None)
        v_xn.append(vr)
        bank_rel[ia] = ("v", vr)
        bank_rel[ib] = ("v", vr)

    wfcb_rel = [0, 0, 0, 0, 0, 0, 0, 0]
    tP0 = S.c["t"]
    for p8 in range(4, 8):
        wfcb_rel[p8] = tP0

    def wfcb_chunk(p8, c0, c1):
        if p8 < 4:
            return WFCB[:, p8 * 1024 + c0: p8 * 1024 + c1]
        return WFCB2[:, (p8 - 4) * 1024 + c0: (p8 - 4) * 1024 + c1]

    g_wfc_pend = {}

    def emit_wfc(fi):
        p8 = fi % 8
        S.w("gpsimd", "t", wfcb_rel[p8])
        g_wfc_pend[fi] = dma("gpsimd", "wf", 8, wfcb_chunk(p8, 0, 1024),
                             wfc_t[(fi % FC) * 128:(fi % FC + 1) * 128, :])

    for fi0 in range(8):
        emit_wfc(fi0)
    wprojb_rel = [0, 0, 0, 0]
    ysb_rel = {0: ("v", 0), 1: ("v", 0)}
    mbuf_gate = [0, 0]
    snb_g_gate = 0
    YSB = [SLOT_B, SLOT_D]
    wp_read_p = [0, 0]
    ssb_y = [None, None]
    p_cnt = 0
    v_spk_last = 0

    for tm in range(TM):
        g_wp = {}

        def emit_wp(dd):
            par4 = dd % 4
            S.w("gpsimd", "t", wprojb_rel[par4])
            g_wp[dd] = dma("gpsimd", "wp", 4, WPROJB[:, par4 * 4096:(par4 + 1) * 4096],
                           wproj_t[dd * 128:(dd + 1) * 128, :])

        v_hf = None
        for f in range(FC):
            fi = tm * FC + f
            p8 = fi % 8
            S.wt("tensor", g_wfc_pend[fi])
            S.w("tensor", "v", v_xn[2 * tm + 1])
            i0, i1 = 2 * (f % 2), 2 * (f % 2) + 1
            hb0, hb1 = PB[i0], PB[i1]
            bwait(i0)
            bwait(i1)
            t_m7 = None
            for d in range(DC):
                wt = wfcb_chunk(p8, d * 128, (d + 1) * 128)
                mm(hb0[:, :], wt, xnc(d, tm * 1024, tm * 1024 + 512),
                   start=(d == 0), stop=(d == DC - 1))
                t_m7 = mm(hb1[:, :], wt, xnc(d, tm * 1024 + 512, tm * 1024 + 1024),
                          start=(d == 0), stop=(d == DC - 1),
                          sem="t" if d == DC - 1 else None)
            wfcb_rel[p8] = t_m7
            if fi + 8 < TM * FC:
                emit_wfc(fi + 8)
            if f in (24, 27, 30):
                emit_wp((f - 24) // 3)
            for th in range(2):
                par2 = (f * 2 + th) % 4
                hb = hb0 if th == 0 else hb1
                sem, val = slotA_rel[par2]
                S.w("scalar", sem, val)
                S.w("scalar", "t", t_m7)
                s_r = sop(lambda e, hb=hb, par2=par2: e.activation(
                    slotA(par2), hb[:, :], AF.Relu), "s")
                bank_rel[i0 if th == 0 else i1] = ("s", s_r)
                s_hf = sop(lambda e, f=f, th=th, par2=par2: e.activation(
                    HF[:, f * 1024 + th * 512: f * 1024 + (th + 1) * 512],
                    slotA(par2), AF.Square), "s")
                slotA_rel[par2] = ("s", s_hf)
        hf_s_last = S.c.get("s", 0)
        hf_v_last = S.c.get("v", 0)
        for dout in range(DC):
            par = dout % 4
            if dout + 3 < DC:
                emit_wp(dout + 3)
            S.wt("tensor", g_wp[dout])
            S.w("tensor", "s", hf_s_last)
            S.w("tensor", "v", hf_v_last)
            i0, i1 = 4 + 2 * (dout % 2), 5 + 2 * (dout % 2)
            yb0, yb1 = PB[i0], PB[i1]
            bwait(i0)
            bwait(i1)
            t_m8 = None
            for f in range(FC):
                wt = WPROJB[:, par * 4096 + f * 128: par * 4096 + (f + 1) * 128]
                mm(yb0[:, :], wt, HF[:, f * 1024: f * 1024 + 512],
                   start=(f == 0), stop=(f == FC - 1))
                t_m8 = mm(yb1[:, :], wt, HF[:, f * 1024 + 512: f * 1024 + 1024],
                          start=(f == 0), stop=(f == FC - 1),
                          sem="t" if f == FC - 1 else None)
            wprojb_rel[par] = t_m8
            for th in range(2):
                yb = yb0 if th == 0 else yb1
                ypar = th
                sem, val = ysb_rel[ypar]
                S.w("scalar", sem, val)
                S.w("scalar", "t", t_m8)
                s_y = sop(lambda e, yb=yb, ypar=ypar: e.activation(
                    YSB[ypar][:, :], yb[:, :], AF.Copy), "s")
                bank_rel[i0 if th == 0 else i1] = ("s", s_y)
                y = YSB[ypar]
                MB = MBUF if th == 0 else MBUF2
                S.w("scalar", "v", mbuf_gate[th])
                s_mb = sop(lambda e, y=y, MB=MB: e.activation(
                    MB[:, :], y[:, :], AF.Copy, scale=1.9), "s")
                S.w("vector", "s", s_y)
                S.w("vector", "p", snb_g_gate)
                vop(lambda e, y=y: e.tensor_scalar(
                    SNB[:, :], y[:, :], 1.0, -1.0, OP.is_gt, OP.mult))
                S.w("vector", "s", s_mb)
                vop(lambda e, MB=MB: e.tensor_add(MB[:, :], MB[:, :], SNB[:, :]))
                S.w("vector", "p", wp_read_p[0])
                v4 = vop(lambda e, MB=MB: e.tensor_scalar(
                    WP[0][:, :], MB[:, :], 1.0, -1.0, OP.is_gt, OP.mult), "v")
                S.w("gpsimd", "v", v4)
                p_cnt = gop(lambda e: e.tensor_add(
                    SNB[:, :], SNB[:, :], WP[0][:, :]), "p")
                wp_read_p[0] = p_cnt
                vlast = None
                for step in range(3):
                    wp_cur = WP[step % 2]
                    nxt = (step + 1) % 2
                    vop(lambda e, y=y, MB=MB: e.scalar_tensor_tensor(
                        MB[:, :], MB[:, :], 0.9, y[:, :], OP.mult, OP.add))
                    vop(lambda e, wp=wp_cur, MB=MB: e.tensor_add(
                        MB[:, :], MB[:, :], wp[:, :]))
                    S.w("vector", "p", wp_read_p[nxt])
                    vlast = vop(lambda e, wp=WP[nxt], MB=MB: e.tensor_scalar(
                        wp[:, :], MB[:, :], 1.0, -1.0, OP.is_gt, OP.mult), "v")
                    S.w("gpsimd", "v", vlast)
                    p_cnt = gop(lambda e, wp=WP[nxt]: e.tensor_add(
                        SNB[:, :], SNB[:, :], wp[:, :]), "p")
                    wp_read_p[nxt] = p_cnt
                ysb_rel[ypar] = ("v", vlast)
                mbuf_gate[th] = vlast
                spar = (dout * 2 + th) % 2
                S.wt("gpsimd", ssb_y[spar])
                g_sn = gop(lambda e, spar=spar: e.tensor_copy(
                    SSB[:, spar * 512:(spar + 1) * 512], SNB[:, :]), "p")
                snb_g_gate = g_sn
                S.w("sync", "p", g_sn)
                yd = dma("sync", "ss", 2,
                         ssum_hbm[dout * 128:(dout + 1) * 128,
                                  tm * 1024 + th * 512: tm * 1024 + (th + 1) * 512],
                         SSB[:, spar * 512:(spar + 1) * 512])
                ssb_y[spar] = yd

    for tok in sem_total("ss", 2):
        S.w("scalar", tok[0], tok[1])
    kv_stage_rel = 0
    for tok in sem_total("wk", 2):
        S.w("tensor", tok[0], tok[1])
    S.wt("vector", g_consts)
    v_kvdone = None
    for tt in range(TT):
        ks = [0, 1, 2, 3] if tt % 2 == 0 else [4, 5, 6, 7]
        for i in ks:
            bwait(i)
        S.w("tensor", "v", v_xn[tt // 4])
        t_kv = None
        for d in range(DC):
            st = XN[:, d * T + tt * 128: d * T + (tt + 1) * 128]
            for j in range(4):
                t_kv = mm(PB[ks[j]][:, :], st,
                          WKV[:, d * 2048 + j * 512: d * 2048 + (j + 1) * 512],
                          start=(d == 0), stop=(d == DC - 1),
                          sem="t" if (d == DC - 1 and j == 3) else None)
        for eh in range(2):
            tb = SLOT_B if eh == 0 else SLOT_G
            S.w("vector", "t", t_kv)
            v_t1 = vop(lambda e, tb=tb, eh=eh, ks=ks: e.tensor_add(
                tb[:, :], PB[ks[eh]][:, :], BKVREP[:, eh * 512:(eh + 1) * 512]), "v")
            bank_rel[ks[eh]] = ("v", v_t1)
            S.w("scalar", "v", max(v_t1, kv_stage_rel))
            sop(lambda e, tb=tb: e.activation(SLOT_C[:, :], tb[:, :], AF.Exp))
            s_rl = sop(lambda e, tb=tb: e.activation(SLOT_F[:, :], tb[:, :], AF.Relu), "s")
            S.w("vector", "s", s_rl)
            v_kp = vop(lambda e, tt=tt, eh=eh: e.scalar_tensor_tensor(
                KP[:, tt * 1024 + eh * 512: tt * 1024 + (eh + 1) * 512],
                SLOT_C[:, :], 1.0, SLOT_F[:, :], OP.min, OP.add), "v")
            kv_stage_rel = v_kp
        for eh in range(2):
            S.w("vector", "t", t_kv)
            v_v = vop(lambda e, tt=tt, eh=eh, ks=ks: e.tensor_add(
                VTOK[:, tt * 1024 + eh * 512: tt * 1024 + (eh + 1) * 512],
                PB[ks[2 + eh]][:, :],
                BKVREP[:, 1024 + eh * 512: 1024 + (eh + 1) * 512]), "v")
            bank_rel[ks[2 + eh]] = ("v", v_v)
        v_kvdone = v_v

    S.w("gpsimd", "t", S.c["t"])
    g_wq = [dma("gpsimd", "wq", 2, WQF[:, d * 1024:(d + 1) * 1024],
                wq[d * 128:(d + 1) * 128, :]) for d in range(DC)]
    S.w("scalar", "v", v_kvdone)
    q_stage_rel = [0, 0]
    q_stage_rel2 = [0, 0]
    v_q = []
    t_q_last = None
    for tg in range(TG):
        for i in range(8):
            bwait(i)
        t_d = None
        for d in range(DC):
            S.wt("tensor", g_wq[d])
            for e_ in range(8):
                t_d = mm(PB[e_][:, :], WQF[:, d * 1024 + e_ * 128: d * 1024 + (e_ + 1) * 128],
                         xnc(d, tg * 512, tg * 512 + 512),
                         start=(d == 0), stop=(d == DC - 1),
                         sem="t" if e_ == 7 else None)
        t_q_last = t_d
        vq = None
        for e_ in range(8):
            ex = SLOT_C if e_ % 2 == 0 else SLOT_B
            rl = SLOT_F if e_ % 2 == 0 else SLOT_G
            bq = COLS[:, e_ * 5: e_ * 5 + 1]
            S.w("scalar", "t", t_d)
            S.wt("scalar", g_consts)
            S.w("scalar", "v", q_stage_rel[e_ % 2])
            sop(lambda e, e_=e_, ex=ex, bq=bq: e.activation(
                ex[:, :], PB[e_][:, :], AF.Exp, bias=bq))
            s_rl = sop(lambda e, e_=e_, rl=rl, bq=bq: e.activation(
                rl[:, :], PB[e_][:, :], AF.Relu, bias=bq), "s")
            bank_rel[e_] = ("s", s_rl)
            S.w("vector", "s", s_rl)
            vq = vop(lambda e, e_=e_, tg=tg, ex=ex, rl=rl: e.scalar_tensor_tensor(
                qpc(e_, tg * 512, tg * 512 + 512), ex[:, :], 1.0, rl[:, :],
                OP.min, OP.add), "v")
            q_stage_rel[e_ % 2] = vq
        v_q.append(vq)

    v_kv2 = []
    t_p2_last = None
    for d in range(DC):
        ks = [0, 1, 2] if d % 2 == 0 else [3, 4, 5]
        for i in ks:
            bwait(i)
        S.w("tensor", "v", v_kvdone)
        t_2 = None
        for si in range(TT):
            st = KP[:, si * 1024 + d * 128: si * 1024 + (d + 1) * 128]
            mm(PB[ks[0]][:, :], st, VTOK[:, si * 1024: si * 1024 + 512],
               start=(si == 0), stop=(si == TT - 1))
            mm(PB[ks[1]][:, :], st, VTOK[:, si * 1024 + 512: si * 1024 + 1024],
               start=(si == 0), stop=(si == TT - 1))
            t_2 = mm(PB[ks[2]][:, 0:1], st, ones1,
                     start=(si == 0), stop=(si == TT - 1),
                     sem="t" if si == TT - 1 else None)
        t_p2_last = t_2
        S.w("vector", "t", t_2)
        vop(lambda e, d=d, ks=ks: e.tensor_copy(
            KVB[:, d * 1024: d * 1024 + 512], PB[ks[0]][:, :]))
        vop(lambda e, d=d, ks=ks: e.tensor_copy(
            KVB[:, d * 1024 + 512: d * 1024 + 1024], PB[ks[1]][:, :]))
        v2 = vop(lambda e, d=d, ks=ks: e.tensor_copy(
            KSUM[:, d: d + 1], PB[ks[2]][:, 0:1]), "v")
        v_kv2.append(v2)
        for i in ks:
            bank_rel[i] = ("v", v2)

    v_den4 = []
    for g in range(TG):
        bwait(6)
        S.w("tensor", "v", v_kv2[-1])
        S.w("tensor", "v", v_q[g])
        t_den = None
        for d in range(DC):
            t_den = mm(PB[6][0:1, :], KSUM[:, d: d + 1],
                       qpc(d, g * 512, g * 512 + 512),
                       start=(d == 0), stop=(d == DC - 1),
                       sem="t" if d == DC - 1 else None)
        S.w("vector", "t", t_den)
        vop(lambda e: e.tensor_scalar_add(SLOT_B[0:1, :], PB[6][0:1, :], 1.0e-6))
        vop(lambda e: e.reciprocal(SLOT_B[0:1, :], SLOT_B[0:1, :]))
        vd0 = vop(lambda e, g=g: e.tensor_copy(
            RECB4[0:1, g * 512:(g + 1) * 512], SLOT_B[0:1, :]), "v")
        bank_rel[6] = ("v", vd0)
        bwait(7)
        S.w("tensor", "v", vd0)
        t_rb0 = mm(PB[7][:, :], ONESR[0:1, :],
                   RECB4[0:1, g * 512:(g + 1) * 512], True, True, "t")
        S.w("vector", "t", t_rb0)
        vd = None
        for d in range(DC):
            vd = vop(lambda e, d=d, g=g: e.tensor_mul(
                qpc(d, g * 512, g * 512 + 512), qpc(d, g * 512, g * 512 + 512),
                PB[7][:, :]), "v" if d == DC - 1 else None)
        v_den4.append(vd)
        bank_rel[7] = ("v", vd)
    t_den_done = S.c["t"]

    wdb_rel = [0, 0]
    wgu_rel3 = [0, 0, t_den_done]
    ssg_rel = [0, 0]
    outt2 = [None, None]
    gb_rel = [("p", 0), ("p", 0)]
    xr2_rel = [("p", 0), ("p", 0)]
    cscr_rel = ("p", 0)
    g5_last = 0
    v3_last = 0
    comb_vf = None
    comb_tbc3 = None
    ssg_started = [False]
    g_ssg = {}
    g_wgu = {}

    NUMBANKS = [0, 1, 2, 3, 6, 7, 0, 1]

    def emit_num(gx, es):
        ts_ = []
        for e_ in es:
            bk = NUMBANKS[e_]
            bwait(bk)
            if e_ == 0:
                S.w("tensor", "v", v_den4[gx])
            t_num = None
            for d in range(DC):
                t_num = mm(PB[bk][:, :],
                           KVB[:, d * 1024 + e_ * 128: d * 1024 + (e_ + 1) * 128],
                           qpc(d, gx * 512, gx * 512 + 512),
                           start=(d == 0), stop=(d == DC - 1),
                           sem="t" if d == DC - 1 else None)
            ts_.append(t_num)
        return ts_

    def wgu_aps(par3):
        if par3 < 2:
            return (WGUB, par3 * 1024, WGUB, (2 + par3) * 1024)
        return (WGUB2, 0, WGUB2, 1024)

    def emit_wgu(h):
        par3 = h % 3
        gt_, go_, ut_, uo_ = wgu_aps(par3)
        S.w("gpsimd", "t", wgu_rel3[par3])
        g_wg = dma("gpsimd", "wg", 2, gt_[:, go_:go_ + 1024],
                   wg_t[h * 128:(h + 1) * 128, :])
        g_wu = dma("gpsimd", "wu", 2, ut_[:, uo_:uo_ + 1024],
                   wu_t[h * 128:(h + 1) * 128, :])
        return (g_wg, g_wu)

    def emit_ssg(cg, dd):
        par = dd % 2
        S.w("gpsimd", "s", ssg_rel[par])
        g_ssg[dd] = dma("gpsimd", "sg", 2, SSG2[:, par * 512:(par + 1) * 512],
                        ssum_hbm[dd * 128:(dd + 1) * 128,
                                 cg * 512:(cg + 1) * 512])

    def combine_start(cg):
        if not ssg_started[0]:
            for tok in sem_total("ss", 2):
                S.w("gpsimd", tok[0], tok[1])
            ssg_started[0] = True
        emit_ssg(cg, 0)
        emit_ssg(cg, 1)

    def combine_dout(cg, dout, tail=False):
        nonlocal cscr_rel, g5_last, v3_last
        par = dout % 2
        on_v = tail and par == 1
        if dout < 2:
            S.w("sync", "v", comb_vf)
        S.wt("sync", xr2_rel[par])
        XR2 = SLOT_D if par == 0 else SLOT_E
        y_x2 = dma("sync", "x2", 2, XR2[:, :],
                   xT[dout * 128:(dout + 1) * 128, cg * 512:(cg + 1) * 512])
        GB = SLOT_G if par == 0 else GBUF2
        OB = SLOT_F if par == 0 else SLOT_C
        S.wt("scalar", g_ssg[dout])
        if dout == 0:
            S.w("scalar", "v", comb_vf)
        S.wt("scalar", gb_rel[par])
        s_g = sop(lambda e, par=par, GB=GB: e.activation(
            GB[:, :], SSG2[:, par * 512:(par + 1) * 512],
            AF.Silu, scale=-0.2), "s")
        ssg_rel[par] = s_g
        if dout == 0:
            S.w("vector", "t", comb_tbc3)
        S.wt("vector", cscr_rel)
        vop(lambda e, dout=dout: e.tensor_mul(
            CSCR[:, :], Z[:, dout * 512:(dout + 1) * 512], PB[4][:, :]))
        vop(lambda e: e.tensor_sub(CSCR[:, :], CSCR[:, :], PB[5][:, :]))
        v3 = vop(lambda e, dout=dout: e.tensor_scalar(
            CSCR[:, :], CSCR[:, :],
            COLS[:, dout * 5 + 3: dout * 5 + 4], COLS[:, dout * 5 + 4: dout * 5 + 5],
            OP.mult, OP.add), "v")
        v3_last = v3
        if on_v:
            S.w("vector", "s", s_g)
            S.wt("vector", outt2[par])
            g4v = vop(lambda e, GB=GB, OB=OB: e.tensor_mul(
                OB[:, :], CSCR[:, :], GB[:, :]), "v")
            cscr_rel = ("v", g4v)
            gb_rel[par] = ("v", g4v)
            S.wt("vector", y_x2)
            g5 = vop(lambda e, XR2=XR2, OB=OB: e.tensor_add(
                OB[:, :], OB[:, :], XR2[:, :]), "v")
            xr2_rel[par] = ("v", g5)
            S.w("sync", "v", g5)
        else:
            S.w("gpsimd", "v", v3)
            S.w("gpsimd", "s", s_g)
            S.wt("gpsimd", outt2[par])
            g4 = gop(lambda e, GB=GB, OB=OB: e.tensor_mul(
                OB[:, :], CSCR[:, :], GB[:, :]), "p")
            cscr_rel = ("p", g4)
            gb_rel[par] = ("p", g4)
            S.wt("gpsimd", y_x2)
            g5 = gop(lambda e, XR2=XR2, OB=OB: e.tensor_add(
                OB[:, :], OB[:, :], XR2[:, :]), "p")
            xr2_rel[par] = ("p", g5)
            g5_last = g5
            S.w("sync", "p", g5)
        outt2[par] = dma("sync", "ou", 2,
                         outT[dout * 128:(dout + 1) * 128, cg * 512:(cg + 1) * 512],
                         OB[:, :])
        if dout + 2 < DC:
            emit_ssg(cg, dout + 2)

    pend_tnum = None
    for g in range(TG):
        if g == 0:
            S.w("gpsimd", "t", t_q_last)
            S.w("vector", "t", t_p2_last)
            S.wt("vector", g_consts)
            S.wt("scalar", g_consts)
            vop(lambda e: e.memset(FUSED[32:64, 21 * 512: 22 * 512], 1.0), "v")
        for hh in range(3):
            g_wgu[hh] = emit_wgu(hh)
        if g > 0:
            combine_start(g - 1)

        v_z = []
        s_z2 = []
        if pend_tnum is None:
            pend_tnum = emit_num(g, range(6))
        tnums, pend_tnum = pend_tnum, None
        for e_ in range(DC):
            if g > 0:
                combine_dout(g - 1, e_)
            bk = NUMBANKS[e_]
            pe = PB[bk]
            if e_ >= 6:
                tnums.append(emit_num(g, [e_])[0])
            S.w("vector", "t", tnums[e_])
            vz = vop(lambda e, e_=e_, g=g, pe=pe: e.tensor_add(
                Z[:, e_ * 512:(e_ + 1) * 512], pe[:, :],
                xnc(e_, g * 512, g * 512 + 512)), "v")
            v_z.append(vz)
            bank_rel[bk] = ("v", vz)
            S.w("scalar", "v", vz)
            sop(lambda e, e_=e_: e.activation(
                X1F[:, e_ * 512:(e_ + 1) * 512], Z[:, e_ * 512:(e_ + 1) * 512], AF.Copy))
            sz = sop(lambda e, e_=e_: e.activation(
                X1B[:, e_ * 512:(e_ + 1) * 512], Z[:, e_ * 512:(e_ + 1) * 512],
                AF.Square), "s")
            s_z2.append(sz)
        if g > 0:
            bank_rel[4] = ("v", v3_last)
            bank_rel[5] = ("v", v3_last)

        bwait(2)
        bwait(3)
        t_st1 = None
        for e_ in range(DC):
            S.w("tensor", "s", s_z2[e_])
            mm(PB[2][0:1, :], onescol, X1F[:, e_ * 512:(e_ + 1) * 512],
               start=(e_ == 0), stop=(e_ == DC - 1))
            t_st1 = mm(PB[3][0:1, :], onescol, X1B[:, e_ * 512:(e_ + 1) * 512],
                       start=(e_ == 0), stop=(e_ == DC - 1), sem="t")
        S.w("vector", "t", t_st1)
        vop(lambda e: e.tensor_copy(ROWW[0:1, :], PB[2][0:1, :]))
        vop(lambda e: e.tensor_mul(SLOT_B[0:1, :], ROWW[0:1, :], ROWW[0:1, :]))
        v_var = vop(lambda e: e.tensor_sub(SLOT_B[0:1, :], PB[3][0:1, :], SLOT_B[0:1, :]), "v")
        S.w("scalar", "v", v_var)
        s_sq1 = sop(lambda e: e.activation(
            SLOT_B[0:1, :], SLOT_B[0:1, :], AF.Sqrt, bias=eps5), "s")
        S.w("vector", "s", s_sq1)
        vop(lambda e: e.reciprocal(SLOT_B[0:1, :], SLOT_B[0:1, :]))
        vop(lambda e: e.tensor_copy(RSTD1[0:1, :], SLOT_B[0:1, :]))
        v_r1 = vop(lambda e: e.tensor_mul(
            MURS1[0:1, :], ROWW[0:1, :], SLOT_B[0:1, :]), "v")
        bank_rel[2] = ("v", v_r1)
        bank_rel[3] = ("v", v_r1)

        bwait(6)
        bwait(7)
        S.w("tensor", "v", v_r1)
        mm(PB[6][:, :], ONESR[0:1, :], RSTD1[0:1, :], True, True)
        t_bc2 = mm(PB[7][:, :], ONESR[0:1, :], MURS1[0:1, :], True, True, "t")
        v_x1 = []
        s_x1 = []
        for e_ in range(DC):
            S.w("vector", "t", t_bc2)
            vop(lambda e, e_=e_: e.tensor_mul(
                SLOT_B[:, :], Z[:, e_ * 512:(e_ + 1) * 512], PB[6][:, :]))
            vop(lambda e: e.tensor_sub(SLOT_B[:, :], SLOT_B[:, :], PB[7][:, :]))
            vx = vop(lambda e, e_=e_: e.tensor_scalar(
                X1[:, e_ * 512:(e_ + 1) * 512], SLOT_B[:, :],
                COLS[:, e_ * 5 + 1: e_ * 5 + 2], COLS[:, e_ * 5 + 2: e_ * 5 + 3],
                OP.mult, OP.add), "v")
            v_x1.append(vx)
            S.w("scalar", "t", t_st1)
            S.w("scalar", "v", vx)
            s_x1.append(sop(lambda e, e_=e_: e.activation(
                X1F[:, e_ * 512:(e_ + 1) * 512], X1[:, e_ * 512:(e_ + 1) * 512],
                AF.Copy), "s"))
        bank_rel[6] = ("v", v_x1[-1])
        bank_rel[7] = ("v", v_x1[-1])

        v_f = None
        t_5 = None
        for h in range(HC):
            par3 = h % 3
            gt_, go_, ut_, uo_ = wgu_aps(par3)
            g_wg, g_wu = g_wgu[h]
            ig, iu = [(0, 1), (2, 3), (6, 7)][h % 3]
            bwait(ig)
            bwait(iu)
            S.wt("tensor", g_wg)
            S.wt("tensor", g_wu)
            for d in range(DC):
                if h == 0:
                    S.w("tensor", "s", s_x1[d])
                mm(PB[ig][:, :], gt_[:, go_ + d * 128: go_ + (d + 1) * 128],
                   X1F[:, d * 512:(d + 1) * 512], start=(d == 0), stop=(d == DC - 1))
                t_5 = mm(PB[iu][:, :], ut_[:, uo_ + d * 128: uo_ + (d + 1) * 128],
                         X1F[:, d * 512:(d + 1) * 512], start=(d == 0), stop=(d == DC - 1),
                         sem="t" if d == DC - 1 else None)
            wgu_rel3[par3] = t_5
            if h + 3 < HC:
                g_wgu[h + 3] = emit_wgu(h + 3)
            rows = 128 if h < HC - 1 else HLAST
            SIL = SLOT_D if h % 2 == 0 else SLOT_F
            UPT = SLOT_E if h % 2 == 0 else SLOT_G
            S.w("scalar", "t", t_5)
            if h == 0:
                S.wt("scalar", outt2[0])
                S.wt("scalar", outt2[1])
                if g5_last:
                    S.w("scalar", "p", g5_last)
            sop(lambda e, h=h, ig=ig, rows=rows, SIL=SIL: e.activation(
                SIL[0:rows, :], PB[ig][0:rows, :], AF.Silu,
                bias=COLSGU[0:rows, 2 * h: 2 * h + 1]))
            s_5 = sop(lambda e, h=h, iu=iu, rows=rows, UPT=UPT: e.activation(
                UPT[0:rows, :], PB[iu][0:rows, :], AF.Identity,
                bias=COLSGU[0:rows, 2 * h + 1: 2 * h + 2]), "s")
            bank_rel[ig] = ("s", s_5)
            bank_rel[iu] = ("s", s_5)
            S.w("vector", "s", s_5)
            v_f = vop(lambda e, h=h, rows=rows, SIL=SIL, UPT=UPT: e.tensor_mul(
                FUSED[0:rows, h * 512:(h + 1) * 512], SIL[0:rows, :], UPT[0:rows, :]), "v")

        v_u = []
        s_u2 = []
        t_6 = None
        for dout in range(DC):
            par = dout % 2
            S.w("gpsimd", "t", wdb_rel[par])
            g_wd = dma("gpsimd", "wd", 2, WDB[:, par * 2816:(par + 1) * 2816],
                       wd_t[dout * 128:(dout + 1) * 128, :])
            ib_ = dout % 4
            bwait(ib_)
            S.wt("tensor", g_wd)
            S.w("tensor", "v", v_f)
            for h in range(HC):
                K = 128 if h < HC - 1 else HLAST + 1
                t_6 = mm(PB[ib_][:, :],
                         WDB[0:K, par * 2816 + h * 128: par * 2816 + (h + 1) * 128],
                         FUSED[0:K, h * 512:(h + 1) * 512],
                         start=(h == 0), stop=(h == HC - 1),
                         sem="t" if h == HC - 1 else None)
            wdb_rel[par] = t_6
            S.w("vector", "t", t_6)
            if dout == 0:
                S.w("vector", "t", t_5)
            vu = vop(lambda e, dout=dout, ib_=ib_: e.tensor_add(
                Z[:, dout * 512:(dout + 1) * 512], PB[ib_][:, :],
                X1[:, dout * 512:(dout + 1) * 512]), "v")
            v_u.append(vu)
            bank_rel[ib_] = ("v", vu)
            S.w("scalar", "v", vu)
            if dout == 0:
                S.w("scalar", "t", t_5)
                S.w("scalar", "t", t_st1)
            sop(lambda e, dout=dout: e.activation(
                X1F[:, dout * 512:(dout + 1) * 512],
                Z[:, dout * 512:(dout + 1) * 512], AF.Copy))
            su = sop(lambda e, dout=dout: e.activation(
                X1B[:, dout * 512:(dout + 1) * 512],
                Z[:, dout * 512:(dout + 1) * 512], AF.Square), "s")
            s_u2.append(su)

        bwait(2)
        bwait(3)
        t_st2 = None
        for dout in range(DC):
            S.w("tensor", "s", s_u2[dout])
            mm(PB[2][0:1, :], onescol, X1F[:, dout * 512:(dout + 1) * 512],
               start=(dout == 0), stop=(dout == DC - 1))
            t_st2 = mm(PB[3][0:1, :], onescol, X1B[:, dout * 512:(dout + 1) * 512],
                       start=(dout == 0), stop=(dout == DC - 1), sem="t")
        S.w("vector", "t", t_st2)
        vop(lambda e: e.tensor_copy(ROWW[0:1, :], PB[2][0:1, :]))
        vop(lambda e: e.tensor_mul(SLOT_B[0:1, :], ROWW[0:1, :], ROWW[0:1, :]))
        v_var2 = vop(lambda e: e.tensor_sub(SLOT_B[0:1, :], PB[3][0:1, :], SLOT_B[0:1, :]), "v")
        S.w("scalar", "v", v_var2)
        s_sq2 = sop(lambda e: e.activation(
            SLOT_B[0:1, :], SLOT_B[0:1, :], AF.Sqrt, bias=eps5), "s")
        S.w("vector", "s", s_sq2)
        vop(lambda e: e.reciprocal(SLOT_B[0:1, :], SLOT_B[0:1, :]))
        vop(lambda e: e.tensor_copy(RSTD2[0:1, :], SLOT_B[0:1, :]))
        v_r2 = vop(lambda e: e.tensor_mul(
            MURS2[0:1, :], ROWW[0:1, :], SLOT_B[0:1, :]), "v")
        bank_rel[2] = ("v", v_r2)
        bank_rel[3] = ("v", v_r2)

        bwait(4)
        bwait(5)
        S.w("tensor", "v", v_r2)
        mm(PB[4][:, :], ONESR[0:1, :], RSTD2[0:1, :], True, True)
        t_bc3 = mm(PB[5][:, :], ONESR[0:1, :], MURS2[0:1, :], True, True, "t")
        if g + 1 < TG:
            pend_tnum = emit_num(g + 1, range(6))
        comb_vf = v_f
        comb_tbc3 = t_bc3

    combine_start(TG - 1)
    for dout in range(DC):
        combine_dout(TG - 1, dout, tail=True)

    for tok in sem_total("ou", 2):
        S.w("sync", tok[0], tok[1])

    sems = {n: nc.alloc_semaphore(n) for n in sorted(S.c.keys())}
    with nc.Block() as block:
        def make(engname):
            def body(eng):
                for item in S.q[engname]:
                    if item[0] == "w":
                        eng.wait_ge(sems[item[1]], item[2])
                    else:
                        ins = item[1](eng)
                        if item[2] is not None:
                            ins.then_inc(sems[item[2]], item[3])
            return body

        block.tensor(make("tensor"))
        block.vector(make("vector"))
        block.scalar(make("scalar"))
        block.gpsimd(make("gpsimd"))
        block.sync(make("sync"))

    return nc


def _pack_weights(i):
    f32 = np.float32
    Wqkv = np.ascontiguousarray(np.asarray(i["L0_attn_Wqkv"], dtype=f32))
    bqkv = np.asarray(i["L0_attn_bqkv"], dtype=f32)
    Wg = np.asarray(i["L0_attn_Wg"], dtype=f32)
    bg = np.asarray(i["L0_attn_bg"], dtype=f32)
    Wu = np.asarray(i["L0_attn_Wu"], dtype=f32)
    bu = np.asarray(i["L0_attn_bu"], dtype=f32)
    Wd = np.asarray(i["L0_attn_Wd"], dtype=f32)
    bd = np.asarray(i["L0_attn_bd"], dtype=f32)
    g1 = np.asarray(i["L0_attn_g1"], dtype=f32)
    b1 = np.asarray(i["L0_attn_b1"], dtype=f32)
    g2 = np.asarray(i["L0_attn_g2"], dtype=f32)
    b2 = np.asarray(i["L0_attn_b2"], dtype=f32)
    W_fc = np.asarray(i["L0_ffn_W_fc"], dtype=f32)
    W_proj = np.asarray(i["L0_ffn_W_proj"], dtype=f32)

    d = {}
    d["wq"] = np.ascontiguousarray(Wqkv[:, 0:1024])
    d["wkv"] = np.ascontiguousarray(Wqkv[:, 1024:3072])

    wgp = np.zeros((D, HC * 128), dtype=f32)
    wgp[:, :2730] = Wg
    d["wg_t"] = np.ascontiguousarray(
        wgp.reshape(DC, 128, HC, 128).transpose(2, 1, 0, 3).reshape(HC * 128, 1024))
    wup = np.zeros((D, HC * 128), dtype=f32)
    wup[:, :2730] = Wu
    d["wu_t"] = np.ascontiguousarray(
        wup.reshape(DC, 128, HC, 128).transpose(2, 1, 0, 3).reshape(HC * 128, 1024))

    wda = np.zeros((HC * 128, D), dtype=f32)
    wda[:2730] = Wd
    wda[2730] = bd
    d["wd_t"] = np.ascontiguousarray(
        wda.reshape(HC, 128, DC, 128).transpose(2, 1, 0, 3).reshape(DC * 128, HC * 128))

    d["wfc_t"] = np.ascontiguousarray(
        W_fc.reshape(FC, 128, DC, 128).transpose(0, 3, 2, 1).reshape(FC * 128, 1024))
    d["wproj_t"] = np.ascontiguousarray(
        W_proj.reshape(DC, 128, FC, 128).transpose(0, 3, 2, 1).reshape(DC * 128, FC * 128))

    import ml_dtypes
    for kk in ("wq", "wkv", "wg_t", "wu_t", "wd_t", "wfc_t", "wproj_t"):
        d[kk] = np.ascontiguousarray(d[kk].astype(ml_dtypes.bfloat16))

    colsmat = np.stack([bqkv[0:1024], g1, b1, g2, b2], axis=1)
    d["cols"] = np.ascontiguousarray(
        colsmat.reshape(DC, 128, 5).transpose(1, 0, 2).reshape(128, 40))
    gum = np.zeros((HC * 128, 2), dtype=f32)
    gum[:2730, 0] = bg
    gum[:2730, 1] = bu
    d["colsgu"] = np.ascontiguousarray(
        gum.reshape(HC, 128, 2).transpose(1, 0, 2).reshape(128, 44))
    d["bkvrep"] = np.ascontiguousarray(
        np.broadcast_to(bqkv[1024:3072], (128, 2048)))
    return d


def kernel(**inputs):
    x = np.asarray(inputs["L0_attn_x"], dtype=np.float32)
    wd = _pack_weights(inputs)

    nc = _build()

    in_maps = []
    for c in range(8):
        m = dict(wd)
        m["xT"] = np.ascontiguousarray(x[c].T)
        in_maps.append(m)

    res = run_bass_kernel_spmd(nc, in_maps, core_ids=list(range(8)))
    out = np.stack([np.asarray(r["outT"]).T for r in res.results], axis=0)
    return np.ascontiguousarray(out.astype(np.float32))
